# version 01
# baseline (speedup 1.0000x reference)
import numpy as np
from contextlib import ExitStack, contextmanager

import concourse.bass as bass
import concourse.mybir as mybir
from concourse.bass_utils import run_bass_kernel_spmd
from concourse.alu_op_type import AluOpType as OP

F32 = mybir.dt.float32
BF16 = mybir.dt.bfloat16
AF = mybir.ActivationFunctionType
AX = mybir.AxisListType.X

D = 1024
T = 4096
NCH = 8
NCORES = 8
EPS = 1e-5
NEXP = 8
P4_DEBUG = 0
P4_DEBUG_NE = 1
P4_DEBUG_NT = 16
_ign = 0


class Prog:
    def __init__(self):
        self.nc = bass.Bass("TRN2", target_bir_lowering=False)
        self.top = ExitStack()
        self.ctx = self.top
        self.sems = {}
        self.uid = 0

    def name(self, p="t"):
        self.uid += 1
        return "%s%d" % (p, self.uid)

    def sb(self, shape, dtype, name=None):
        return self.ctx.enter_context(self.nc.sbuf_tensor(name or self.name(), shape, dtype))

    def psum(self, shape, dtype=F32, name=None):
        return self.ctx.enter_context(self.nc.psum_tensor(name or self.name("p"), shape, dtype))

    def dram(self, shape, dtype=F32, name=None):
        return self.nc.dram_tensor(name or self.name("d"), shape, dtype).ap()

    @contextmanager
    def scope(self):
        old = self.ctx
        try:
            with ExitStack() as es:
                self.ctx = es
                yield
        finally:
            self.ctx = old

    def sem(self, nm):
        if nm not in self.sems:
            self.sems[nm] = [self.top.enter_context(self.nc.semaphore(nm)), 0]
        return self.sems[nm]

    def inc(self, nm, instr, k=1):
        s = self.sem(nm)
        instr.then_inc(s[0], k)
        s[1] += k
        return s[1]

    def val(self, nm):
        return self.sem(nm)[1]

    def wait(self, eng, nm, v=None):
        s = self.sem(nm)
        eng.wait_ge(s[0], s[1] if v is None else v)

    def fence(self, eng, nm, instr):
        v = self.inc(nm, instr)
        eng.wait_ge(self.sem(nm)[0], v)
        return instr


def make_consts(P):
    nc = P.nc
    c = {}
    c["identf"] = P.sb([128, 128], F32, "identf")
    c["identb"] = P.sb([128, 128], BF16, "identb")
    c["ones_col_bf"] = P.sb([128, 1], BF16, "onescolbf")
    c["ones1_f"] = P.sb([1, 128], F32, "ones1f")
    c["ones1_bf"] = P.sb([1, 128], BF16, "ones1bf")
    c["eps11"] = P.sb([1, 1], F32, "eps11")
    c["ones_row_bf"] = P.sb([1, 512], BF16, "onesrowbf")
    ident_hbm = P.hbm["cc_ident"]
    with nc.Block("consts") as blk:
        @blk.sync
        def _(s):
            P.inc("ldc", s.dma_start(out=c["identf"][:], in_=ident_hbm[:, :]), 16)

        @blk.gpsimd
        def _(g):
            P.inc("ldc", g.dma_start(out=c["identb"][:], in_=ident_hbm[:, :]), 16)
            g.memset(c["ones_col_bf"][:], 1.0)
            g.memset(c["eps11"][:], EPS)
            g.memset(c["ones1_f"][:], 1.0)
            g.memset(c["ones1_bf"][:], 1.0)
            g.memset(c["ones_row_bf"][:], 1.0)

        @blk.vector
        def _(v):
            P.wait(v, "ldc")
    return c


def ln_tile_block(P, PS, W, src, dst, bfbuf, C, gcols=None, bcols=None, res=None, tag="ln",
                  tmp1=None, tmp2=None):
    nc = P.nc
    with P.scope():
        mu_row = P.sb([1, W], F32)
        r1 = P.sb([1, W], F32)
        r2 = P.sb([1, W], F32)
        ex2_row = r1
        var_row = r1
        srow = r2
        rstd_row = r1
        tmp1 = tmp1[:, 0:W] if tmp1 is not None else P.sb([128, W], F32)[:, :]
        tmp2 = tmp2[:, 0:W] if tmp2 is not None else P.sb([128, W], F32)[:, :]
        psA = PS[0:1, 6, 0:W]
        psB = PS[0:1, 7, 0:W]
        bcmu = PS[:, 4, 0:W]
        bcrstd = PS[:, 5, 0:W]
        with nc.Block(tag) as blk:
            @blk.vector
            def _(v):
                for c in range(NCH):
                    ins = v.tensor_copy(bfbuf[:, c, 0:W], src[c])
                P.inc("evV", ins)
            mark_cast = P.val("evV")

            @blk.tensor
            def _(t):
                P.wait(t, "evV", mark_cast)
                for c in range(NCH):
                    ins = t.matmul(psA, C["ones_col_bf"][:], bfbuf[:, c, 0:W],
                                   start=(c == 0), stop=(c == NCH - 1))
                P.inc("mm", ins)
            markA = P.val("mm")

            @blk.scalar
            def _(s):
                P.wait(s, "mm", markA)
                for c in range(NCH):
                    ins = s.activation(bfbuf[:, c, 0:W], src[c], AF.Square, bias=0.0, scale=1.0)
                P.inc("evS", ins)
            mark_sq = P.val("evS")

            @blk.tensor
            def _(t):
                P.wait(t, "evS", mark_sq)
                for c in range(NCH):
                    ins = t.matmul(psB, C["ones_col_bf"][:], bfbuf[:, c, 0:W],
                                   start=(c == 0), stop=(c == NCH - 1))
                P.inc("mm", ins)
            markB = P.val("mm")

            @blk.vector
            def _(v):
                v.wait_ge(P.sem("mm")[0], markB)
                P.fence(v, "fV", v.tensor_scalar(mu_row[:], psA, 1.0 / D, None, op0=OP.mult))
                P.fence(v, "fV", v.tensor_scalar(ex2_row[:], psB, 1.0 / D, None, op0=OP.mult))
                P.fence(v, "fV", v.tensor_tensor(r2[:], mu_row[:], mu_row[:], op=OP.mult))
                ins = v.tensor_tensor(var_row[:], ex2_row[:], r2[:], op=OP.subtract)
                P.inc("evV", ins)
            mark_var = P.val("evV")

            @blk.scalar
            def _(s):
                s.wait_ge(P.sem("evV")[0], mark_var)
                ins = s.activation(srow[:], var_row[:], AF.Sqrt, bias=C["eps11"][:, 0:1], scale=1.0)
                P.inc("evS", ins)
            mark_sqrt = P.val("evS")

            @blk.vector
            def _(v):
                v.wait_ge(P.sem("evS")[0], mark_sqrt)
                ins = v.reciprocal(rstd_row[:], srow[:])
                P.inc("evV", ins)
            mark_rstd = P.val("evV")

            @blk.tensor
            def _(t):
                t.wait_ge(P.sem("evV")[0], mark_rstd)
                t.matmul(bcmu, C["ones1_f"][:], mu_row[:], start=True, stop=True)
                ins = t.matmul(bcrstd, C["ones1_f"][:], rstd_row[:], start=True, stop=True)
                P.inc("mm", ins)
            markC = P.val("mm")

            @blk.vector
            def _(v):
                v.wait_ge(P.sem("mm")[0], markC)
                for c in range(NCH):
                    lastc = (c == NCH - 1)
                    P.fence(v, "fV", v.tensor_tensor(tmp1, src[c], bcmu, op=OP.subtract))
                    if gcols is None and res is None:
                        ins = v.tensor_tensor(dst[c], tmp1, bcrstd, op=OP.mult)
                    else:
                        P.fence(v, "fV", v.tensor_tensor(tmp2, tmp1, bcrstd, op=OP.mult))
                        if gcols is not None:
                            y_ap = tmp1
                            P.fence(v, "fV", v.tensor_scalar(y_ap, tmp2, gcols[:, c:c + 1], bcols[:, c:c + 1],
                                                             op0=OP.mult, op1=OP.add))
                        else:
                            y_ap = tmp2
                        if res is not None:
                            ins = v.tensor_tensor(dst[c], res[c], y_ap, op=OP.add)
                        else:
                            ins = v.tensor_copy(dst[c], y_ap)
                    if not lastc:
                        P.fence(v, "fV", ins)
                P.inc("evV", ins)


def emit_phase1(P, R, C):
    nc = P.nc
    H = P.hbm
    NG = 8
    NQT = 8
    xm = {}

    with P.scope():
        PS = P.psum([128, 8, 512], F32, "ps1")
        K_g = P.sb([128, T], BF16, "p1K")
        Q_g = P.sb([128, T], BF16, "p1Q")
        V_g = P.sb([128, 32, 130], BF16, "p1V")
        Z_g = P.sb([128, T], BF16, "p1Z")
        xbf0 = P.sb([128, NCH, 512], BF16, "p1x0")
        xbf = [xbf0, xbf0]
        wk = P.sb([128, NCH, 128], BF16, "p1wk")
        wq = P.sb([128, NCH, 128], BF16, "p1wq")
        wv = P.sb([128, NCH, 128], BF16, "p1wv")
        wo_g = P.sb([128, 1024], BF16, "p1wo")
        E_sb = P.sb([128, 2, 512], BF16, "p1E")
        trim = P.sb([128, 4, 512], BF16, "p1tri")
        QS_t = P.sb([128, 512], BF16, "p1qs")
        row_a = P.sb([1, 512], F32, "p1ra")
        row_b = P.sb([1, 512], F32, "p1rb")
        srow_a = P.sb([1, 512], F32, "p1sa")
        srow_b = P.sb([1, 512], F32, "p1sb")
        negB = P.sb([1, 2, 512], BF16, "p1nb")
        maxK2 = P.sb([1, 2], F32, "p1mk")
        zc_sb = P.sb([64, 512], F32, "p1zc")
        lntmp1 = P.sb([128, 512], F32, "p1lt1")
        lntmp2 = P.sb([128, 512], F32, "p1lt2")

        with nc.Block("p1prolog") as blk:
            @blk.gpsimd
            def _(g):
                g.memset(V_g[:, :, 64:65], 1.0)
                g.memset(V_g[:, :, 129:130], 1.0)
                P.inc("ldg", g.dma_start(
                    out=trim[:, :, :],
                    in_=H["cc_trimask"].rearrange("a p f -> p a f")), 16)

        for g_ in range(NG):
            _p1_group(P, R, C, H, PS, g_, xm,
                      K_g, Q_g, V_g, Z_g, xbf, wk, wq, wv, wo_g, E_sb, trim,
                      QS_t, row_a, row_b, srow_a, srow_b, negB, maxK2, zc_sb)

        for tt in range(NQT):
            q0 = tt * 512
            rs = [R[c][:, q0:q0 + 512] for c in range(NCH)]
            ln_tile_block(P, PS, 512, rs, rs, xbf0, C, tag="p1ln%d" % tt,
                          tmp1=lntmp1, tmp2=lntmp2)


def _p1_group(P, R, C, H, PS, g_, xm,
              K_g, Q_g, V_g, Z_g, xbf, wk, wq, wv, wo_g, E_sb, trim,
              QS_t, row_a, row_b, srow_a, srow_b, negB, maxK2, zc_sb):
    nc = P.nc
    NQT = 8

    mm0 = P.val("mm")
    mmark = {}
    m = mm0
    for n in range(8):
        m += 1; mmark[("K", n)] = m
        m += 1; mmark[("Q", n)] = m
        for pt in range(4):
            m += 1; mmark[("V", n, pt)] = m
    for sl in range(8):
        m += 1; mmark[("ks", sl)] = m
    s0 = P.val("evS")
    smark = {}
    s = s0
    for n in range(8):
        s += 1; smark[("K", n)] = s
        s += 1; smark[("Q", n)] = s
    for sl in range(8):
        s += 1; smark[("ksq", sl)] = s
    v0 = P.val("evV")
    vmark = {}
    v = v0
    for n in range(8):
        for pt in range(4):
            v += 1; vmark[("V", n, pt)] = v
    for sl in range(8):
        v += 1; vmark[("mx", sl)] = v

    with nc.Block("p1g%dprod" % g_) as blk:
        @blk.gpsimd
        def _(g):
            P.inc("ldg", g.dma_start(
                out=wk[:, :, :],
                in_=H["l0a_WKT"][:, g_ * 128:(g_ + 1) * 128].rearrange("(c p) hd -> p c hd", p=128)), 16)
            P.inc("ldg", g.dma_start(
                out=wq[:, :, :],
                in_=H["l0a_WQT"][:, g_ * 128:(g_ + 1) * 128].rearrange("(c p) hd -> p c hd", p=128)), 16)
            P.inc("ldg", g.dma_start(
                out=wv[:, :, :],
                in_=H["l0a_WVT"][:, g_ * 128:(g_ + 1) * 128].rearrange("(c p) hd -> p c hd", p=128)), 16)
            P.inc("ldg", g.dma_start(
                out=wo_g[:, :], in_=H["l0a_WOT"][g_ * 128:(g_ + 1) * 128, :]), 16)
            for n in range(8):
                if n >= 1:
                    g.wait_ge(P.sem("mm")[0], mmark[("V", n - 1, 3)])
                xm[(g_, n)] = P.inc("xd1", g.dma_start(
                    out=xbf[n % 2][:, :, :],
                    in_=H["stream_in"][:, n * 512:(n + 1) * 512].rearrange("(c p) t -> p c t", p=128)), 16)

        @blk.tensor
        def _(t):
            P.wait(t, "ldg")
            for n in range(8):
                t.wait_ge(P.sem("xd1")[0], xm[(g_, n)])
                if n >= 1:
                    t.wait_ge(P.sem("evS")[0], smark[("Q", n - 1)])
                for k in range(NCH):
                    ins = t.matmul(PS[:, 0, :], wk[:, k, :], xbf[n % 2][:, k, :],
                                   start=(k == 0), stop=(k == NCH - 1))
                P.inc("mm", ins)
                for k in range(NCH):
                    ins = t.matmul(PS[:, 1, :], wq[:, k, :], xbf[n % 2][:, k, :],
                                   start=(k == 0), stop=(k == NCH - 1))
                P.inc("mm", ins)
                for pt in range(4):
                    if (n, pt) != (0, 0) and (n, pt) != (0, 1):
                        prev = (n, pt - 2) if pt >= 2 else (n - 1, pt + 2)
                        t.wait_ge(P.sem("evV")[0], vmark[("V", prev[0], prev[1])])
                    for k in range(NCH):
                        ins = t.matmul(PS[:, 2 + pt % 2, 0:128],
                                       xbf[n % 2][:, k, pt * 128:(pt + 1) * 128],
                                       wv[:, k, :], start=(k == 0), stop=(k == NCH - 1))
                    P.inc("mm", ins)
            for sl in range(8):
                t.wait_ge(P.sem("evS")[0], smark[("ksq", sl)])
                if sl >= 1:
                    t.wait_ge(P.sem("evV")[0], vmark[("mx", sl - 1)])
                t.matmul(PS[0:1, 6, :], C["ones_col_bf"][0:64, 0:1], QS_t[0:64, :],
                         start=True, stop=True)
                ins = t.matmul(PS[0:1, 7, :], C["ones_col_bf"][64:128, 0:1], QS_t[64:128, :],
                               start=True, stop=True)
                P.inc("mm", ins)

        @blk.scalar
        def _(s):
            for n in range(8):
                s.wait_ge(P.sem("mm")[0], mmark[("K", n)])
                ins = s.activation(K_g[:, n * 512:(n + 1) * 512], PS[:, 0, :], AF.Copy,
                                   bias=0.0, scale=1.0)
                P.inc("evS", ins)
                s.wait_ge(P.sem("mm")[0], mmark[("Q", n)])
                ins = s.activation(Q_g[:, n * 512:(n + 1) * 512], PS[:, 1, :], AF.Copy,
                                   bias=0.0, scale=1.0)
                P.inc("evS", ins)
            for sl in range(8):
                if sl == 0:
                    s.wait_ge(P.sem("evS")[0], smark[("Q", 7)])
                else:
                    s.wait_ge(P.sem("mm")[0], mmark[("ks", sl - 1)])
                ins = s.activation(QS_t[:, :], K_g[:, sl * 512:(sl + 1) * 512], AF.Square,
                                   bias=0.0, scale=1.0)
                P.inc("evS", ins)
                assert P.val("evS") == smark[("ksq", sl)]

        @blk.vector
        def _(v):
            for n in range(8):
                for pt in range(4):
                    v.wait_ge(P.sem("mm")[0], mmark[("V", n, pt)])
                    pc = n * 4 + pt
                    src = PS[:, 2 + pt % 2, 0:128].rearrange("p (h dh) -> p h dh", h=2)
                    dst = V_g[:, pc, 0:130].rearrange("p (h dhx) -> p h dhx", dhx=65)[:, :, 0:64]
                    ins = v.tensor_copy(dst, src)
                    P.inc("evV", ins)
                    assert P.val("evV") == vmark[("V", n, pt)]
            for sl in range(8):
                v.wait_ge(P.sem("mm")[0], mmark[("ks", sl)])
                P.fence(v, "fV", v.tensor_reduce(row_a[:, 0:1], PS[0:1, 6, :], axis=AX, op=OP.max))
                P.fence(v, "fV", v.tensor_reduce(row_b[:, 0:1], PS[0:1, 7, :], axis=AX, op=OP.max))
                if sl == 0:
                    P.fence(v, "fV", v.tensor_copy(maxK2[:, 0:1], row_a[:, 0:1]))
                    ins = v.tensor_copy(maxK2[:, 1:2], row_b[:, 0:1])
                else:
                    P.fence(v, "fV", v.tensor_tensor(maxK2[:, 0:1], maxK2[:, 0:1], row_a[:, 0:1], op=OP.max))
                    ins = v.tensor_tensor(maxK2[:, 1:2], maxK2[:, 1:2], row_b[:, 0:1], op=OP.max)
                P.inc("evV", ins)
                assert P.val("evV") == vmark[("mx", sl)]

    for qt in range(NQT):
        _p1_qtile(P, R, C, PS, g_, qt,
                  K_g, Q_g, V_g, Z_g, wo_g, E_sb, trim,
                  QS_t, row_a, row_b, srow_a, srow_b, negB, maxK2, zc_sb)


def _p1_qtile(P, R, C, PS, g_, qt,
              K_g, Q_g, V_g, Z_g, wo_g, E_sb, trim,
              QS_t, row_a, row_b, srow_a, srow_b, negB, maxK2, zc_sb):
    nc = P.nc
    qs = qt * 512
    npc = 4 * (qt + 1)

    mm0 = P.val("mm")
    s0 = P.val("evS")
    v0 = P.val("evV")
    mmark, smark, vmark = {}, {}, {}
    m = mm0
    m += 1; mmark["qrows"] = m
    for h in range(2):
        for j in range(npc):
            m += 1; mmark[("S", h, j)] = m
            m += 1; mmark[("A", h, j)] = m
        m += 1; mmark[("bc", h)] = m
    for dt in range(NCH):
        m += 1; mmark[("wo", dt)] = m
    s = s0
    s += 1; smark["qsq"] = s
    s += 1; smark["sqrt"] = s
    for h in range(2):
        for j in range(npc):
            s += 1; smark[("E", h, j)] = s
    v = v0
    v += 1; vmark["rows"] = v
    v += 1; vmark["negb"] = v
    for h in range(2):
        for j in range(max(0, npc - 4), npc):
            v += 1; vmark[("tm", h, j)] = v
        v += 1; vmark[("rl", h)] = v
        v += 1; vmark[("zn", h)] = v
    for dt in range(NCH):
        v += 1; vmark[("ra", dt)] = v

    with nc.Block("p1g%dq%d" % (g_, qt)) as blk:
        @blk.scalar
        def _(s):
            ins = s.activation(QS_t[:, :], Q_g[:, qs:qs + 512], AF.Square, bias=0.0, scale=1.0)
            P.inc("evS", ins)
            assert P.val("evS") == smark["qsq"]
            s.wait_ge(P.sem("evV")[0], vmark["rows"])
            s.activation(srow_a[:, :], row_a[:, :], AF.Sqrt, bias=0.0, scale=1.0)
            ins = s.activation(srow_b[:, :], row_b[:, :], AF.Sqrt, bias=0.0, scale=1.0)
            P.inc("evS", ins)
            assert P.val("evS") == smark["sqrt"]
            for h in range(2):
                for j in range(npc):
                    s.wait_ge(P.sem("mm")[0], mmark[("S", h, j)])
                    if (h, j) > (0, 1):
                        ph, pj = (h, j - 2) if j >= 2 else (h - 1, npc - 1 if j == 1 else npc - 2)
                        t_ = mmark[("A", ph, pj)]
                        s.wait_ge(P.sem("mm")[0], t_)
                    ins = s.activation(E_sb[:, j % 2, :], PS[:, j % 4, :], AF.Exp,
                                       bias=0.0, scale=0.125)
                    P.inc("evS", ins)
                    assert P.val("evS") == smark[("E", h, j)]

        @blk.tensor
        def _(t):
            t.wait_ge(P.sem("evS")[0], smark["qsq"])
            t.matmul(PS[0:1, 6, :], C["ones_col_bf"][0:64, 0:1], QS_t[0:64, :], start=True, stop=True)
            ins = t.matmul(PS[0:1, 7, :], C["ones_col_bf"][64:128, 0:1], QS_t[64:128, :], start=True, stop=True)
            P.inc("mm", ins)
            assert P.val("mm") == mmark["qrows"]
            t.wait_ge(P.sem("evV")[0], vmark["negb"])
            for h in range(2):
                hh = 64 * h
                for j in range(npc):
                    if j >= 4:
                        t.wait_ge(P.sem("evS")[0], smark[("E", h, j - 4)])
                    elif h == 1 and npc - 4 + j >= 0:
                        t.wait_ge(P.sem("evS")[0], smark[("E", 0, npc - 4 + j)])
                    t.matmul(PS[:, j % 4, :], K_g[hh:hh + 64, j * 128:(j + 1) * 128],
                             Q_g[hh:hh + 64, qs:qs + 512], start=True, stop=False)
                    ins = t.matmul(PS[:, j % 4, :], C["ones1_bf"][:],
                                   negB[:, h, :], start=False, stop=True)
                    P.inc("mm", ins)
                    assert P.val("mm") == mmark[("S", h, j)]
                    t.wait_ge(P.sem("evS")[0], smark[("E", h, j)])
                    if j >= npc - 4:
                        t.wait_ge(P.sem("evV")[0], vmark[("tm", h, j)])
                    ins = t.matmul(PS[0:65, 4, :], V_g[:, j, 65 * h:65 * h + 65],
                                   E_sb[:, j % 2, :], start=(j == 0), stop=(j == npc - 1))
                    P.inc("mm", ins)
                    assert P.val("mm") == mmark[("A", h, j)]
                t.wait_ge(P.sem("evV")[0], vmark[("rl", h)])
                ins = t.matmul(PS[:, 5, :], C["ones1_f"][:], row_a[:, :], start=True, stop=True)
                P.inc("mm", ins)
                assert P.val("mm") == mmark[("bc", h)]
            t.wait_ge(P.sem("evV")[0], vmark[("zn", 1)])
            for dt in range(NCH):
                if dt >= 2:
                    t.wait_ge(P.sem("evV")[0], vmark[("ra", dt - 2)])
                ins = t.matmul(PS[:, 6 + dt % 2, :], wo_g[:, dt * 128:(dt + 1) * 128],
                               Z_g[:, qs:qs + 512], start=True, stop=True)
                P.inc("mm", ins)
                assert P.val("mm") == mmark[("wo", dt)]

        @blk.vector
        def _(v):
            v.wait_ge(P.sem("mm")[0], mmark["qrows"])
            P.fence(v, "fV", v.tensor_scalar(row_a[:, :], PS[0:1, 6, :], maxK2[0:1, 0:1], None, op0=OP.mult))
            ins = v.tensor_scalar(row_b[:, :], PS[0:1, 7, :], maxK2[0:1, 1:2], None, op0=OP.mult)
            P.inc("evV", ins)
            assert P.val("evV") == vmark["rows"]
            v.wait_ge(P.sem("evS")[0], smark["sqrt"])
            P.fence(v, "fV", v.tensor_scalar(negB[:, 0, :], srow_a[:, :], -1.0, None, op0=OP.mult))
            ins = v.tensor_scalar(negB[:, 1, :], srow_b[:, :], -1.0, None, op0=OP.mult)
            P.inc("evV", ins)
            assert P.val("evV") == vmark["negb"]
            for h in range(2):
                for j in range(max(0, npc - 4), npc):
                    a = j - (npc - 4)
                    v.wait_ge(P.sem("evS")[0], smark[("E", h, j)])
                    ins = v.tensor_tensor(E_sb[:, j % 2, :], E_sb[:, j % 2, :], trim[:, a, :], op=OP.mult)
                    P.inc("evV", ins)
                    assert P.val("evV") == vmark[("tm", h, j)]
                v.wait_ge(P.sem("mm")[0], mmark[("A", h, npc - 1)])
                P.fence(v, "fV", v.reciprocal(row_a[:, :], PS[64:65, 4, :]))
                P.inc("evV", v.tensor_copy(zc_sb[:, :], PS[0:64, 4, :]))
                assert P.val("evV") == vmark[("rl", h)]
                v.wait_ge(P.sem("mm")[0], mmark[("bc", h)])
                ins = v.tensor_tensor(Z_g[64 * h:64 * h + 64, qs:qs + 512], zc_sb[:, :],
                                      PS[0:64, 5, :], op=OP.mult)
                P.inc("evV", ins)
                assert P.val("evV") == vmark[("zn", h)]
            for dt in range(NCH):
                v.wait_ge(P.sem("mm")[0], mmark[("wo", dt)])
                ins = v.tensor_tensor(R[dt][:, qs:qs + 512], R[dt][:, qs:qs + 512],
                                      PS[:, 6 + dt % 2, :], op=OP.add)
                P.inc("evV", ins)
                assert P.val("evV") == vmark[("ra", dt)]


def emit_phase3(P, R, C):
    nc = P.nc
    H = P.hbm
    TC = 256
    NCK = T // TC
    wim, wsm, wom = {}, {}, {}

    with P.scope():
        PS = P.psum([128, 8, 512], F32, "ps3")
        xbf = P.sb([128, NCH, TC], BF16, "p3x")
        gate = P.sb([128, NCH, TC], F32, "p3g")
        val_bf = P.sb([128, NCH, TC], BF16, "p3v")
        tanh_sb = P.sb([128, NCH, TC], F32, "p3t")
        a_sb = P.sb([128, NCH, TC], F32, "p3a")
        b_sb = P.sb([128, NCH, TC], F32, "p3b")
        h_sb = P.sb([128, NCH, TC], F32, "p3h")
        hstate = P.sb([128, NCH], F32, "p3hs")
        osb = P.sb([128, 2, TC], F32, "p3o")
        winbuf = P.sb([128, 2, NCH, 128], BF16, "p3wi")
        wsbuf = P.sb([128, 2, NCH, 128], BF16, "p3ws")
        wobuf = P.sb([128, 2, NCH, 128], BF16, "p3wo")
        bin_c = P.sb([128, 16], F32, "p3bin")
        binrow = P.sb([1, D], BF16, "p3binrow")
        bs_c = P.sb([128, NCH], F32, "p3bs")
        bo_c = P.sb([128, NCH], F32, "p3bo")
        borow = P.sb([1, D], BF16, "p3borow")

        with nc.Block("p3prolog") as blk:
            @blk.sync
            def _(s):
                for m in range(16):
                    P.inc("lds", s.dma_start(
                        out=bin_c[:, m:m + 1],
                        in_=H["l1a_b_in"][m * 128:(m + 1) * 128].rearrange("(p one) -> p one", one=1)), 16)
                for m in range(NCH):
                    P.inc("lds", s.dma_start(
                        out=bs_c[:, m:m + 1],
                        in_=H["l1a_b_s"][m * 128:(m + 1) * 128].rearrange("(p one) -> p one", one=1)), 16)
                    P.inc("lds", s.dma_start(
                        out=bo_c[:, m:m + 1],
                        in_=H["l1a_b_o"][m * 128:(m + 1) * 128].rearrange("(p one) -> p one", one=1)), 16)

            @blk.gpsimd
            def _(g):
                P.inc("ldg", g.dma_start(
                    out=binrow[:, :],
                    in_=H["l1a_b_in"][D:2 * D].rearrange("(one e) -> one e", one=1)), 16)
                P.inc("ldg", g.dma_start(
                    out=borow[:, :],
                    in_=H["l1a_b_o"].rearrange("(one e) -> one e", one=1)), 16)
                for m in range(2):
                    wim[(0, m)] = P.inc("w1d3", g.dma_start(
                        out=winbuf[:, m % 2, :, :],
                        in_=H["l1a_WinT"][:, m * 128:(m + 1) * 128].rearrange("(c p) e -> p c e", p=128)), 16)

            @blk.vector
            def _(v):
                P.wait(v, "lds")
                P.wait(v, "ldg")

        def p3_chunk(ct):
            q0 = ct * TC
            last = (ct == NCK - 1)
            mm0 = P.val("mm")
            mmk = lambda g_: mm0 + g_ + 1
            s0 = P.val("evS")
            smk = lambda g_: s0 + g_ + 1
            v0 = P.val("evV")
            vm = {"cast": v0 + 1, "hbf": v0 + 2}
            for m in range(NCH):
                vm[("stt", m)] = v0 + 3 + m

            with nc.Block("p3c%d" % ct) as blk:
                @blk.gpsimd
                def _(g):
                    for m in range(2, 16):
                        g.wait_ge(P.sem("mm")[0], mmk(m - 2))
                        wim[(ct, m)] = P.inc("w1d3", g.dma_start(
                            out=winbuf[:, m % 2, :, :],
                            in_=H["l1a_WinT"][:, m * 128:(m + 1) * 128].rearrange("(c p) e -> p c e", p=128)), 16)
                    for m in range(NCH):
                        if m >= 2:
                            g.wait_ge(P.sem("mm")[0], mmk(16 + m - 2))
                        wsm[(ct, m)] = P.inc("w2d3", g.dma_start(
                            out=wsbuf[:, m % 2, :, :],
                            in_=H["l1a_WsT"][:, m * 128:(m + 1) * 128].rearrange("(c p) e -> p c e", p=128)), 16)
                    for m in range(NCH):
                        if m >= 2:
                            g.wait_ge(P.sem("mm")[0], mmk(24 + m - 2))
                        wom[(ct, m)] = P.inc("w3d3", g.dma_start(
                            out=wobuf[:, m % 2, :, :],
                            in_=H["l1a_WoT"][:, m * 128:(m + 1) * 128].rearrange("(c p) e -> p c e", p=128)), 16)
                    if not last:
                        for m in range(2):
                            g.wait_ge(P.sem("mm")[0], mmk(14 + m))
                            wim[(ct + 1, m)] = P.inc("w1d3", g.dma_start(
                                out=winbuf[:, m % 2, :, :],
                                in_=H["l1a_WinT"][:, m * 128:(m + 1) * 128].rearrange("(c p) e -> p c e", p=128)), 16)

                @blk.vector
                def _(v):
                    for c in range(NCH):
                        ins = v.tensor_copy(xbf[:, c, :], R[c][:, q0:q0 + TC])
                    P.inc("evV", ins)
                    assert P.val("evV") == vm["cast"]
                    v.wait_ge(P.sem("evS")[0], smk(23))
                    for c in range(NCH):
                        v.tensor_scalar(a_sb[:, c, :], gate[:, c, :], -1.0, 1.0, op0=OP.mult, op1=OP.add)
                    for c in range(NCH):
                        ins = v.tensor_tensor(b_sb[:, c, :], gate[:, c, :], tanh_sb[:, c, :], op=OP.mult)
                    P.fence(v, "fV", ins)
                    for c in range(NCH):
                        if ct == 0:
                            ins = v.tensor_tensor_scan(h_sb[:, c, :], a_sb[:, c, :], b_sb[:, c, :],
                                                       0.0, op0=OP.mult, op1=OP.add)
                        else:
                            ins = v.tensor_tensor_scan(h_sb[:, c, :], a_sb[:, c, :], b_sb[:, c, :],
                                                       hstate[:, c:c + 1], op0=OP.mult, op1=OP.add)
                    P.fence(v, "fV", ins)
                    for c in range(NCH):
                        v.tensor_copy(hstate[:, c:c + 1], h_sb[:, c, TC - 1:TC])
                    for c in range(NCH):
                        ins = v.tensor_copy(val_bf[:, c, :], h_sb[:, c, :])
                    P.inc("evV", ins)
                    assert P.val("evV") == vm["hbf"]
                    for m in range(NCH):
                        v.wait_ge(P.sem("evS")[0], smk(24 + m))
                        ins = v.scalar_tensor_tensor(R[m][:, q0:q0 + TC], R[m][:, q0:q0 + TC], 2.0,
                                                     osb[:, m % 2, :], op0=OP.mult, op1=OP.add)
                        P.inc("evV", ins)
                        assert P.val("evV") == vm[("stt", m)]

                @blk.tensor
                def _(t):
                    t.wait_ge(P.sem("evV")[0], vm["cast"])
                    for m in range(16):
                        t.wait_ge(P.sem("w1d3")[0], wim[(ct, m)])
                        if m >= 2:
                            t.wait_ge(P.sem("evS")[0], smk(m - 2))
                        isval = (m >= NCH)
                        for k in range(NCH):
                            ins = t.matmul(PS[:, m % 2, 0:TC], winbuf[:, m % 2, k, :],
                                           xbf[:, k, :], start=(k == 0), stop=(k == NCH - 1 and not isval))
                        if isval:
                            ins = t.matmul(PS[:, m % 2, 0:TC],
                                           binrow[0:1, (m - NCH) * 128:(m - NCH + 1) * 128],
                                           C["ones_row_bf"][0:1, 0:TC], start=False, stop=True)
                        P.inc("mm", ins)
                        assert P.val("mm") == mmk(m)
                    t.wait_ge(P.sem("evS")[0], smk(15))
                    for m in range(NCH):
                        t.wait_ge(P.sem("w2d3")[0], wsm[(ct, m)])
                        t.wait_ge(P.sem("evS")[0], smk(16 + m - 2))
                        for k in range(NCH):
                            ins = t.matmul(PS[:, m % 2, 0:TC], wsbuf[:, m % 2, k, :],
                                           val_bf[:, k, :], start=(k == 0), stop=(k == NCH - 1))
                        P.inc("mm", ins)
                        assert P.val("mm") == mmk(16 + m)
                    t.wait_ge(P.sem("evV")[0], vm["hbf"])
                    for m in range(NCH):
                        t.wait_ge(P.sem("w3d3")[0], wom[(ct, m)])
                        t.wait_ge(P.sem("evS")[0], smk(24 + m - 2))
                        for k in range(NCH):
                            ins = t.matmul(PS[:, m % 2, 0:TC], wobuf[:, m % 2, k, :],
                                           val_bf[:, k, :], start=(k == 0), stop=False)
                        ins = t.matmul(PS[:, m % 2, 0:TC],
                                       borow[0:1, m * 128:(m + 1) * 128],
                                       C["ones_row_bf"][0:1, 0:TC], start=False, stop=True)
                        P.inc("mm", ins)
                        assert P.val("mm") == mmk(24 + m)

                @blk.scalar
                def _(s):
                    for m in range(16):
                        s.wait_ge(P.sem("mm")[0], mmk(m))
                        if m < NCH:
                            ins = s.activation(gate[:, m, :], PS[:, m % 2, 0:TC], AF.Sigmoid,
                                               bias=bin_c[:, m:m + 1], scale=1.0)
                        else:
                            ins = s.activation(val_bf[:, m - NCH, :], PS[:, m % 2, 0:TC], AF.Copy,
                                               bias=0.0, scale=1.0)
                        P.inc("evS", ins)
                        assert P.val("evS") == smk(m)
                    for m in range(NCH):
                        s.wait_ge(P.sem("mm")[0], mmk(16 + m))
                        ins = s.activation(tanh_sb[:, m, :], PS[:, m % 2, 0:TC], AF.Tanh,
                                           bias=bs_c[:, m:m + 1], scale=1.0)
                        P.inc("evS", ins)
                        assert P.val("evS") == smk(16 + m)
                    for m in range(NCH):
                        s.wait_ge(P.sem("mm")[0], mmk(24 + m))
                        if m >= 2:
                            s.wait_ge(P.sem("evV")[0], vm[("stt", m - 2)])
                        ins = s.activation(osb[:, m % 2, :], PS[:, m % 2, 0:TC], AF.Copy,
                                           bias=0.0, scale=1.0)
                        P.inc("evS", ins)
                        assert P.val("evS") == smk(24 + m)

        for ct in range(NCK):
            p3_chunk(ct)
        for tt in range(NCK):
            q0 = tt * TC
            rs = [R[c][:, q0:q0 + TC] for c in range(NCH)]
            ln_tile_block(P, PS, TC, rs, rs, xbf, C, tag="p3ln%d" % tt,
                          tmp1=gate[:, 0, :], tmp2=gate[:, 1, :])


def emit_phase2(P, R, C):
    nc = P.nc
    H = P.hbm
    TTW = 256
    NTT = T // TTW
    NFT = 8
    selrows = P.dram([NEXP, T], F32, "p2selrows")
    cc_in = P.dram([NEXP, 1], F32, "p2ccin")
    cc_out = P.dram([NEXP * NCORES, 1], F32, "p2ccout")
    w1m, w2m, sl8m, slem = {}, {}, {}, {}

    with P.scope():
        PS = P.psum([128, 8, 512], F32, "ps2")
        bfbuf = P.sb([128, NCH, TTW], BF16, "p2bf")
        acc = P.sb([128, NCH, TTW], F32, "p2acc")
        h_bf = P.sb([128, NFT, TTW], BF16, "p2h")
        w1buf = P.sb([128, 2, NCH, 128], BF16, "p2w1")
        w2buf = P.sb([128, 2, 1024], BF16, "p2w2")
        bcsel = P.sb([128, TTW], F32, "p2bcsel")
        otmp = P.sb([128, TTW], F32, "p2otmp")
        sls8 = P.sb([NEXP, TTW], BF16, "p2sls8")
        selrow = P.sb([1, 2, TTW], BF16, "p2selrow")
        wr_sb = P.sb([128, NCH, NEXP], F32, "p2wr")
        br_row = P.sb([1, NEXP], F32, "p2br")
        b2_sb = P.sb([NEXP, 1024], BF16, "p2b2")
        b1c = P.sb([128, NEXP, NFT], F32, "p2b1c")
        rmask = P.sb([NEXP, 1], F32, "p2rmask")

        with nc.Block("p2prolog") as blk:
            @blk.sync
            def _(s):
                P.inc("lds", s.dma_start(
                    out=wr_sb[:, :, :],
                    in_=H["l0_Wr"].rearrange("(c p) e -> p c e", p=128)), 16)
                P.inc("lds", s.dma_start(
                    out=br_row[:, :],
                    in_=H["l0_br"].rearrange("(one e) -> one e", one=1)), 16)
                P.inc("lds", s.dma_start(out=rmask[:, :], in_=H["rankmask"][:, :]), 16)
                for e in range(NEXP):
                    for ft in range(NFT):
                        P.inc("lds", s.dma_start(
                            out=b1c[:, e, ft:ft + 1],
                            in_=H["l0_b1"][e, ft * 128:(ft + 1) * 128].rearrange("(p one) -> p one", one=1)), 16)

            @blk.gpsimd
            def _(g):
                P.inc("ldg", g.dma_start(out=b2_sb[:, :], in_=H["l0_b2"][:, :]), 16)

        with P.scope():
            mask_fm = P.sb([NEXP, T], F32, "p2maskfm")
            pos_fm = P.sb([NEXP, T], F32, "p2posfm")
            kept = mask_fm
            sel_fm = pos_fm
            gatec = P.sb([128, 32], F32, "p2gate")
            counts = P.sb([NEXP, 1], F32, "p2counts")
            call_sb = P.sb([NEXP, NEXP], F32, "p2call")
            offs = P.sb([NEXP, 1], F32, "p2offs")
            lsc = {}
            for nm, sh in [("lcopy", [128, NEXP]), ("m1", [128, 1]), ("negm1", [128, 1]),
                           ("mask_tm", [128, NEXP]), ("eexp", [128, NEXP]),
                           ("den", [128, 1]), ("ktm", [128, NEXP]), ("stm", [128, NEXP])]:
                lsc[nm] = P.sb(sh, F32, "p2" + nm)
            lg_ps = PS[:, 0, 0:NEXP]
            tr_ps = PS[0:8, 1, 0:128]
            tb_ps = PS[:, 2, 0:NEXP]

            for t32 in range(32):
                t0 = t32 * 128
                with nc.Block("p2r%d" % t32) as blk:
                    @blk.tensor
                    def _(t):
                        if t32 == 0:
                            P.wait(t, "lds")
                            P.wait(t, "ldg")
                        for c in range(NCH):
                            t.matmul(lg_ps, R[c][:, t0:t0 + 128], wr_sb[:, c, :],
                                     start=(c == 0), stop=False)
                        ins = t.matmul(lg_ps, C["ones1_f"][:], br_row[:, :], start=False, stop=True)
                        P.inc("mm", ins)

                    @blk.vector
                    def _(v):
                        P.wait(v, "mm")
                        P.fence(v, "fV", v.tensor_copy(lsc["lcopy"][:], lg_ps))
                        P.fence(v, "fV", v.tensor_reduce(lsc["m1"][:], lsc["lcopy"][:], axis=AX, op=OP.max))
                        v.tensor_reduce(lsc["negm1"][:], lsc["lcopy"][:], axis=AX, op=OP.max, negate=True)
                        ins = v.tensor_scalar(lsc["mask_tm"][:], lsc["lcopy"][:], lsc["m1"][:, 0:1], None, op0=OP.is_equal)
                        P.inc("evV", ins)

                    @blk.scalar
                    def _(s):
                        P.wait(s, "evV")
                        ins = s.activation(lsc["eexp"][:], lsc["lcopy"][:], AF.Exp,
                                           bias=lsc["negm1"][:, 0:1], scale=1.0)
                        P.inc("evS", ins)

                    @blk.vector
                    def _(v):
                        P.wait(v, "evS")
                        P.fence(v, "fV", v.tensor_reduce(lsc["den"][:], lsc["eexp"][:], axis=AX, op=OP.add))
                        ins = v.reciprocal(gatec[:, t32:t32 + 1], lsc["den"][:])
                        P.inc("evV", ins)

                    @blk.tensor
                    def _(t):
                        P.wait(t, "evV")
                        ins = t.transpose(tr_ps, lsc["mask_tm"][:], C["identf"][:])
                        P.inc("mm", ins)

                    @blk.vector
                    def _(v):
                        P.wait(v, "mm")
                        ins = v.tensor_copy(mask_fm[:, t0:t0 + 128], tr_ps)
                        P.inc("evV", ins)

            with nc.Block("p2cc") as blk:
                @blk.vector
                def _(v):
                    ins = v.tensor_reduce(counts[:, 0:1], mask_fm[:, :], axis=AX, op=OP.add)
                    P.inc("evV", ins)

                @blk.sync
                def _(s):
                    P.wait(s, "evV")
                    P.inc("st", s.dma_start(out=cc_in[:, :], in_=counts[:, :]), 16)

                @blk.gpsimd
                def _(g):
                    P.wait(g, "st")
                    P.inc("cc", g.collective_compute(
                        "AllGather", OP.bypass,
                        replica_groups=[list(range(NCORES))],
                        ins=[cc_in[:, :].opt()],
                        outs=[cc_out[:, :].opt()]), 1)
                    P.wait(g, "cc")
                    P.inc("ldg", g.dma_start(
                        out=call_sb[:, :],
                        in_=cc_out.rearrange("(r e) one -> r (e one)", e=NEXP)), 16)

            with nc.Block("p2off") as blk:
                @blk.tensor
                def _(t):
                    P.wait(t, "ldg")
                    ins = t.matmul(PS[0:8, 3, 0:1], call_sb[:, :], rmask[:, :], start=True, stop=True)
                    P.inc("mm", ins)

                @blk.vector
                def _(v):
                    P.wait(v, "mm")
                    P.fence(v, "fV", v.tensor_copy(offs[:, 0:1], PS[0:8, 3, 0:1]))
                    P.fence(v, "fV", v.tensor_tensor_scan(pos_fm[:, :], mask_fm[:, :], mask_fm[:, :],
                                                          offs[:, 0:1], op0=OP.add, op1=OP.max))
                    P.fence(v, "fV", v.tensor_scalar(pos_fm[:, :], pos_fm[:, :], float(T * NCORES // NEXP), None, op0=OP.is_lt))
                    ins = v.tensor_tensor(kept[:, :], mask_fm[:, :], pos_fm[:, :], op=OP.mult)
                    P.inc("evV", ins)

            for t32 in range(32):
                t0 = t32 * 128
                with nc.Block("p2s%d" % t32) as blk:
                    @blk.tensor
                    def _(t):
                        ins = t.transpose(PS[:, 2, 0:NEXP], kept[:, t0:t0 + 128], C["identf"][0:8, 0:8])
                        P.inc("mm", ins)

                    @blk.vector
                    def _(v):
                        P.wait(v, "mm")
                        P.fence(v, "fV", v.tensor_copy(lsc["ktm"][:], tb_ps))
                        ins = v.tensor_scalar(lsc["stm"][:], lsc["ktm"][:], gatec[:, t32:t32 + 1], None, op0=OP.mult)
                        P.inc("evV", ins)

                    @blk.tensor
                    def _(t):
                        P.wait(t, "evV")
                        ins = t.transpose(tr_ps, lsc["stm"][:], C["identf"][:])
                        P.inc("mm", ins)

                    @blk.vector
                    def _(v):
                        P.wait(v, "mm")
                        ins = v.tensor_copy(sel_fm[:, t0:t0 + 128], tr_ps)
                        P.inc("evV", ins)

            with nc.Block("p2selst") as blk:
                @blk.sync
                def _(s):
                    P.wait(s, "evV")
                    P.inc("st", s.dma_start(out=selrows[:, :], in_=sel_fm[:, :]), 16)

                @blk.gpsimd
                def _(g):
                    for s_ in range(2):
                        w1m[(0, 0, s_)] = P.inc("w1d2", g.dma_start(
                            out=w1buf[:, s_ % 2, :, :],
                            in_=H["l0_W1"][0, :, s_ * 128:(s_ + 1) * 128].rearrange("(c p) f -> p c f", p=128)), 16)
                    for fc in range(2):
                        w2m[(0, 0, fc)] = P.inc("w2d2", g.dma_start(
                            out=w2buf[:, fc % 2, :],
                            in_=H["l0_W2"][0, fc * 128:(fc + 1) * 128, :]), 16)
                    P.wait(g, "st")
                    sl8m[0] = P.inc("sld2", g.dma_start(out=sls8[:, :], in_=selrows[:, 0:TTW]), 16)
                    slem[(0, 0)] = P.inc("sld2", g.dma_start(
                        out=selrow[:, 0, :], in_=selrows[0, 0:TTW].rearrange("(one f) -> one f", one=1)), 16)

        def p2_block(tt, e, q0):
            last = (tt == NTT - 1 and e == NEXP - 1)
            ne_tt, ne_e = (tt, e + 1) if e < NEXP - 1 else (tt + 1, 0)
            mm0 = P.val("mm")
            marks = {}
            m = mm0
            if e == 0:
                m += 1
                marks["b2"] = m
            m += 1
            marks["bc"] = m
            for ft in range(NFT):
                m += 1
                marks[("h", ft)] = m
            for fc in range(NFT):
                m += 1
                marks[("o", fc)] = m
            s0 = P.val("evS")
            smarks = {"bcsel": s0 + 1}
            for ft in range(NFT):
                smarks[("h", ft)] = s0 + 2 + ft
            v0 = P.val("evV")
            vmarks = {}
            v = v0
            if e == 0:
                v += 1
                vmarks["b2"] = v
            v += 1
            vmarks["cast"] = v
            v += 1
            vmarks["oev"] = v

            with nc.Block("p2t%de%d" % (tt, e)) as blk:
                @blk.gpsimd
                def _(g):
                    for s_ in range(2, NFT):
                        g.wait_ge(P.sem("mm")[0], marks[("h", s_ - 2)])
                        w1m[(tt, e, s_)] = P.inc("w1d2", g.dma_start(
                            out=w1buf[:, s_ % 2, :, :],
                            in_=H["l0_W1"][e, :, s_ * 128:(s_ + 1) * 128].rearrange("(c p) f -> p c f", p=128)), 16)
                    if not last:
                        for s_ in range(2):
                            g.wait_ge(P.sem("mm")[0], marks[("h", NFT - 2 + s_)])
                            w1m[(ne_tt, ne_e, s_)] = P.inc("w1d2", g.dma_start(
                                out=w1buf[:, s_ % 2, :, :],
                                in_=H["l0_W1"][ne_e, :, s_ * 128:(s_ + 1) * 128].rearrange("(c p) f -> p c f", p=128)), 16)
                    for fc in range(2, NFT):
                        g.wait_ge(P.sem("mm")[0], marks[("o", fc - 2)])
                        w2m[(tt, e, fc)] = P.inc("w2d2", g.dma_start(
                            out=w2buf[:, fc % 2, :],
                            in_=H["l0_W2"][e, fc * 128:(fc + 1) * 128, :]), 16)
                    if not last:
                        for fc in range(2):
                            g.wait_ge(P.sem("mm")[0], marks[("o", NFT - 2 + fc)])
                            w2m[(ne_tt, ne_e, fc)] = P.inc("w2d2", g.dma_start(
                                out=w2buf[:, fc % 2, :],
                                in_=H["l0_W2"][ne_e, fc * 128:(fc + 1) * 128, :]), 16)
                        nq0 = ne_tt * TTW
                        if ne_e == 0:
                            sl8m[ne_tt] = P.inc("sld2", g.dma_start(
                                out=sls8[:, :], in_=selrows[:, nq0:nq0 + TTW]), 16)
                        slem[(ne_tt, ne_e)] = P.inc("sld2", g.dma_start(
                            out=selrow[:, (tt * NEXP + e + 1) % 2, :],
                            in_=selrows[ne_e, nq0:nq0 + TTW].rearrange("(one f) -> one f", one=1)), 16)

                @blk.vector
                def _(v):
                    if e == 0:
                        v.wait_ge(P.sem("mm")[0], marks["b2"])
                        for dt in range(NCH):
                            ins = v.tensor_copy(acc[:, dt, :], PS[:, dt, 0:TTW])
                        P.inc("evV", ins)
                        assert P.val("evV") == vmarks["b2"]
                    v.wait_ge(P.sem("evS")[0], smarks["bcsel"])
                    for c in range(NCH):
                        ins = v.tensor_tensor(bfbuf[:, c, :], R[c][:, q0:q0 + TTW], bcsel[:, :], op=OP.mult)
                    P.inc("evV", ins)
                    assert P.val("evV") == vmarks["cast"]
                    v.wait_ge(P.sem("mm")[0], marks[("o", NFT - 1)])
                    for dt in range(NCH):
                        P.fence(v, "fV", v.tensor_tensor(otmp[:], PS[:, dt, 0:TTW], bcsel[:, :], op=OP.mult))
                        ins = v.tensor_tensor(acc[:, dt, :], acc[:, dt, :], otmp[:], op=OP.add)
                        if dt < NCH - 1:
                            P.fence(v, "fV", ins)
                    P.inc("evV", ins)
                    assert P.val("evV") == vmarks["oev"]

                @blk.tensor
                def _(t):
                    if e == 0:
                        t.wait_ge(P.sem("sld2")[0], sl8m[tt])
                        for dt in range(NCH):
                            ins = t.matmul(PS[:, dt, 0:TTW], b2_sb[:, dt * 128:(dt + 1) * 128],
                                           sls8[:, :], start=True, stop=True)
                        P.inc("mm", ins)
                        assert P.val("mm") == marks["b2"]
                        t.wait_ge(P.sem("evV")[0], vmarks["b2"])
                    t.wait_ge(P.sem("sld2")[0], slem[(tt, e)])
                    ins = t.matmul(PS[:, 2, 0:TTW], C["ones1_bf"][:], selrow[:, (tt * NEXP + e) % 2, :],
                                   start=True, stop=True)
                    P.inc("mm", ins)
                    assert P.val("mm") == marks["bc"]
                    t.wait_ge(P.sem("evV")[0], vmarks["cast"])
                    for ft in range(NFT):
                        t.wait_ge(P.sem("w1d2")[0], w1m[(tt, e, ft)])
                        if ft >= 2:
                            t.wait_ge(P.sem("evS")[0], smarks[("h", ft - 2)])
                        for k in range(NCH):
                            ins = t.matmul(PS[:, ft % 2, 0:TTW],
                                           w1buf[:, ft % 2, k, :],
                                           bfbuf[:, k, :], start=(k == 0), stop=(k == NCH - 1))
                        P.inc("mm", ins)
                        assert P.val("mm") == marks[("h", ft)]
                    t.wait_ge(P.sem("evS")[0], smarks[("h", NFT - 1)])
                    for fc in range(NFT):
                        t.wait_ge(P.sem("w2d2")[0], w2m[(tt, e, fc)])
                        for dt in range(NCH):
                            ins = t.matmul(PS[:, dt, 0:TTW], w2buf[:, fc % 2, dt * 128:(dt + 1) * 128],
                                           h_bf[:, fc, :], start=(fc == 0), stop=(fc == NFT - 1))
                        P.inc("mm", ins)
                        assert P.val("mm") == marks[("o", fc)]

                @blk.scalar
                def _(s):
                    s.wait_ge(P.sem("mm")[0], marks["bc"])
                    ins = s.activation(bcsel[:, :], PS[:, 2, 0:TTW], AF.Copy, bias=0.0, scale=1.0)
                    P.inc("evS", ins)
                    assert P.val("evS") == smarks["bcsel"]
                    for ft in range(NFT):
                        s.wait_ge(P.sem("mm")[0], marks[("h", ft)])
                        ins = s.activation(h_bf[:, ft, :], PS[:, ft % 2, 0:TTW], AF.Relu,
                                           bias=b1c[:, e, ft:ft + 1], scale=1.0)
                        P.inc("evS", ins)
                        assert P.val("evS") == smarks[("h", ft)]

        for tt in range(NTT):
            q0 = tt * TTW
            for e in range(NEXP):
                p2_block(tt, e, q0)
            srcs = [acc[:, c, :] for c in range(NCH)]
            rs = [R[c][:, q0:q0 + TTW] for c in range(NCH)]
            with nc.Block("p2res%d" % tt) as blk:
                @blk.vector
                def _(v):
                    for c in range(NCH):
                        ins = v.tensor_tensor(acc[:, c, :], acc[:, c, :], rs[c], op=OP.add)
                    P.inc("evV", ins)
            ln_tile_block(P, PS, TTW, srcs, rs, bfbuf, C, tag="p2ln%d" % tt,
                          tmp1=otmp, tmp2=bcsel)


def emit_phase4(P, R, C):
    nc = P.nc
    H = P.hbm
    TTW = 256
    NTT = T // TTW
    NFT = 32
    rwrows = P.dram([NEXP, T], F32, "p4rwrows")

    with P.scope():
        PS = P.psum([128, 8, 512], F32, "ps4")
        bfbuf = P.sb([128, NCH, TTW], BF16, "p4bf")
        acc = P.sb([128, NCH, TTW], F32, "p4acc")
        h_bf = P.sb([128, NFT, TTW], BF16, "p4h")
        w1buf = P.sb([128, 2, NCH, 128], BF16, "p4w1")
        w2buf = P.sb([128, 2, 1024], BF16, "p4w2")
        bcrw = P.sb([128, TTW], F32, "p4bcrw")
        otmp = P.sb([128, TTW], F32, "p4otmp")
        rws8 = P.sb([NEXP, TTW], BF16, "p4rws8")
        rwrow = P.sb([1, 2, TTW], BF16, "p4rwrow")
        wr_sb = P.sb([128, NCH, NEXP], F32, "p4wr")
        br_row = P.sb([1, NEXP], F32, "p4br")
        b2_sb = P.sb([NEXP, 1024], BF16, "p4b2")
        b1c = P.sb([128, NEXP, NFT], F32, "p4b1c")
        gcols = P.sb([128, NCH], F32, "p4g")
        bcols = P.sb([128, NCH], F32, "p4b")

        with nc.Block("p4prolog") as blk:
            @blk.sync
            def _(s):
                P.inc("lds", s.dma_start(
                    out=wr_sb[:, :, :],
                    in_=H["l1f_WrT"].rearrange("(c p) e -> p c e", p=128)), 16)
                P.inc("lds", s.dma_start(
                    out=br_row[:, :],
                    in_=H["l1f_br"].rearrange("(one e) -> one e", one=1)), 16)
                for c in range(NCH):
                    P.inc("lds", s.dma_start(
                        out=gcols[:, c:c + 1],
                        in_=H["l1f_gamma"][c * 128:(c + 1) * 128].rearrange("(p one) -> p one", one=1)), 16)
                    P.inc("lds", s.dma_start(
                        out=bcols[:, c:c + 1],
                        in_=H["l1f_beta"][c * 128:(c + 1) * 128].rearrange("(p one) -> p one", one=1)), 16)
                for e in range(NEXP):
                    for ft in range(NFT):
                        P.inc("lds", s.dma_start(
                            out=b1c[:, e, ft:ft + 1],
                            in_=H["l1f_b1"][e, ft * 128:(ft + 1) * 128].rearrange("(p one) -> p one", one=1)), 16)

            @blk.gpsimd
            def _(g):
                P.inc("ldg", g.dma_start(out=b2_sb[:, :], in_=H["l1f_b2"][:, :]), 16)

        with P.scope():
            rw_fm = P.sb([NEXP, T], F32, "p4rwfm")
            lsc = {}
            for nm, sh in [("lcopy", [128, NEXP]), ("m1", [128, 1]), ("negm1", [128, 1]),
                           ("is1", [128, NEXP]), ("tmp8", [128, NEXP]), ("m2", [128, 1]),
                           ("keep", [128, NEXP]), ("eexp", [128, NEXP]), ("ek", [128, NEXP]),
                           ("den", [128, 1]), ("rden", [128, 1]), ("rw_tm", [128, NEXP])]:
                lsc[nm] = P.sb(sh, F32, "p4" + nm)
            _p4_router(P, R, C, H, PS, wr_sb, br_row, rw_fm, lsc, rwrows, w1buf, w2buf, rws8, rwrow)
            if P4_DEBUG == 1:
                P.skip_store = True
                with nc.Block("dbg1") as blk:
                    @blk.sync
                    def _(s):
                        P.inc("st", s.dma_start(out=H["stream_out"][0:8, :], in_=rw_fm[:, :]), 16)
                        P.wait(s, "st")
                return

        for tt in range(1 if P4_DEBUG in (2, 3) else (P4_DEBUG_NT if P4_DEBUG == 4 else NTT)):
            q0 = tt * TTW
            for e in range(NEXP if P4_DEBUG != 3 else P4_DEBUG_NE):
                _p4_block(P, R, C, H, PS, tt, e, q0, TTW,
                          bfbuf, acc, h_bf, w1buf, w2buf, bcrw, otmp, rws8, rwrow, b1c, b2_sb, rwrows)
            if P4_DEBUG == 3:
                P.skip_store = True
                with nc.Block("dbg3") as blk:
                    @blk.gpsimd
                    def _(g):
                        for ft in range(16):
                            P.inc("st", g.dma_start(out=H["stream_out"][0:128, ft * TTW:(ft + 1) * TTW], in_=h_bf[:, ft, :]), 16)
                        for ft in range(16, NFT):
                            P.inc("st", g.dma_start(out=H["stream_out"][384:512, (ft - 16) * TTW:(ft - 15) * TTW], in_=h_bf[:, ft, :]), 16)
                        P.inc("st", g.dma_start(out=H["stream_out"][128:256, 0:TTW], in_=bcrw[:, :]), 16)
                        for c in range(NCH):
                            P.inc("st", g.dma_start(out=H["stream_out"][256:384, c * TTW:(c + 1) * TTW], in_=acc[:, c, :]), 16)
                        P.wait(g, "st")
                return
            srcs = [acc[:, c, :] for c in range(NCH)]
            rs = [R[c][:, q0:q0 + TTW] for c in range(NCH)]
            with nc.Block("p4res%d" % tt) as blk:
                @blk.vector
                def _(v):
                    for c in range(NCH):
                        ins = v.tensor_tensor(acc[:, c, :], acc[:, c, :], rs[c], op=OP.add)
                    P.inc("evV", ins)
            if P4_DEBUG == 2:
                P.skip_store = True
                with nc.Block("dbg2") as blk:
                    @blk.sync
                    def _(s):
                        for c in range(NCH):
                            P.inc("st", s.dma_start(out=H["stream_out"][c * 128:(c + 1) * 128, 0:TTW], in_=acc[:, c, :]), 16)
                        P.wait(s, "st")
                return
            ln_tile_block(P, PS, TTW, srcs, rs, bfbuf, C,
                          gcols=gcols, bcols=bcols, res=rs, tag="p4ln%d" % tt,
                          tmp1=otmp, tmp2=bcrw)
        if P4_DEBUG == 4:
            return
        for tt in range(NTT):
            q0 = tt * TTW
            rs = [R[c][:, q0:q0 + TTW] for c in range(NCH)]
            ln_tile_block(P, PS, TTW, rs, rs, bfbuf, C, tag="p4lno%d" % tt,
                          tmp1=otmp, tmp2=bcrw)


def _p4_router(P, R, C, H, PS, wr_sb, br_row, rw_fm, lsc, rwrows, w1buf, w2buf, rws8, rwrow):
    nc = P.nc
    NEXP_ = NEXP
    lg_ps = PS[:, 0, 0:NEXP]
    tr_ps = PS[0:8, 1, 0:128]
    if True:
        for t32 in range(32):
            t0 = t32 * 128
            with nc.Block("p4r%d" % t32) as blk:
                @blk.tensor
                def _(t):
                    if t32 == 0:
                        P.wait(t, "lds")
                        P.wait(t, "ldg")
                    for c in range(NCH):
                        t.matmul(lg_ps, R[c][:, t0:t0 + 128], wr_sb[:, c, :],
                                 start=(c == 0), stop=False)
                    ins = t.matmul(lg_ps, C["ones1_f"][:], br_row[:, :], start=False, stop=True)
                    P.inc("mm", ins)

                @blk.vector
                def _(v):
                    P.wait(v, "mm")
                    P.fence(v, "fV", v.tensor_copy(lsc["lcopy"][:], lg_ps))
                    P.fence(v, "fV", v.tensor_reduce(lsc["m1"][:], lsc["lcopy"][:], axis=AX, op=OP.max))
                    v.tensor_reduce(lsc["negm1"][:], lsc["lcopy"][:], axis=AX, op=OP.max, negate=True)
                    P.fence(v, "fV", v.tensor_scalar(lsc["is1"][:], lsc["lcopy"][:], lsc["m1"][:, 0:1], None, op0=OP.is_equal))
                    P.fence(v, "fV", v.scalar_tensor_tensor(lsc["tmp8"][:], lsc["is1"][:], -1e30, lsc["lcopy"][:],
                                                            op0=OP.mult, op1=OP.add))
                    P.fence(v, "fV", v.tensor_reduce(lsc["m2"][:], lsc["tmp8"][:], axis=AX, op=OP.max))
                    ins = v.tensor_scalar(lsc["keep"][:], lsc["lcopy"][:], lsc["m2"][:, 0:1], None, op0=OP.is_ge)
                    P.inc("evV", ins)

                @blk.scalar
                def _(s):
                    P.wait(s, "evV")
                    ins = s.activation(lsc["eexp"][:], lsc["lcopy"][:], AF.Exp,
                                       bias=lsc["negm1"][:, 0:1], scale=1.0)
                    P.inc("evS", ins)

                @blk.vector
                def _(v):
                    P.wait(v, "evS")
                    P.fence(v, "fV", v.tensor_tensor(lsc["ek"][:], lsc["eexp"][:], lsc["keep"][:], op=OP.mult))
                    P.fence(v, "fV", v.tensor_reduce(lsc["den"][:], lsc["ek"][:], axis=AX, op=OP.add))
                    P.fence(v, "fV", v.reciprocal(lsc["rden"][:], lsc["den"][:]))
                    ins = v.tensor_scalar(lsc["rw_tm"][:], lsc["ek"][:], lsc["rden"][:, 0:1], None, op0=OP.mult)
                    P.inc("evV", ins)

                @blk.tensor
                def _(t):
                    P.wait(t, "evV")
                    ins = t.transpose(tr_ps, lsc["rw_tm"][:], C["identf"][:])
                    P.inc("mm", ins)

                @blk.vector
                def _(v):
                    P.wait(v, "mm")
                    ins = v.tensor_copy(rw_fm[:, t0:t0 + 128], tr_ps)
                    P.inc("evV", ins)

        with nc.Block("p4rw") as blk:
            @blk.sync
            def _(s):
                P.wait(s, "evV")
                P.inc("st", s.dma_start(out=rwrows[:, :], in_=rw_fm[:, :]), 16)

            @blk.gpsimd
            def _(g):
                TTW = 256
                for s_ in range(2):
                    P.w1_marks[(0, 0, s_)] = P.inc("w1d", g.dma_start(
                        out=w1buf[:, s_ % 2, :, :],
                        in_=H["l1f_W1T"][0, :, s_ * 128:(s_ + 1) * 128].rearrange("(c p) f -> p c f", p=128)), 16)
                for fc in range(2):
                    P.w2_marks[(0, 0, fc)] = P.inc("w2d", g.dma_start(
                        out=w2buf[:, fc % 2, :],
                        in_=H["l1f_W2T"][0, fc * 128:(fc + 1) * 128, :]), 16)
                P.wait(g, "st")
                P.rw8_marks[0] = P.inc("rwd", g.dma_start(out=rws8[:, :], in_=rwrows[:, 0:TTW]), 16)
                P.rwe_marks[(0, 0)] = P.inc("rwd", g.dma_start(
                    out=rwrow[:, 0, :], in_=rwrows[0, 0:TTW].rearrange("(one f) -> one f", one=1)), 16)


def _p4_block(P, R, C, H, PS, tt, e, q0, TTW, bfbuf, acc, h_bf, w1buf, w2buf, bcrw, otmp,
              rws8, rwrow, b1c, b2_sb, rwrows):
    nc = P.nc
    NFT = 32
    NTT = T // TTW
    last = (tt == NTT - 1 and e == NEXP - 1)
    ne_tt, ne_e = (tt, e + 1) if e < NEXP - 1 else (tt + 1, 0)

    mm0 = P.val("mm")
    marks = {}
    m = mm0
    if e == 0:
        m += 1
        marks["b2"] = m
    m += 1
    marks["bc"] = m
    for ft in range(NFT):
        m += 1
        marks[("h", ft)] = m
    for fc in range(NFT):
        m += 1
        marks[("o", fc)] = m
    mm_end = m
    s0 = P.val("evS")
    smarks = {"bcrw": s0 + 1}
    for ft in range(NFT):
        smarks[("h", ft)] = s0 + 2 + ft
    s_end = s0 + 2 + NFT - 1
    v0 = P.val("evV")
    vmarks = {}
    v = v0
    if e == 0:
        v += 1
        vmarks["cast"] = v
        v += 1
        vmarks["b2"] = v
    v += 1
    vmarks["oev"] = v
    v_end = v

    with nc.Block("p4t%de%d" % (tt, e)) as blk:
        @blk.gpsimd
        def _(g):
            for s_ in range(2, NFT):
                g.wait_ge(P.sem("mm")[0], marks[("h", s_ - 2)])
                P.w1_marks[(tt, e, s_)] = P.inc("w1d", g.dma_start(
                    out=w1buf[:, s_ % 2, :, :],
                    in_=H["l1f_W1T"][e, :, s_ * 128:(s_ + 1) * 128].rearrange("(c p) f -> p c f", p=128)), 16)
            if not last:
                for s_ in range(2):
                    g.wait_ge(P.sem("mm")[0], marks[("h", NFT - 2 + s_)])
                    P.w1_marks[(ne_tt, ne_e, s_)] = P.inc("w1d", g.dma_start(
                        out=w1buf[:, s_ % 2, :, :],
                        in_=H["l1f_W1T"][ne_e, :, s_ * 128:(s_ + 1) * 128].rearrange("(c p) f -> p c f", p=128)), 16)
            for fc in range(2, NFT):
                g.wait_ge(P.sem("mm")[0], marks[("o", fc - 2)])
                P.w2_marks[(tt, e, fc)] = P.inc("w2d", g.dma_start(
                    out=w2buf[:, fc % 2, :],
                    in_=H["l1f_W2T"][e, fc * 128:(fc + 1) * 128, :]), 16)
            if not last:
                for fc in range(2):
                    g.wait_ge(P.sem("mm")[0], marks[("o", NFT - 2 + fc)])
                    P.w2_marks[(ne_tt, ne_e, fc)] = P.inc("w2d", g.dma_start(
                        out=w2buf[:, fc % 2, :],
                        in_=H["l1f_W2T"][ne_e, fc * 128:(fc + 1) * 128, :]), 16)
                nq0 = ne_tt * TTW
                if ne_e == 0:
                    P.rw8_marks[ne_tt] = P.inc("rwd", g.dma_start(
                        out=rws8[:, :], in_=rwrows[:, nq0:nq0 + TTW]), 16)
                P.rwe_marks[(ne_tt, ne_e)] = P.inc("rwd", g.dma_start(
                    out=rwrow[:, (tt * NEXP + e + 1) % 2, :],
                    in_=rwrows[ne_e, nq0:nq0 + TTW].rearrange("(one f) -> one f", one=1)), 16)

        @blk.vector
        def _(v):
            if e == 0:
                for c in range(NCH):
                    ins = v.tensor_copy(bfbuf[:, c, :], R[c][:, q0:q0 + TTW])
                P.inc("evV", ins)
                assert P.val("evV") == vmarks["cast"]
                v.wait_ge(P.sem("mm")[0], marks["b2"])
                for dt in range(NCH):
                    ins = v.tensor_copy(acc[:, dt, :], PS[:, dt, 0:TTW])
                P.inc("evV", ins)
                assert P.val("evV") == vmarks["b2"]
            v.wait_ge(P.sem("mm")[0], marks[("o", NFT - 1)])
            v.wait_ge(P.sem("evS")[0], smarks["bcrw"])
            for dt in range(NCH):
                P.fence(v, "fV", v.tensor_tensor(otmp[:], PS[:, dt, 0:TTW], bcrw[:], op=OP.mult))
                ins = v.tensor_tensor(acc[:, dt, :], acc[:, dt, :], otmp[:], op=OP.add)
                if dt < NCH - 1:
                    P.fence(v, "fV", ins)
            P.inc("evV", ins)
            assert P.val("evV") == vmarks["oev"]

        @blk.tensor
        def _(t):
            if e == 0:
                t.wait_ge(P.sem("rwd")[0], P.rw8_marks[tt])
                for dt in range(NCH):
                    ins = t.matmul(PS[:, dt, 0:TTW], b2_sb[:, dt * 128:(dt + 1) * 128],
                                   rws8[:, :], start=True, stop=True)
                P.inc("mm", ins)
                assert P.val("mm") == marks["b2"]
                t.wait_ge(P.sem("evV")[0], vmarks["b2"])
            t.wait_ge(P.sem("rwd")[0], P.rwe_marks[(tt, e)])
            ins = t.matmul(PS[:, 2, 0:TTW], C["ones1_bf"][:], rwrow[:, (tt * NEXP + e) % 2, :],
                           start=True, stop=True)
            P.inc("mm", ins)
            assert P.val("mm") == marks["bc"]
            for ft in range(NFT):
                t.wait_ge(P.sem("w1d")[0], P.w1_marks[(tt, e, ft)])
                if ft >= 2:
                    t.wait_ge(P.sem("evS")[0], smarks[("h", ft - 2)])
                for k in range(NCH):
                    ins = t.matmul(PS[:, ft % 2, 0:TTW],
                                   w1buf[:, ft % 2, k, :],
                                   bfbuf[:, k, :], start=(k == 0), stop=(k == NCH - 1))
                P.inc("mm", ins)
                assert P.val("mm") == marks[("h", ft)]
            t.wait_ge(P.sem("evS")[0], smarks[("h", NFT - 1)])
            for fc in range(NFT):
                t.wait_ge(P.sem("w2d")[0], P.w2_marks[(tt, e, fc)])
                for dt in range(NCH):
                    ins = t.matmul(PS[:, dt, 0:TTW], w2buf[:, fc % 2, dt * 128:(dt + 1) * 128],
                                   h_bf[:, fc, :], start=(fc == 0), stop=(fc == NFT - 1))
                P.inc("mm", ins)
                assert P.val("mm") == marks[("o", fc)]

        @blk.scalar
        def _(s):
            s.wait_ge(P.sem("mm")[0], marks["bc"])
            ins = s.activation(bcrw[:], PS[:, 2, 0:TTW], AF.Copy, bias=0.0, scale=1.0)
            P.inc("evS", ins)
            assert P.val("evS") == smarks["bcrw"]
            for ft in range(NFT):
                s.wait_ge(P.sem("mm")[0], marks[("h", ft)])
                ins = s.activation(h_bf[:, ft, :], PS[:, ft % 2, 0:TTW], AF.Relu,
                                   bias=b1c[:, e, ft:ft + 1], scale=1.0)
                P.inc("evS", ins)
                assert P.val("evS") == smarks[("h", ft)]

    assert P.val("mm") == mm_end
    assert P.val("evS") == s_end
    assert P.val("evV") == v_end


def build_program(phases, fused_in_hbm=True):
    P = Prog()
    nc = P.nc
    H = {}
    P.hbm = H
    P.w1_marks = {}
    P.w2_marks = {}
    P.rw8_marks = {}
    P.rwe_marks = {}

    def din(nm, shape, dt=F32):
        H[nm] = nc.dram_tensor(nm, shape, dt, kind="ExternalInput").ap()

    din("cc_ident", [128, 128])
    din("stream_in", [D, T])
    if 2 in phases:
        din("rankmask", [NEXP, 1])
        din("l0_Wr", [D, NEXP])
        din("l0_br", [NEXP])
        din("l0_W1", [NEXP, D, D])
        din("l0_W2", [NEXP, D, D])
        din("l0_b1", [NEXP, D])
        din("l0_b2", [NEXP, D])
    H["stream_out"] = nc.dram_tensor("stream_out", [D, T], F32, kind="ExternalOutput").ap()
    if 1 in phases:
        din("l0a_WKT", [D, D])
        din("l0a_WQT", [D, D])
        din("l0a_WVT", [D, D])
        din("l0a_WOT", [D, D])
        din("cc_trimask", [4, 128, 512])
    if 3 in phases:
        din("l1a_WinT", [D, 2 * D])
        din("l1a_WsT", [D, D])
        din("l1a_WoT", [D, D])
        din("l1a_b_in", [2 * D])
        din("l1a_b_s", [D])
        din("l1a_b_o", [D])
    if 4 in phases:
        din("l1f_WrT", [D, NEXP])
        din("l1f_br", [NEXP])
        din("l1f_W1T", [NEXP, D, 4096])
        din("l1f_W2T", [NEXP, 4096, D])
        din("l1f_b1", [NEXP, 4096])
        din("l1f_b2", [NEXP, D])
        din("l1f_gamma", [D])
        din("l1f_beta", [D])

    R = [P.sb([128, T], F32, "R%d" % c) for c in range(NCH)]
    C = make_consts(P)

    with nc.Block("ldR") as blk:
        @blk.sync
        def _(s):
            for c in range(NCH):
                P.inc("lds", s.dma_start(out=R[c][:], in_=H["stream_in"][c * 128:(c + 1) * 128, :]), 16)

        @blk.vector
        def _(v):
            P.wait(v, "lds")

    for ph in phases:
        if ph == 1:
            emit_phase1(P, R, C)
        elif ph == 2:
            emit_phase2(P, R, C)
        elif ph == 3:
            emit_phase3(P, R, C)
        elif ph == 4:
            emit_phase4(P, R, C)
        else:
            raise NotImplementedError("phase %d" % ph)

    if getattr(P, "skip_store", False):
        P.top.close()
        return P
    with nc.Block("stR") as blk:
        @blk.sync
        def _(s):
            for c in range(NCH):
                P.inc("st", s.dma_start(out=H["stream_out"][c * 128:(c + 1) * 128, :], in_=R[c][:]), 16)
            P.wait(s, "st")

    P.top.close()
    return P


def prep_weights(ins):
    w = {}
    w["cc_ident"] = np.eye(128, dtype=np.float32)
    w["l0a_WKT"] = np.ascontiguousarray(ins["L0_attn_W_K"].reshape(D, D).T)
    w["l0a_WQT"] = np.ascontiguousarray(ins["L0_attn_W_Q"].reshape(D, D).T)
    w["l0a_WVT"] = np.ascontiguousarray(ins["L0_attn_W_V"].reshape(D, D).T)
    w["l0a_WOT"] = np.ascontiguousarray(ins["L0_attn_W_O"].T)
    tri = np.zeros((4, 128, 512), dtype=np.float32)
    for a in range(4):
        for pr in range(128):
            q0 = pr + 128 * a
            if q0 < 512:
                tri[a, pr, q0:] = 1.0
    w["cc_trimask"] = tri
    w["l0_Wr"] = np.ascontiguousarray(ins["L0_ffn_Wr"])
    w["l0_br"] = np.ascontiguousarray(ins["L0_ffn_br"])
    w["l0_W1"] = np.ascontiguousarray(ins["L0_ffn_W1"])
    w["l0_W2"] = np.ascontiguousarray(ins["L0_ffn_W2"])
    w["l0_b1"] = np.ascontiguousarray(ins["L0_ffn_b1"])
    w["l0_b2"] = np.ascontiguousarray(ins["L0_ffn_b2"])
    w["l1a_WinT"] = np.ascontiguousarray(ins["L1_attn_W_in"].T)
    w["l1a_WsT"] = np.ascontiguousarray(ins["L1_attn_W_s"].T)
    w["l1a_WoT"] = np.ascontiguousarray(ins["L1_attn_W_o"].T)
    w["l1a_b_in"] = np.ascontiguousarray(ins["L1_attn_b_in"])
    w["l1a_b_s"] = np.ascontiguousarray(ins["L1_attn_b_s"])
    w["l1a_b_o"] = np.ascontiguousarray(ins["L1_attn_b_o"])
    w["l1f_WrT"] = np.ascontiguousarray(ins["L1_ffn_Wr"].T)
    w["l1f_br"] = np.ascontiguousarray(ins["L1_ffn_br"])
    w["l1f_W1T"] = np.ascontiguousarray(ins["L1_ffn_W1"].transpose(0, 2, 1))
    w["l1f_W2T"] = np.ascontiguousarray(ins["L1_ffn_W2"].transpose(0, 2, 1))
    w["l1f_b1"] = np.ascontiguousarray(ins["L1_ffn_b1"])
    w["l1f_b2"] = np.ascontiguousarray(ins["L1_ffn_b2"])
    w["l1f_gamma"] = np.ascontiguousarray(ins["L1_ffn_gamma"])
    w["l1f_beta"] = np.ascontiguousarray(ins["L1_ffn_beta"])
    return w


def run_phases(phases, stream_in_cores, weights):
    P = build_program(phases)
    in_maps = []
    for cid in range(NCORES):
        m = {"stream_in": stream_in_cores[cid]}
        if "rankmask" in P.hbm:
            m["rankmask"] = (np.arange(NEXP) < cid).astype(np.float32).reshape(NEXP, 1)
        for nm, arr in weights.items():
            if nm in P.hbm and nm != "stream_out":
                m[nm] = arr
        in_maps.append(m)
    res = run_bass_kernel_spmd(P.nc, in_maps, core_ids=list(range(NCORES)))
    return [res.results[c]["stream_out"] for c in range(NCORES)]


PHASE_GROUPS = [(1,), (2,), (3,), (4,)]


def kernel(**inputs):
    x = inputs["L0_attn_x"]
    w = prep_weights(inputs)
    streams = [np.ascontiguousarray(x[b].T).astype(np.float32) for b in range(NCORES)]
    for group in PHASE_GROUPS:
        streams = run_phases(group, streams, w)
    y = np.stack([o.T for o in streams], axis=0)
    return np.ascontiguousarray(y.astype(np.float32))


# version 03
# speedup vs baseline: 1.4777x; 1.4777x over previous
import numpy as np
from contextlib import ExitStack, contextmanager

import concourse.bass as bass
import concourse.mybir as mybir
from concourse.bass_utils import run_bass_kernel_spmd
from concourse.alu_op_type import AluOpType as OP

F32 = mybir.dt.float32
BF16 = mybir.dt.bfloat16
AF = mybir.ActivationFunctionType
AX = mybir.AxisListType.X

D = 1024
T = 4096
NCH = 8
NCORES = 8
EPS = 1e-5
NEXP = 8
P4_DEBUG = 0
P4_DEBUG_NE = 1
P4_DEBUG_NT = 16
_ign = 0


class Prog:
    def __init__(self):
        self.nc = bass.Bass("TRN2", target_bir_lowering=False)
        self.top = ExitStack()
        self.ctx = self.top
        self.sems = {}
        self.uid = 0

    def name(self, p="t"):
        self.uid += 1
        return "%s%d" % (p, self.uid)

    def sb(self, shape, dtype, name=None):
        return self.ctx.enter_context(self.nc.sbuf_tensor(name or self.name(), shape, dtype))

    def psum(self, shape, dtype=F32, name=None):
        return self.ctx.enter_context(self.nc.psum_tensor(name or self.name("p"), shape, dtype))

    def dram(self, shape, dtype=F32, name=None):
        return self.nc.dram_tensor(name or self.name("d"), shape, dtype).ap()

    @contextmanager
    def scope(self):
        old = self.ctx
        try:
            with ExitStack() as es:
                self.ctx = es
                yield
        finally:
            self.ctx = old

    def sem(self, nm):
        if nm not in self.sems:
            self.sems[nm] = [self.top.enter_context(self.nc.semaphore(nm)), 0]
        return self.sems[nm]

    def inc(self, nm, instr, k=1):
        s = self.sem(nm)
        instr.then_inc(s[0], k)
        s[1] += k
        return s[1]

    def val(self, nm):
        return self.sem(nm)[1]

    def wait(self, eng, nm, v=None):
        s = self.sem(nm)
        eng.wait_ge(s[0], s[1] if v is None else v)

    def fence(self, eng, nm, instr):
        v = self.inc(nm, instr)
        eng.wait_ge(self.sem(nm)[0], v)
        return instr


def make_consts(P):
    nc = P.nc
    c = {}
    c["identf"] = P.sb([128, 128], F32, "identf")
    c["identb"] = P.sb([128, 128], BF16, "identb")
    c["ones_col_bf"] = P.sb([128, 1], BF16, "onescolbf")
    c["ones1_f"] = P.sb([1, 128], F32, "ones1f")
    c["ones1_bf"] = P.sb([1, 128], BF16, "ones1bf")
    c["eps11"] = P.sb([1, 1], F32, "eps11")
    c["ones_row_bf"] = P.sb([1, 512], BF16, "onesrowbf")
    ident_hbm = P.hbm["cc_ident"]
    with nc.Block("consts") as blk:
        @blk.sync
        def _(s):
            P.inc("ldc", s.dma_start(out=c["identf"][:], in_=ident_hbm[:, :]), 16)

        @blk.gpsimd
        def _(g):
            P.inc("ldc", g.dma_start(out=c["identb"][:], in_=ident_hbm[:, :]), 16)
            g.memset(c["ones_col_bf"][:], 1.0)
            g.memset(c["eps11"][:], EPS)
            g.memset(c["ones1_f"][:], 1.0)
            g.memset(c["ones1_bf"][:], 1.0)
            g.memset(c["ones_row_bf"][:], 1.0)

        @blk.vector
        def _(v):
            P.wait(v, "ldc")
    return c


def ln_tile_block(P, PS, W, src, dst, bfbuf, C, gcols=None, bcols=None, res=None, tag="ln",
                  tmp1=None, tmp2=None):
    nc = P.nc
    with P.scope():
        if tmp1 is not None and tmp2 is not None:
            mu_row = tmp1[0:1, 0:W]
            r1 = tmp2[0:1, 0:W]
        else:
            mu_row = P.sb([1, W], F32)[:, :]
            r1 = P.sb([1, W], F32)[:, :]
        r2 = P.sb([1, W], F32)
        ex2_row = r1
        var_row = r1
        srow = r2
        rstd_row = r1
        tmp1 = tmp1[:, 0:W] if tmp1 is not None else P.sb([128, W], F32)[:, :]
        tmp2 = tmp2[:, 0:W] if tmp2 is not None else P.sb([128, W], F32)[:, :]
        psA = PS[0:1, 6, 0:W]
        psB = PS[0:1, 7, 0:W]
        bcmu = PS[:, 4, 0:W]
        bcrstd = PS[:, 5, 0:W]
        with nc.Block(tag) as blk:
            @blk.vector
            def _(v):
                for c in range(NCH):
                    ins = v.tensor_copy(bfbuf[:, c, 0:W], src[c])
                P.inc("evV", ins)
            mark_cast = P.val("evV")

            @blk.tensor
            def _(t):
                P.wait(t, "evV", mark_cast)
                for c in range(NCH):
                    ins = t.matmul(psA, C["ones_col_bf"][:], bfbuf[:, c, 0:W],
                                   start=(c == 0), stop=(c == NCH - 1))
                P.inc("mm", ins)
            markA = P.val("mm")

            @blk.scalar
            def _(s):
                P.wait(s, "mm", markA)
                for c in range(NCH):
                    ins = s.activation(bfbuf[:, c, 0:W], src[c], AF.Square, bias=0.0, scale=1.0)
                P.inc("evS", ins)
            mark_sq = P.val("evS")

            @blk.tensor
            def _(t):
                P.wait(t, "evS", mark_sq)
                for c in range(NCH):
                    ins = t.matmul(psB, C["ones_col_bf"][:], bfbuf[:, c, 0:W],
                                   start=(c == 0), stop=(c == NCH - 1))
                P.inc("mm", ins)
            markB = P.val("mm")

            @blk.vector
            def _(v):
                v.wait_ge(P.sem("mm")[0], markB)
                P.fence(v, "fV", v.tensor_scalar(mu_row, psA, 1.0 / D, None, op0=OP.mult))
                P.fence(v, "fV", v.tensor_scalar(ex2_row, psB, 1.0 / D, None, op0=OP.mult))
                P.fence(v, "fV", v.tensor_tensor(r2[:], mu_row, mu_row, op=OP.mult))
                ins = v.tensor_tensor(var_row, ex2_row, r2[:], op=OP.subtract)
                P.inc("evV", ins)
            mark_var = P.val("evV")

            @blk.scalar
            def _(s):
                s.wait_ge(P.sem("evV")[0], mark_var)
                ins = s.activation(srow[:], var_row, AF.Sqrt, bias=C["eps11"][:, 0:1], scale=1.0)
                P.inc("evS", ins)
            mark_sqrt = P.val("evS")

            @blk.vector
            def _(v):
                v.wait_ge(P.sem("evS")[0], mark_sqrt)
                ins = v.reciprocal(rstd_row, srow[:])
                P.inc("evV", ins)
            mark_rstd = P.val("evV")

            @blk.tensor
            def _(t):
                t.wait_ge(P.sem("evV")[0], mark_rstd)
                t.matmul(bcmu, C["ones1_f"][:], mu_row, start=True, stop=True)
                ins = t.matmul(bcrstd, C["ones1_f"][:], rstd_row, start=True, stop=True)
                P.inc("mm", ins)
            markC = P.val("mm")

            @blk.vector
            def _(v):
                v.wait_ge(P.sem("mm")[0], markC)
                for c in range(NCH):
                    lastc = (c == NCH - 1)
                    P.fence(v, "fV", v.tensor_tensor(tmp1, src[c], bcmu, op=OP.subtract))
                    if gcols is None and res is None:
                        ins = v.tensor_tensor(dst[c], tmp1, bcrstd, op=OP.mult)
                    else:
                        P.fence(v, "fV", v.tensor_tensor(tmp2, tmp1, bcrstd, op=OP.mult))
                        if gcols is not None:
                            y_ap = tmp1
                            P.fence(v, "fV", v.tensor_scalar(y_ap, tmp2, gcols[:, c:c + 1], bcols[:, c:c + 1],
                                                             op0=OP.mult, op1=OP.add))
                        else:
                            y_ap = tmp2
                        if res is not None:
                            ins = v.tensor_tensor(dst[c], res[c], y_ap, op=OP.add)
                        else:
                            ins = v.tensor_copy(dst[c], y_ap)
                    if not lastc:
                        P.fence(v, "fV", ins)
                P.inc("evV", ins)


def emit_phase1(P, R, C):
    nc = P.nc
    H = P.hbm
    NG = 8
    NQT = 8
    xm = {}

    with P.scope():
        PS = P.psum([128, 8, 512], F32, "ps1")
        K_g = P.sb([128, T], BF16, "p1K")
        Q_g = P.sb([128, T], BF16, "p1Q")
        V_g = P.sb([128, 32, 130], BF16, "p1V")
        Z_g = P.sb([128, T], BF16, "p1Z")
        xbf0 = P.sb([128, NCH, 512], BF16, "p1x0")
        xbf = [xbf0, xbf0]
        wk = P.sb([128, NCH, 128], BF16, "p1wk")
        wq = P.sb([128, NCH, 128], BF16, "p1wq")
        wv = P.sb([128, NCH, 128], BF16, "p1wv")
        wo_g = P.sb([128, 1024], BF16, "p1wo")
        E_sb = P.sb([128, 2, 512], BF16, "p1E")
        trim = P.sb([128, 4, 512], BF16, "p1tri")
        QS_t = P.sb([128, 512], BF16, "p1qs")
        row_a = P.sb([1, 512], F32, "p1ra")
        row_b = P.sb([1, 512], F32, "p1rb")
        srow_a = P.sb([1, 512], F32, "p1sa")
        srow_b = P.sb([1, 512], F32, "p1sb")
        negB = P.sb([1, 2, 512], BF16, "p1nb")
        maxK2 = P.sb([1, 2], F32, "p1mk")
        zc_sb = P.sb([64, 512], F32, "p1zc")
        lntmp1 = P.sb([128, 512], F32, "p1lt1")
        lntmp2 = P.sb([128, 512], F32, "p1lt2")

        with nc.Block("p1prolog") as blk:
            @blk.gpsimd
            def _(g):
                g.memset(V_g[:, :, 64:65], 1.0)
                g.memset(V_g[:, :, 129:130], 1.0)
                P.inc("ldg", g.dma_start(
                    out=trim[:, :, :],
                    in_=H["cc_trimask"].rearrange("a p f -> p a f")), 16)

        for g_ in range(NG):
            _p1_group(P, R, C, H, PS, g_, xm,
                      K_g, Q_g, V_g, Z_g, xbf, wk, wq, wv, wo_g, E_sb, trim,
                      QS_t, row_a, row_b, srow_a, srow_b, negB, maxK2, zc_sb)

        for tt in range(NQT):
            q0 = tt * 512
            rs = [R[c][:, q0:q0 + 512] for c in range(NCH)]
            ln_tile_block(P, PS, 512, rs, rs, xbf0, C, tag="p1ln%d" % tt,
                          tmp1=lntmp1, tmp2=lntmp2)


def _p1_group(P, R, C, H, PS, g_, xm,
              K_g, Q_g, V_g, Z_g, xbf, wk, wq, wv, wo_g, E_sb, trim,
              QS_t, row_a, row_b, srow_a, srow_b, negB, maxK2, zc_sb):
    nc = P.nc
    NQT = 8

    mm0 = P.val("mm")
    mmark = {}
    m = mm0
    for n in range(8):
        m += 1; mmark[("K", n)] = m
        m += 1; mmark[("Q", n)] = m
        for pt in range(4):
            m += 1; mmark[("V", n, pt)] = m
    for sl in range(8):
        m += 1; mmark[("ks", sl)] = m
    s0 = P.val("evS")
    smark = {}
    s = s0
    for n in range(8):
        s += 1; smark[("K", n)] = s
        s += 1; smark[("Q", n)] = s
    for sl in range(8):
        s += 1; smark[("ksq", sl)] = s
    v0 = P.val("evV")
    vmark = {}
    v = v0
    for n in range(8):
        for pt in range(4):
            v += 1; vmark[("V", n, pt)] = v
    for sl in range(8):
        v += 1; vmark[("mx", sl)] = v

    with nc.Block("p1g%dprod" % g_) as blk:
        @blk.gpsimd
        def _(g):
            P.inc("ldg", g.dma_start(
                out=wk[:, :, :],
                in_=H["l0a_WKT"][:, g_ * 128:(g_ + 1) * 128].rearrange("(c p) hd -> p c hd", p=128)), 16)
            P.inc("ldg", g.dma_start(
                out=wq[:, :, :],
                in_=H["l0a_WQT"][:, g_ * 128:(g_ + 1) * 128].rearrange("(c p) hd -> p c hd", p=128)), 16)
            P.inc("ldg", g.dma_start(
                out=wv[:, :, :],
                in_=H["l0a_WVT"][:, g_ * 128:(g_ + 1) * 128].rearrange("(c p) hd -> p c hd", p=128)), 16)
            P.inc("ldg", g.dma_start(
                out=wo_g[:, :], in_=H["l0a_WOT"][g_ * 128:(g_ + 1) * 128, :]), 16)
            for n in range(8):
                if n >= 1:
                    g.wait_ge(P.sem("mm")[0], mmark[("V", n - 1, 3)])
                xm[(g_, n)] = P.inc("xd1", g.dma_start(
                    out=xbf[n % 2][:, :, :],
                    in_=H["stream_in"][:, n * 512:(n + 1) * 512].rearrange("(c p) t -> p c t", p=128)), 16)

        @blk.tensor
        def _(t):
            P.wait(t, "ldg")
            for n in range(8):
                t.wait_ge(P.sem("xd1")[0], xm[(g_, n)])
                if n >= 1:
                    t.wait_ge(P.sem("evS")[0], smark[("Q", n - 1)])
                for k in range(NCH):
                    ins = t.matmul(PS[:, 0, :], wk[:, k, :], xbf[n % 2][:, k, :],
                                   start=(k == 0), stop=(k == NCH - 1))
                P.inc("mm", ins)
                for k in range(NCH):
                    ins = t.matmul(PS[:, 1, :], wq[:, k, :], xbf[n % 2][:, k, :],
                                   start=(k == 0), stop=(k == NCH - 1))
                P.inc("mm", ins)
                for pt in range(4):
                    if (n, pt) != (0, 0) and (n, pt) != (0, 1):
                        prev = (n, pt - 2) if pt >= 2 else (n - 1, pt + 2)
                        t.wait_ge(P.sem("evV")[0], vmark[("V", prev[0], prev[1])])
                    for k in range(NCH):
                        ins = t.matmul(PS[:, 2 + pt % 2, 0:128],
                                       xbf[n % 2][:, k, pt * 128:(pt + 1) * 128],
                                       wv[:, k, :], start=(k == 0), stop=(k == NCH - 1))
                    P.inc("mm", ins)
            for sl in range(8):
                t.wait_ge(P.sem("evS")[0], smark[("ksq", sl)])
                if sl >= 1:
                    t.wait_ge(P.sem("evV")[0], vmark[("mx", sl - 1)])
                t.matmul(PS[0:1, 6, :], C["ones_col_bf"][0:64, 0:1], QS_t[0:64, :],
                         start=True, stop=True)
                ins = t.matmul(PS[0:1, 7, :], C["ones_col_bf"][64:128, 0:1], QS_t[64:128, :],
                               start=True, stop=True)
                P.inc("mm", ins)

        @blk.scalar
        def _(s):
            for n in range(8):
                s.wait_ge(P.sem("mm")[0], mmark[("K", n)])
                ins = s.activation(K_g[:, n * 512:(n + 1) * 512], PS[:, 0, :], AF.Copy,
                                   bias=0.0, scale=1.0)
                P.inc("evS", ins)
                s.wait_ge(P.sem("mm")[0], mmark[("Q", n)])
                ins = s.activation(Q_g[:, n * 512:(n + 1) * 512], PS[:, 1, :], AF.Copy,
                                   bias=0.0, scale=1.0)
                P.inc("evS", ins)
            for sl in range(8):
                if sl == 0:
                    s.wait_ge(P.sem("evS")[0], smark[("Q", 7)])
                else:
                    s.wait_ge(P.sem("mm")[0], mmark[("ks", sl - 1)])
                ins = s.activation(QS_t[:, :], K_g[:, sl * 512:(sl + 1) * 512], AF.Square,
                                   bias=0.0, scale=1.0)
                P.inc("evS", ins)
                assert P.val("evS") == smark[("ksq", sl)]

        @blk.vector
        def _(v):
            for n in range(8):
                for pt in range(4):
                    v.wait_ge(P.sem("mm")[0], mmark[("V", n, pt)])
                    pc = n * 4 + pt
                    src = PS[:, 2 + pt % 2, 0:128].rearrange("p (h dh) -> p h dh", h=2)
                    dst = V_g[:, pc, 0:130].rearrange("p (h dhx) -> p h dhx", dhx=65)[:, :, 0:64]
                    ins = v.tensor_copy(dst, src)
                    P.inc("evV", ins)
                    assert P.val("evV") == vmark[("V", n, pt)]
            for sl in range(8):
                v.wait_ge(P.sem("mm")[0], mmark[("ks", sl)])
                P.fence(v, "fV", v.tensor_reduce(row_a[:, 0:1], PS[0:1, 6, :], axis=AX, op=OP.max))
                P.fence(v, "fV", v.tensor_reduce(row_b[:, 0:1], PS[0:1, 7, :], axis=AX, op=OP.max))
                if sl == 0:
                    P.fence(v, "fV", v.tensor_copy(maxK2[:, 0:1], row_a[:, 0:1]))
                    ins = v.tensor_copy(maxK2[:, 1:2], row_b[:, 0:1])
                else:
                    P.fence(v, "fV", v.tensor_tensor(maxK2[:, 0:1], maxK2[:, 0:1], row_a[:, 0:1], op=OP.max))
                    ins = v.tensor_tensor(maxK2[:, 1:2], maxK2[:, 1:2], row_b[:, 0:1], op=OP.max)
                P.inc("evV", ins)
                assert P.val("evV") == vmark[("mx", sl)]

    for qt in range(NQT):
        _p1_qtile(P, R, C, PS, g_, qt,
                  K_g, Q_g, V_g, Z_g, wo_g, E_sb, trim,
                  QS_t, row_a, row_b, srow_a, srow_b, negB, maxK2, zc_sb)


def _p1_qtile(P, R, C, PS, g_, qt,
              K_g, Q_g, V_g, Z_g, wo_g, E_sb, trim,
              QS_t, row_a, row_b, srow_a, srow_b, negB, maxK2, zc_sb):
    nc = P.nc
    qs = qt * 512
    npc = 4 * (qt + 1)

    mm0 = P.val("mm")
    s0 = P.val("evS")
    v0 = P.val("evV")
    mmark, smark, vmark = {}, {}, {}
    m = mm0
    m += 1; mmark["qrows"] = m
    for h in range(2):
        for j in range(npc):
            m += 1; mmark[("S", h, j)] = m
            m += 1; mmark[("A", h, j)] = m
        m += 1; mmark[("bc", h)] = m
    for dt in range(NCH):
        m += 1; mmark[("wo", dt)] = m
    s = s0
    s += 1; smark["qsq"] = s
    s += 1; smark["sqrt"] = s
    for h in range(2):
        for j in range(npc):
            s += 1; smark[("E", h, j)] = s
    v = v0
    v += 1; vmark["rows"] = v
    v += 1; vmark["negb"] = v
    for h in range(2):
        for j in range(max(0, npc - 4), npc):
            v += 1; vmark[("tm", h, j)] = v
        v += 1; vmark[("rl", h)] = v
        v += 1; vmark[("zn", h)] = v
    for dt in range(NCH):
        v += 1; vmark[("ra", dt)] = v

    with nc.Block("p1g%dq%d" % (g_, qt)) as blk:
        @blk.scalar
        def _(s):
            ins = s.activation(QS_t[:, :], Q_g[:, qs:qs + 512], AF.Square, bias=0.0, scale=1.0)
            P.inc("evS", ins)
            assert P.val("evS") == smark["qsq"]
            s.wait_ge(P.sem("evV")[0], vmark["rows"])
            s.activation(srow_a[:, :], row_a[:, :], AF.Sqrt, bias=0.0, scale=1.0)
            ins = s.activation(srow_b[:, :], row_b[:, :], AF.Sqrt, bias=0.0, scale=1.0)
            P.inc("evS", ins)
            assert P.val("evS") == smark["sqrt"]
            for h in range(2):
                for j in range(npc):
                    s.wait_ge(P.sem("mm")[0], mmark[("S", h, j)])
                    if (h, j) > (0, 1):
                        ph, pj = (h, j - 2) if j >= 2 else (h - 1, npc - 1 if j == 1 else npc - 2)
                        t_ = mmark[("A", ph, pj)]
                        s.wait_ge(P.sem("mm")[0], t_)
                    ins = s.activation(E_sb[:, j % 2, :], PS[:, j % 4, :], AF.Exp,
                                       bias=0.0, scale=0.125)
                    P.inc("evS", ins)
                    assert P.val("evS") == smark[("E", h, j)]

        @blk.tensor
        def _(t):
            t.wait_ge(P.sem("evS")[0], smark["qsq"])
            t.matmul(PS[0:1, 6, :], C["ones_col_bf"][0:64, 0:1], QS_t[0:64, :], start=True, stop=True)
            ins = t.matmul(PS[0:1, 7, :], C["ones_col_bf"][64:128, 0:1], QS_t[64:128, :], start=True, stop=True)
            P.inc("mm", ins)
            assert P.val("mm") == mmark["qrows"]
            t.wait_ge(P.sem("evV")[0], vmark["negb"])
            for h in range(2):
                hh = 64 * h
                for j in range(npc):
                    if j >= 4:
                        t.wait_ge(P.sem("evS")[0], smark[("E", h, j - 4)])
                    elif h == 1 and npc - 4 + j >= 0:
                        t.wait_ge(P.sem("evS")[0], smark[("E", 0, npc - 4 + j)])
                    t.matmul(PS[:, j % 4, :], K_g[hh:hh + 64, j * 128:(j + 1) * 128],
                             Q_g[hh:hh + 64, qs:qs + 512], start=True, stop=False)
                    ins = t.matmul(PS[:, j % 4, :], C["ones1_bf"][:],
                                   negB[:, h, :], start=False, stop=True)
                    P.inc("mm", ins)
                    assert P.val("mm") == mmark[("S", h, j)]
                    t.wait_ge(P.sem("evS")[0], smark[("E", h, j)])
                    if j >= npc - 4:
                        t.wait_ge(P.sem("evV")[0], vmark[("tm", h, j)])
                    ins = t.matmul(PS[0:65, 4, :], V_g[:, j, 65 * h:65 * h + 65],
                                   E_sb[:, j % 2, :], start=(j == 0), stop=(j == npc - 1))
                    P.inc("mm", ins)
                    assert P.val("mm") == mmark[("A", h, j)]
                t.wait_ge(P.sem("evV")[0], vmark[("rl", h)])
                ins = t.matmul(PS[:, 5, :], C["ones1_f"][:], row_a[:, :], start=True, stop=True)
                P.inc("mm", ins)
                assert P.val("mm") == mmark[("bc", h)]
            t.wait_ge(P.sem("evV")[0], vmark[("zn", 1)])
            for dt in range(NCH):
                if dt >= 2:
                    t.wait_ge(P.sem("evV")[0], vmark[("ra", dt - 2)])
                ins = t.matmul(PS[:, 6 + dt % 2, :], wo_g[:, dt * 128:(dt + 1) * 128],
                               Z_g[:, qs:qs + 512], start=True, stop=True)
                P.inc("mm", ins)
                assert P.val("mm") == mmark[("wo", dt)]

        @blk.vector
        def _(v):
            v.wait_ge(P.sem("mm")[0], mmark["qrows"])
            P.fence(v, "fV", v.tensor_scalar(row_a[:, :], PS[0:1, 6, :], maxK2[0:1, 0:1], None, op0=OP.mult))
            ins = v.tensor_scalar(row_b[:, :], PS[0:1, 7, :], maxK2[0:1, 1:2], None, op0=OP.mult)
            P.inc("evV", ins)
            assert P.val("evV") == vmark["rows"]
            v.wait_ge(P.sem("evS")[0], smark["sqrt"])
            P.fence(v, "fV", v.tensor_scalar(negB[:, 0, :], srow_a[:, :], -1.0, None, op0=OP.mult))
            ins = v.tensor_scalar(negB[:, 1, :], srow_b[:, :], -1.0, None, op0=OP.mult)
            P.inc("evV", ins)
            assert P.val("evV") == vmark["negb"]
            for h in range(2):
                for j in range(max(0, npc - 4), npc):
                    a = j - (npc - 4)
                    v.wait_ge(P.sem("evS")[0], smark[("E", h, j)])
                    ins = v.tensor_tensor(E_sb[:, j % 2, :], E_sb[:, j % 2, :], trim[:, a, :], op=OP.mult)
                    P.inc("evV", ins)
                    assert P.val("evV") == vmark[("tm", h, j)]
                v.wait_ge(P.sem("mm")[0], mmark[("A", h, npc - 1)])
                P.fence(v, "fV", v.reciprocal(row_a[:, :], PS[64:65, 4, :]))
                P.inc("evV", v.tensor_copy(zc_sb[:, :], PS[0:64, 4, :]))
                assert P.val("evV") == vmark[("rl", h)]
                v.wait_ge(P.sem("mm")[0], mmark[("bc", h)])
                ins = v.tensor_tensor(Z_g[64 * h:64 * h + 64, qs:qs + 512], zc_sb[:, :],
                                      PS[0:64, 5, :], op=OP.mult)
                P.inc("evV", ins)
                assert P.val("evV") == vmark[("zn", h)]
            for dt in range(NCH):
                v.wait_ge(P.sem("mm")[0], mmark[("wo", dt)])
                ins = v.tensor_tensor(R[dt][:, qs:qs + 512], R[dt][:, qs:qs + 512],
                                      PS[:, 6 + dt % 2, :], op=OP.add)
                P.inc("evV", ins)
                assert P.val("evV") == vmark[("ra", dt)]


def emit_phase3(P, R, C):
    nc = P.nc
    H = P.hbm
    TC = 256
    NCK = T // TC
    wim, wsm, wom = {}, {}, {}

    with P.scope():
        PS = P.psum([128, 8, 512], F32, "ps3")
        xbf = P.sb([128, NCH, TC], BF16, "p3x")
        gate = P.sb([128, NCH, TC], F32, "p3g")
        val_bf = P.sb([128, NCH, TC], BF16, "p3v")
        tanh_sb = P.sb([128, NCH, TC], F32, "p3t")
        a_sb = P.sb([128, NCH, TC], F32, "p3a")
        b_sb = P.sb([128, NCH, TC], F32, "p3b")
        h_sb = P.sb([128, NCH, TC], F32, "p3h")
        hstate = P.sb([128, NCH], F32, "p3hs")
        osb = P.sb([128, 2, TC], F32, "p3o")
        winbuf = P.sb([128, 2, NCH, 128], BF16, "p3wi")
        wsbuf = P.sb([128, 2, NCH, 128], BF16, "p3ws")
        wobuf = P.sb([128, 2, NCH, 128], BF16, "p3wo")
        bin_c = P.sb([128, 16], F32, "p3bin")
        binrow = P.sb([1, D], BF16, "p3binrow")
        bs_c = P.sb([128, NCH], F32, "p3bs")
        bo_c = P.sb([128, NCH], F32, "p3bo")
        borow = P.sb([1, D], BF16, "p3borow")

        with nc.Block("p3prolog") as blk:
            @blk.sync
            def _(s):
                for m in range(16):
                    P.inc("lds", s.dma_start(
                        out=bin_c[:, m:m + 1],
                        in_=H["l1a_b_in"][m * 128:(m + 1) * 128].rearrange("(p one) -> p one", one=1)), 16)
                for m in range(NCH):
                    P.inc("lds", s.dma_start(
                        out=bs_c[:, m:m + 1],
                        in_=H["l1a_b_s"][m * 128:(m + 1) * 128].rearrange("(p one) -> p one", one=1)), 16)
                    P.inc("lds", s.dma_start(
                        out=bo_c[:, m:m + 1],
                        in_=H["l1a_b_o"][m * 128:(m + 1) * 128].rearrange("(p one) -> p one", one=1)), 16)

            @blk.gpsimd
            def _(g):
                P.inc("ldg", g.dma_start(
                    out=binrow[:, :],
                    in_=H["l1a_b_in"][D:2 * D].rearrange("(one e) -> one e", one=1)), 16)
                P.inc("ldg", g.dma_start(
                    out=borow[:, :],
                    in_=H["l1a_b_o"].rearrange("(one e) -> one e", one=1)), 16)
                for m in range(2):
                    wim[(0, m)] = P.inc("w1d3", g.dma_start(
                        out=winbuf[:, m % 2, :, :],
                        in_=H["l1a_WinT"][:, m * 128:(m + 1) * 128].rearrange("(c p) e -> p c e", p=128)), 16)

            @blk.vector
            def _(v):
                P.wait(v, "lds")
                P.wait(v, "ldg")

        def p3_chunk(ct):
            q0 = ct * TC
            last = (ct == NCK - 1)
            mm0 = P.val("mm")
            mmk = lambda g_: mm0 + g_ + 1
            s0 = P.val("evS")
            smk = lambda g_: s0 + g_ + 1
            v0 = P.val("evV")
            vm = {"cast": v0 + 1, "hbf": v0 + 2}
            for m in range(NCH):
                vm[("stt", m)] = v0 + 3 + m

            with nc.Block("p3c%d" % ct) as blk:
                @blk.gpsimd
                def _(g):
                    for m in range(2, 16):
                        g.wait_ge(P.sem("mm")[0], mmk(m - 2))
                        wim[(ct, m)] = P.inc("w1d3", g.dma_start(
                            out=winbuf[:, m % 2, :, :],
                            in_=H["l1a_WinT"][:, m * 128:(m + 1) * 128].rearrange("(c p) e -> p c e", p=128)), 16)
                    for m in range(NCH):
                        if m >= 2:
                            g.wait_ge(P.sem("mm")[0], mmk(16 + m - 2))
                        wsm[(ct, m)] = P.inc("w2d3", g.dma_start(
                            out=wsbuf[:, m % 2, :, :],
                            in_=H["l1a_WsT"][:, m * 128:(m + 1) * 128].rearrange("(c p) e -> p c e", p=128)), 16)
                    for m in range(NCH):
                        if m >= 2:
                            g.wait_ge(P.sem("mm")[0], mmk(24 + m - 2))
                        wom[(ct, m)] = P.inc("w3d3", g.dma_start(
                            out=wobuf[:, m % 2, :, :],
                            in_=H["l1a_WoT"][:, m * 128:(m + 1) * 128].rearrange("(c p) e -> p c e", p=128)), 16)
                    if not last:
                        for m in range(2):
                            g.wait_ge(P.sem("mm")[0], mmk(14 + m))
                            wim[(ct + 1, m)] = P.inc("w1d3", g.dma_start(
                                out=winbuf[:, m % 2, :, :],
                                in_=H["l1a_WinT"][:, m * 128:(m + 1) * 128].rearrange("(c p) e -> p c e", p=128)), 16)

                @blk.vector
                def _(v):
                    for c in range(NCH):
                        ins = v.tensor_copy(xbf[:, c, :], R[c][:, q0:q0 + TC])
                    P.inc("evV", ins)
                    assert P.val("evV") == vm["cast"]
                    v.wait_ge(P.sem("evS")[0], smk(23))
                    for c in range(NCH):
                        v.tensor_scalar(a_sb[:, c, :], gate[:, c, :], -1.0, 1.0, op0=OP.mult, op1=OP.add)
                    for c in range(NCH):
                        ins = v.tensor_tensor(b_sb[:, c, :], gate[:, c, :], tanh_sb[:, c, :], op=OP.mult)
                    P.fence(v, "fV", ins)
                    for c in range(NCH):
                        if ct == 0:
                            ins = v.tensor_tensor_scan(h_sb[:, c, :], a_sb[:, c, :], b_sb[:, c, :],
                                                       0.0, op0=OP.mult, op1=OP.add)
                        else:
                            ins = v.tensor_tensor_scan(h_sb[:, c, :], a_sb[:, c, :], b_sb[:, c, :],
                                                       hstate[:, c:c + 1], op0=OP.mult, op1=OP.add)
                    P.fence(v, "fV", ins)
                    for c in range(NCH):
                        v.tensor_copy(hstate[:, c:c + 1], h_sb[:, c, TC - 1:TC])
                    for c in range(NCH):
                        ins = v.tensor_copy(val_bf[:, c, :], h_sb[:, c, :])
                    P.inc("evV", ins)
                    assert P.val("evV") == vm["hbf"]
                    for m in range(NCH):
                        v.wait_ge(P.sem("evS")[0], smk(24 + m))
                        ins = v.scalar_tensor_tensor(R[m][:, q0:q0 + TC], R[m][:, q0:q0 + TC], 2.0,
                                                     osb[:, m % 2, :], op0=OP.mult, op1=OP.add)
                        P.inc("evV", ins)
                        assert P.val("evV") == vm[("stt", m)]

                @blk.tensor
                def _(t):
                    t.wait_ge(P.sem("evV")[0], vm["cast"])
                    for m in range(16):
                        t.wait_ge(P.sem("w1d3")[0], wim[(ct, m)])
                        if m >= 2:
                            t.wait_ge(P.sem("evS")[0], smk(m - 2))
                        isval = (m >= NCH)
                        for k in range(NCH):
                            ins = t.matmul(PS[:, m % 2, 0:TC], winbuf[:, m % 2, k, :],
                                           xbf[:, k, :], start=(k == 0), stop=(k == NCH - 1 and not isval))
                        if isval:
                            ins = t.matmul(PS[:, m % 2, 0:TC],
                                           binrow[0:1, (m - NCH) * 128:(m - NCH + 1) * 128],
                                           C["ones_row_bf"][0:1, 0:TC], start=False, stop=True)
                        P.inc("mm", ins)
                        assert P.val("mm") == mmk(m)
                    t.wait_ge(P.sem("evS")[0], smk(15))
                    for m in range(NCH):
                        t.wait_ge(P.sem("w2d3")[0], wsm[(ct, m)])
                        t.wait_ge(P.sem("evS")[0], smk(16 + m - 2))
                        for k in range(NCH):
                            ins = t.matmul(PS[:, m % 2, 0:TC], wsbuf[:, m % 2, k, :],
                                           val_bf[:, k, :], start=(k == 0), stop=(k == NCH - 1))
                        P.inc("mm", ins)
                        assert P.val("mm") == mmk(16 + m)
                    t.wait_ge(P.sem("evV")[0], vm["hbf"])
                    for m in range(NCH):
                        t.wait_ge(P.sem("w3d3")[0], wom[(ct, m)])
                        t.wait_ge(P.sem("evS")[0], smk(24 + m - 2))
                        for k in range(NCH):
                            ins = t.matmul(PS[:, m % 2, 0:TC], wobuf[:, m % 2, k, :],
                                           val_bf[:, k, :], start=(k == 0), stop=False)
                        ins = t.matmul(PS[:, m % 2, 0:TC],
                                       borow[0:1, m * 128:(m + 1) * 128],
                                       C["ones_row_bf"][0:1, 0:TC], start=False, stop=True)
                        P.inc("mm", ins)
                        assert P.val("mm") == mmk(24 + m)

                @blk.scalar
                def _(s):
                    for m in range(16):
                        s.wait_ge(P.sem("mm")[0], mmk(m))
                        if m < NCH:
                            ins = s.activation(gate[:, m, :], PS[:, m % 2, 0:TC], AF.Sigmoid,
                                               bias=bin_c[:, m:m + 1], scale=1.0)
                        else:
                            ins = s.activation(val_bf[:, m - NCH, :], PS[:, m % 2, 0:TC], AF.Copy,
                                               bias=0.0, scale=1.0)
                        P.inc("evS", ins)
                        assert P.val("evS") == smk(m)
                    for m in range(NCH):
                        s.wait_ge(P.sem("mm")[0], mmk(16 + m))
                        ins = s.activation(tanh_sb[:, m, :], PS[:, m % 2, 0:TC], AF.Tanh,
                                           bias=bs_c[:, m:m + 1], scale=1.0)
                        P.inc("evS", ins)
                        assert P.val("evS") == smk(16 + m)
                    for m in range(NCH):
                        s.wait_ge(P.sem("mm")[0], mmk(24 + m))
                        if m >= 2:
                            s.wait_ge(P.sem("evV")[0], vm[("stt", m - 2)])
                        ins = s.activation(osb[:, m % 2, :], PS[:, m % 2, 0:TC], AF.Copy,
                                           bias=0.0, scale=1.0)
                        P.inc("evS", ins)
                        assert P.val("evS") == smk(24 + m)

        for ct in range(NCK):
            p3_chunk(ct)
        for tt in range(NCK):
            q0 = tt * TC
            rs = [R[c][:, q0:q0 + TC] for c in range(NCH)]
            ln_tile_block(P, PS, TC, rs, rs, xbf, C, tag="p3ln%d" % tt,
                          tmp1=gate[:, 0, :], tmp2=gate[:, 1, :])


def emit_phase2(P, R, C):
    nc = P.nc
    H = P.hbm
    TTW = 512
    NTT = T // TTW
    NFT = 8
    selrows = P.dram([NEXP, T], F32, "p2selrows")
    cc_in = P.dram([NEXP, 1], F32, "p2ccin")
    cc_out = P.dram([NEXP * NCORES, 1], F32, "p2ccout")
    w1m, w2m, sl8m, slem = {}, {}, {}, {}

    with P.scope():
        PS = P.psum([128, 8, 512], F32, "ps2")
        wr_sb = P.sb([128, NCH, NEXP], F32, "p2wr")
        br_row = P.sb([1, NEXP], F32, "p2br")
        b2_sb = P.sb([NEXP, 1024], BF16, "p2b2")
        b1c = P.sb([128, NEXP, NFT], F32, "p2b1c")
        rmask = P.sb([NEXP, 1], F32, "p2rmask")

        with nc.Block("p2prolog") as blk:
            @blk.sync
            def _(s):
                P.inc("lds", s.dma_start(
                    out=wr_sb[:, :, :],
                    in_=H["l0_Wr"].rearrange("(c p) e -> p c e", p=128)), 16)
                P.inc("lds", s.dma_start(
                    out=br_row[:, :],
                    in_=H["l0_br"].rearrange("(one e) -> one e", one=1)), 16)
                P.inc("lds", s.dma_start(out=rmask[:, :], in_=H["rankmask"][:, :]), 16)
                for e in range(NEXP):
                    for ft in range(NFT):
                        P.inc("lds", s.dma_start(
                            out=b1c[:, e, ft:ft + 1],
                            in_=H["l0_b1"][e, ft * 128:(ft + 1) * 128].rearrange("(p one) -> p one", one=1)), 16)

            @blk.gpsimd
            def _(g):
                P.inc("ldg", g.dma_start(out=b2_sb[:, :], in_=H["l0_b2"][:, :]), 16)

        with P.scope():
            mask_fm = P.sb([NEXP, T], F32, "p2maskfm")
            pos_fm = P.sb([NEXP, T], F32, "p2posfm")
            kept = mask_fm
            sel_fm = pos_fm
            gatec = P.sb([128, 32], F32, "p2gate")
            counts = P.sb([NEXP, 1], F32, "p2counts")
            call_sb = P.sb([NEXP, NEXP], F32, "p2call")
            offs = P.sb([NEXP, 1], F32, "p2offs")
            lsc = {}
            for nm, sh in [("lcopy", [128, NEXP]), ("m1", [128, 1]), ("negm1", [128, 1]),
                           ("mask_tm", [128, NEXP]), ("eexp", [128, NEXP]),
                           ("den", [128, 1]), ("ktm", [128, NEXP]), ("stm", [128, NEXP])]:
                lsc[nm] = P.sb(sh, F32, "p2" + nm)
            lg_ps = PS[:, 0, 0:NEXP]
            tr_ps = PS[0:8, 1, 0:128]
            tb_ps = PS[:, 2, 0:NEXP]

            for t32 in range(32):
                t0 = t32 * 128
                with nc.Block("p2r%d" % t32) as blk:
                    @blk.tensor
                    def _(t):
                        if t32 == 0:
                            P.wait(t, "lds")
                            P.wait(t, "ldg")
                        for c in range(NCH):
                            t.matmul(lg_ps, R[c][:, t0:t0 + 128], wr_sb[:, c, :],
                                     start=(c == 0), stop=False)
                        ins = t.matmul(lg_ps, C["ones1_f"][:], br_row[:, :], start=False, stop=True)
                        P.inc("mm", ins)

                    @blk.vector
                    def _(v):
                        P.wait(v, "mm")
                        P.fence(v, "fV", v.tensor_copy(lsc["lcopy"][:], lg_ps))
                        P.fence(v, "fV", v.tensor_reduce(lsc["m1"][:], lsc["lcopy"][:], axis=AX, op=OP.max))
                        v.tensor_reduce(lsc["negm1"][:], lsc["lcopy"][:], axis=AX, op=OP.max, negate=True)
                        ins = v.tensor_scalar(lsc["mask_tm"][:], lsc["lcopy"][:], lsc["m1"][:, 0:1], None, op0=OP.is_equal)
                        P.inc("evV", ins)

                    @blk.scalar
                    def _(s):
                        P.wait(s, "evV")
                        ins = s.activation(lsc["eexp"][:], lsc["lcopy"][:], AF.Exp,
                                           bias=lsc["negm1"][:, 0:1], scale=1.0)
                        P.inc("evS", ins)

                    @blk.vector
                    def _(v):
                        P.wait(v, "evS")
                        P.fence(v, "fV", v.tensor_reduce(lsc["den"][:], lsc["eexp"][:], axis=AX, op=OP.add))
                        ins = v.reciprocal(gatec[:, t32:t32 + 1], lsc["den"][:])
                        P.inc("evV", ins)

                    @blk.tensor
                    def _(t):
                        P.wait(t, "evV")
                        ins = t.transpose(tr_ps, lsc["mask_tm"][:], C["identf"][:])
                        P.inc("mm", ins)

                    @blk.vector
                    def _(v):
                        P.wait(v, "mm")
                        ins = v.tensor_copy(mask_fm[:, t0:t0 + 128], tr_ps)
                        P.inc("evV", ins)

            with nc.Block("p2cc") as blk:
                @blk.vector
                def _(v):
                    ins = v.tensor_reduce(counts[:, 0:1], mask_fm[:, :], axis=AX, op=OP.add)
                    P.inc("evV", ins)

                @blk.sync
                def _(s):
                    P.wait(s, "evV")
                    P.inc("st", s.dma_start(out=cc_in[:, :], in_=counts[:, :]), 16)

                @blk.gpsimd
                def _(g):
                    P.wait(g, "st")
                    P.inc("cc", g.collective_compute(
                        "AllGather", OP.bypass,
                        replica_groups=[list(range(NCORES))],
                        ins=[cc_in[:, :].opt()],
                        outs=[cc_out[:, :].opt()]), 1)
                    P.wait(g, "cc")
                    P.inc("ldg", g.dma_start(
                        out=call_sb[:, :],
                        in_=cc_out.rearrange("(r e) one -> r (e one)", e=NEXP)), 16)

            with nc.Block("p2off") as blk:
                @blk.tensor
                def _(t):
                    P.wait(t, "ldg")
                    ins = t.matmul(PS[0:8, 3, 0:1], call_sb[:, :], rmask[:, :], start=True, stop=True)
                    P.inc("mm", ins)

                @blk.vector
                def _(v):
                    P.wait(v, "mm")
                    P.fence(v, "fV", v.tensor_copy(offs[:, 0:1], PS[0:8, 3, 0:1]))
                    P.fence(v, "fV", v.tensor_tensor_scan(pos_fm[:, :], mask_fm[:, :], mask_fm[:, :],
                                                          offs[:, 0:1], op0=OP.add, op1=OP.max))
                    P.fence(v, "fV", v.tensor_scalar(pos_fm[:, :], pos_fm[:, :], float(T * NCORES // NEXP), None, op0=OP.is_lt))
                    ins = v.tensor_tensor(kept[:, :], mask_fm[:, :], pos_fm[:, :], op=OP.mult)
                    P.inc("evV", ins)

            for t32 in range(32):
                t0 = t32 * 128
                with nc.Block("p2s%d" % t32) as blk:
                    @blk.tensor
                    def _(t):
                        ins = t.transpose(PS[:, 2, 0:NEXP], kept[:, t0:t0 + 128], C["identf"][0:8, 0:8])
                        P.inc("mm", ins)

                    @blk.vector
                    def _(v):
                        P.wait(v, "mm")
                        P.fence(v, "fV", v.tensor_copy(lsc["ktm"][:], tb_ps))
                        ins = v.tensor_scalar(lsc["stm"][:], lsc["ktm"][:], gatec[:, t32:t32 + 1], None, op0=OP.mult)
                        P.inc("evV", ins)

                    @blk.tensor
                    def _(t):
                        P.wait(t, "evV")
                        ins = t.transpose(tr_ps, lsc["stm"][:], C["identf"][:])
                        P.inc("mm", ins)

                    @blk.vector
                    def _(v):
                        P.wait(v, "mm")
                        ins = v.tensor_copy(sel_fm[:, t0:t0 + 128], tr_ps)
                        P.inc("evV", ins)

            with nc.Block("p2selst") as blk:
                @blk.sync
                def _(s):
                    P.wait(s, "evV")
                    P.inc("st", s.dma_start(out=selrows[:, :], in_=sel_fm[:, :]), 16)

        bfbuf = P.sb([128, NCH, TTW], BF16, "p2bf")
        acc = P.sb([128, NCH, TTW], F32, "p2acc")
        h_bf = P.sb([128, NFT, TTW], BF16, "p2h")
        w1buf = P.sb([128, 2, NCH, 128], BF16, "p2w1")
        w2buf = P.sb([128, 2, 1024], BF16, "p2w2")
        bcsel = P.sb([128, TTW], F32, "p2bcsel")
        otmp = P.sb([128, TTW], F32, "p2otmp")
        sls8 = P.sb([NEXP, TTW], BF16, "p2sls8")
        selrow = P.sb([1, 2, TTW], BF16, "p2selrow")
        with nc.Block("p2pref") as blk:
            @blk.gpsimd
            def _(g):
                for s_ in range(2):
                    w1m[(0, 0, s_)] = P.inc("w1d2", g.dma_start(
                        out=w1buf[:, s_ % 2, :, :],
                        in_=H["l0_W1"][0, :, s_ * 128:(s_ + 1) * 128].rearrange("(c p) f -> p c f", p=128)), 16)
                for fc in range(2):
                    w2m[(0, 0, fc)] = P.inc("w2d2", g.dma_start(
                        out=w2buf[:, fc % 2, :],
                        in_=H["l0_W2"][0, fc * 128:(fc + 1) * 128, :]), 16)
                P.wait(g, "st")
                sl8m[0] = P.inc("sld2", g.dma_start(out=sls8[:, :], in_=selrows[:, 0:TTW]), 16)
                slem[(0, 0)] = P.inc("sld2", g.dma_start(
                    out=selrow[:, 0, :], in_=selrows[0, 0:TTW].rearrange("(one f) -> one f", one=1)), 16)
        def p2_block(tt, e, q0):
            last = (tt == NTT - 1 and e == NEXP - 1)
            ne_tt, ne_e = (tt, e + 1) if e < NEXP - 1 else (tt + 1, 0)
            mm0 = P.val("mm")
            marks = {}
            m = mm0
            if e == 0:
                m += 1
                marks["b2"] = m
            m += 1
            marks["bc"] = m
            for ft in range(NFT):
                m += 1
                marks[("h", ft)] = m
            for fc in range(NFT):
                m += 1
                marks[("o", fc)] = m
            s0 = P.val("evS")
            smarks = {"bcsel": s0 + 1}
            for ft in range(NFT):
                smarks[("h", ft)] = s0 + 2 + ft
            v0 = P.val("evV")
            vmarks = {}
            v = v0
            if e == 0:
                v += 1
                vmarks["b2"] = v
            v += 1
            vmarks["cast"] = v
            v += 1
            vmarks["oev"] = v

            with nc.Block("p2t%de%d" % (tt, e)) as blk:
                @blk.gpsimd
                def _(g):
                    for s_ in range(2, NFT):
                        g.wait_ge(P.sem("mm")[0], marks[("h", s_ - 2)])
                        w1m[(tt, e, s_)] = P.inc("w1d2", g.dma_start(
                            out=w1buf[:, s_ % 2, :, :],
                            in_=H["l0_W1"][e, :, s_ * 128:(s_ + 1) * 128].rearrange("(c p) f -> p c f", p=128)), 16)
                    if not last:
                        for s_ in range(2):
                            g.wait_ge(P.sem("mm")[0], marks[("h", NFT - 2 + s_)])
                            w1m[(ne_tt, ne_e, s_)] = P.inc("w1d2", g.dma_start(
                                out=w1buf[:, s_ % 2, :, :],
                                in_=H["l0_W1"][ne_e, :, s_ * 128:(s_ + 1) * 128].rearrange("(c p) f -> p c f", p=128)), 16)
                    for fc in range(2, NFT):
                        g.wait_ge(P.sem("mm")[0], marks[("o", fc - 2)])
                        w2m[(tt, e, fc)] = P.inc("w2d2", g.dma_start(
                            out=w2buf[:, fc % 2, :],
                            in_=H["l0_W2"][e, fc * 128:(fc + 1) * 128, :]), 16)
                    if not last:
                        for fc in range(2):
                            g.wait_ge(P.sem("mm")[0], marks[("o", NFT - 2 + fc)])
                            w2m[(ne_tt, ne_e, fc)] = P.inc("w2d2", g.dma_start(
                                out=w2buf[:, fc % 2, :],
                                in_=H["l0_W2"][ne_e, fc * 128:(fc + 1) * 128, :]), 16)
                        nq0 = ne_tt * TTW
                        if ne_e == 0:
                            sl8m[ne_tt] = P.inc("sld2", g.dma_start(
                                out=sls8[:, :], in_=selrows[:, nq0:nq0 + TTW]), 16)
                        slem[(ne_tt, ne_e)] = P.inc("sld2", g.dma_start(
                            out=selrow[:, (tt * NEXP + e + 1) % 2, :],
                            in_=selrows[ne_e, nq0:nq0 + TTW].rearrange("(one f) -> one f", one=1)), 16)

                @blk.vector
                def _(v):
                    if e == 0:
                        v.wait_ge(P.sem("mm")[0], marks["b2"])
                        for dt in range(NCH):
                            ins = v.tensor_copy(acc[:, dt, :], PS[:, dt, 0:TTW])
                        P.inc("evV", ins)
                        assert P.val("evV") == vmarks["b2"]
                    v.wait_ge(P.sem("evS")[0], smarks["bcsel"])
                    for c in range(NCH):
                        ins = v.tensor_tensor(bfbuf[:, c, :], R[c][:, q0:q0 + TTW], bcsel[:, :], op=OP.mult)
                    P.inc("evV", ins)
                    assert P.val("evV") == vmarks["cast"]
                    v.wait_ge(P.sem("mm")[0], marks[("o", NFT - 1)])
                    for dt in range(NCH):
                        P.fence(v, "fV", v.tensor_tensor(otmp[:], PS[:, dt, 0:TTW], bcsel[:, :], op=OP.mult))
                        ins = v.tensor_tensor(acc[:, dt, :], acc[:, dt, :], otmp[:], op=OP.add)
                        if dt < NCH - 1:
                            P.fence(v, "fV", ins)
                    P.inc("evV", ins)
                    assert P.val("evV") == vmarks["oev"]

                @blk.tensor
                def _(t):
                    if e == 0:
                        t.wait_ge(P.sem("sld2")[0], sl8m[tt])
                        for dt in range(NCH):
                            ins = t.matmul(PS[:, dt, 0:TTW], b2_sb[:, dt * 128:(dt + 1) * 128],
                                           sls8[:, :], start=True, stop=True)
                        P.inc("mm", ins)
                        assert P.val("mm") == marks["b2"]
                        t.wait_ge(P.sem("evV")[0], vmarks["b2"])
                    t.wait_ge(P.sem("sld2")[0], slem[(tt, e)])
                    ins = t.matmul(PS[:, 2, 0:TTW], C["ones1_bf"][:], selrow[:, (tt * NEXP + e) % 2, :],
                                   start=True, stop=True)
                    P.inc("mm", ins)
                    assert P.val("mm") == marks["bc"]
                    t.wait_ge(P.sem("evV")[0], vmarks["cast"])
                    for ft in range(NFT):
                        t.wait_ge(P.sem("w1d2")[0], w1m[(tt, e, ft)])
                        if ft >= 2:
                            t.wait_ge(P.sem("evS")[0], smarks[("h", ft - 2)])
                        for k in range(NCH):
                            ins = t.matmul(PS[:, ft % 2, 0:TTW],
                                           w1buf[:, ft % 2, k, :],
                                           bfbuf[:, k, :], start=(k == 0), stop=(k == NCH - 1))
                        P.inc("mm", ins)
                        assert P.val("mm") == marks[("h", ft)]
                    t.wait_ge(P.sem("evS")[0], smarks[("h", NFT - 1)])
                    for fc in range(NFT):
                        t.wait_ge(P.sem("w2d2")[0], w2m[(tt, e, fc)])
                        for dt in range(NCH):
                            ins = t.matmul(PS[:, dt, 0:TTW], w2buf[:, fc % 2, dt * 128:(dt + 1) * 128],
                                           h_bf[:, fc, :], start=(fc == 0), stop=(fc == NFT - 1))
                        P.inc("mm", ins)
                        assert P.val("mm") == marks[("o", fc)]

                @blk.scalar
                def _(s):
                    s.wait_ge(P.sem("mm")[0], marks["bc"])
                    ins = s.activation(bcsel[:, :], PS[:, 2, 0:TTW], AF.Copy, bias=0.0, scale=1.0)
                    P.inc("evS", ins)
                    assert P.val("evS") == smarks["bcsel"]
                    for ft in range(NFT):
                        s.wait_ge(P.sem("mm")[0], marks[("h", ft)])
                        ins = s.activation(h_bf[:, ft, :], PS[:, ft % 2, 0:TTW], AF.Relu,
                                           bias=b1c[:, e, ft:ft + 1], scale=1.0)
                        P.inc("evS", ins)
                        assert P.val("evS") == smarks[("h", ft)]

        for tt in range(NTT):
            q0 = tt * TTW
            for e in range(NEXP):
                p2_block(tt, e, q0)
            srcs = [acc[:, c, :] for c in range(NCH)]
            rs = [R[c][:, q0:q0 + TTW] for c in range(NCH)]
            with nc.Block("p2res%d" % tt) as blk:
                @blk.vector
                def _(v):
                    for c in range(NCH):
                        ins = v.tensor_tensor(acc[:, c, :], acc[:, c, :], rs[c], op=OP.add)
                    P.inc("evV", ins)
            ln_tile_block(P, PS, TTW, srcs, rs, bfbuf, C, tag="p2ln%d" % tt,
                          tmp1=otmp, tmp2=bcsel)


def emit_phase4(P, R, C):
    nc = P.nc
    H = P.hbm
    TTW = 512
    NTT = T // TTW
    NFT = 32
    rwrows = P.dram([NEXP, T], F32, "p4rwrows")

    with P.scope():
        PS = P.psum([128, 8, 512], F32, "ps4")
        wr_sb = P.sb([128, NCH, NEXP], F32, "p4wr")
        br_row = P.sb([1, NEXP], F32, "p4br")
        b2_sb = P.sb([NEXP, 1024], BF16, "p4b2")
        b1c = P.sb([128, NEXP, NFT], F32, "p4b1c")
        gcols = P.sb([128, NCH], F32, "p4g")
        bcols = P.sb([128, NCH], F32, "p4b")

        with nc.Block("p4prolog") as blk:
            @blk.sync
            def _(s):
                P.inc("lds", s.dma_start(
                    out=wr_sb[:, :, :],
                    in_=H["l1f_WrT"].rearrange("(c p) e -> p c e", p=128)), 16)
                P.inc("lds", s.dma_start(
                    out=br_row[:, :],
                    in_=H["l1f_br"].rearrange("(one e) -> one e", one=1)), 16)
                for c in range(NCH):
                    P.inc("lds", s.dma_start(
                        out=gcols[:, c:c + 1],
                        in_=H["l1f_gamma"][c * 128:(c + 1) * 128].rearrange("(p one) -> p one", one=1)), 16)
                    P.inc("lds", s.dma_start(
                        out=bcols[:, c:c + 1],
                        in_=H["l1f_beta"][c * 128:(c + 1) * 128].rearrange("(p one) -> p one", one=1)), 16)
                for e in range(NEXP):
                    for ft in range(NFT):
                        P.inc("lds", s.dma_start(
                            out=b1c[:, e, ft:ft + 1],
                            in_=H["l1f_b1"][e, ft * 128:(ft + 1) * 128].rearrange("(p one) -> p one", one=1)), 16)

            @blk.gpsimd
            def _(g):
                P.inc("ldg", g.dma_start(out=b2_sb[:, :], in_=H["l1f_b2"][:, :]), 16)

        with P.scope():
            rw_fm = P.sb([NEXP, T], F32, "p4rwfm")
            lsc = {}
            for nm, sh in [("lcopy", [128, NEXP]), ("m1", [128, 1]), ("negm1", [128, 1]),
                           ("is1", [128, NEXP]), ("tmp8", [128, NEXP]), ("m2", [128, 1]),
                           ("keep", [128, NEXP]), ("eexp", [128, NEXP]), ("ek", [128, NEXP]),
                           ("den", [128, 1]), ("rden", [128, 1]), ("rw_tm", [128, NEXP])]:
                lsc[nm] = P.sb(sh, F32, "p4" + nm)
            _p4_router(P, R, C, H, PS, wr_sb, br_row, rw_fm, lsc, rwrows)
            if P4_DEBUG == 1:
                P.skip_store = True
                with nc.Block("dbg1") as blk:
                    @blk.sync
                    def _(s):
                        P.inc("st", s.dma_start(out=H["stream_out"][0:8, :], in_=rw_fm[:, :]), 16)
                        P.wait(s, "st")
                return

        bfbuf = P.sb([128, NCH, TTW], BF16, "p4bf")
        acc = P.sb([128, NCH, TTW], F32, "p4acc")
        h_bf = P.sb([128, NFT, TTW], BF16, "p4h")
        w1buf = P.sb([128, 2, NCH, 128], BF16, "p4w1")
        w2buf = P.sb([128, 2, 1024], BF16, "p4w2")
        bcrw = P.sb([128, TTW], F32, "p4bcrw")
        otmp = P.sb([128, TTW], F32, "p4otmp")
        rws8 = P.sb([NEXP, TTW], BF16, "p4rws8")
        rwrow = P.sb([1, 2, TTW], BF16, "p4rwrow")
        with nc.Block("p4pref") as blk:
            @blk.gpsimd
            def _(g):
                for s_ in range(2):
                    P.w1_marks[(0, 0, s_)] = P.inc("w1d0", g.dma_start(
                        out=w1buf[:, s_ % 2, :, :],
                        in_=H["l1f_W1T"][0, :, s_ * 128:(s_ + 1) * 128].rearrange("(c p) f -> p c f", p=128)), 16)
                for fc in range(2):
                    P.w2_marks[(0, 0, fc)] = P.inc("w2d0", g.dma_start(
                        out=w2buf[:, fc % 2, :],
                        in_=H["l1f_W2T"][0, fc * 128:(fc + 1) * 128, :]), 16)
                P.wait(g, "st")
                P.rw8_marks[0] = P.inc("rwd", g.dma_start(out=rws8[:, :], in_=rwrows[:, 0:TTW]), 16)
                P.rwe_marks[(0, 0)] = P.inc("rwd", g.dma_start(
                    out=rwrow[:, 0, :], in_=rwrows[0, 0:TTW].rearrange("(one f) -> one f", one=1)), 16)
        for tt in range(1 if P4_DEBUG in (2, 3) else (P4_DEBUG_NT if P4_DEBUG == 4 else NTT)):
            q0 = tt * TTW
            for e in range(NEXP if P4_DEBUG != 3 else P4_DEBUG_NE):
                _p4_block(P, R, C, H, PS, tt, e, q0, TTW,
                          bfbuf, acc, h_bf, w1buf, w2buf, bcrw, otmp, rws8, rwrow, b1c, b2_sb, rwrows)
            if P4_DEBUG == 3:
                P.skip_store = True
                with nc.Block("dbg3") as blk:
                    @blk.gpsimd
                    def _(g):
                        for ft in range(16):
                            P.inc("st", g.dma_start(out=H["stream_out"][0:128, ft * TTW:(ft + 1) * TTW], in_=h_bf[:, ft, :]), 16)
                        for ft in range(16, NFT):
                            P.inc("st", g.dma_start(out=H["stream_out"][384:512, (ft - 16) * TTW:(ft - 15) * TTW], in_=h_bf[:, ft, :]), 16)
                        P.inc("st", g.dma_start(out=H["stream_out"][128:256, 0:TTW], in_=bcrw[:, :]), 16)
                        for c in range(NCH):
                            P.inc("st", g.dma_start(out=H["stream_out"][256:384, c * TTW:(c + 1) * TTW], in_=acc[:, c, :]), 16)
                        P.wait(g, "st")
                return
            srcs = [acc[:, c, :] for c in range(NCH)]
            rs = [R[c][:, q0:q0 + TTW] for c in range(NCH)]
            with nc.Block("p4res%d" % tt) as blk:
                @blk.vector
                def _(v):
                    for c in range(NCH):
                        ins = v.tensor_tensor(acc[:, c, :], acc[:, c, :], rs[c], op=OP.add)
                    P.inc("evV", ins)
            if P4_DEBUG == 2:
                P.skip_store = True
                with nc.Block("dbg2") as blk:
                    @blk.sync
                    def _(s):
                        for c in range(NCH):
                            P.inc("st", s.dma_start(out=H["stream_out"][c * 128:(c + 1) * 128, 0:TTW], in_=acc[:, c, :]), 16)
                        P.wait(s, "st")
                return
            ln_tile_block(P, PS, TTW, srcs, rs, bfbuf, C,
                          gcols=gcols, bcols=bcols, res=rs, tag="p4ln%d" % tt,
                          tmp1=otmp, tmp2=bcrw)
        if P4_DEBUG == 4:
            return
        for tt in range(NTT):
            q0 = tt * TTW
            rs = [R[c][:, q0:q0 + TTW] for c in range(NCH)]
            ln_tile_block(P, PS, TTW, rs, rs, bfbuf, C, tag="p4lno%d" % tt,
                          tmp1=otmp, tmp2=bcrw)


def _p4_router(P, R, C, H, PS, wr_sb, br_row, rw_fm, lsc, rwrows):
    nc = P.nc
    NEXP_ = NEXP
    lg_ps = PS[:, 0, 0:NEXP]
    tr_ps = PS[0:8, 1, 0:128]
    if True:
        for t32 in range(32):
            t0 = t32 * 128
            with nc.Block("p4r%d" % t32) as blk:
                @blk.tensor
                def _(t):
                    if t32 == 0:
                        P.wait(t, "lds")
                        P.wait(t, "ldg")
                    for c in range(NCH):
                        t.matmul(lg_ps, R[c][:, t0:t0 + 128], wr_sb[:, c, :],
                                 start=(c == 0), stop=False)
                    ins = t.matmul(lg_ps, C["ones1_f"][:], br_row[:, :], start=False, stop=True)
                    P.inc("mm", ins)

                @blk.vector
                def _(v):
                    P.wait(v, "mm")
                    P.fence(v, "fV", v.tensor_copy(lsc["lcopy"][:], lg_ps))
                    P.fence(v, "fV", v.tensor_reduce(lsc["m1"][:], lsc["lcopy"][:], axis=AX, op=OP.max))
                    v.tensor_reduce(lsc["negm1"][:], lsc["lcopy"][:], axis=AX, op=OP.max, negate=True)
                    P.fence(v, "fV", v.tensor_scalar(lsc["is1"][:], lsc["lcopy"][:], lsc["m1"][:, 0:1], None, op0=OP.is_equal))
                    P.fence(v, "fV", v.scalar_tensor_tensor(lsc["tmp8"][:], lsc["is1"][:], -1e30, lsc["lcopy"][:],
                                                            op0=OP.mult, op1=OP.add))
                    P.fence(v, "fV", v.tensor_reduce(lsc["m2"][:], lsc["tmp8"][:], axis=AX, op=OP.max))
                    ins = v.tensor_scalar(lsc["keep"][:], lsc["lcopy"][:], lsc["m2"][:, 0:1], None, op0=OP.is_ge)
                    P.inc("evV", ins)

                @blk.scalar
                def _(s):
                    P.wait(s, "evV")
                    ins = s.activation(lsc["eexp"][:], lsc["lcopy"][:], AF.Exp,
                                       bias=lsc["negm1"][:, 0:1], scale=1.0)
                    P.inc("evS", ins)

                @blk.vector
                def _(v):
                    P.wait(v, "evS")
                    P.fence(v, "fV", v.tensor_tensor(lsc["ek"][:], lsc["eexp"][:], lsc["keep"][:], op=OP.mult))
                    P.fence(v, "fV", v.tensor_reduce(lsc["den"][:], lsc["ek"][:], axis=AX, op=OP.add))
                    P.fence(v, "fV", v.reciprocal(lsc["rden"][:], lsc["den"][:]))
                    ins = v.tensor_scalar(lsc["rw_tm"][:], lsc["ek"][:], lsc["rden"][:, 0:1], None, op0=OP.mult)
                    P.inc("evV", ins)

                @blk.tensor
                def _(t):
                    P.wait(t, "evV")
                    ins = t.transpose(tr_ps, lsc["rw_tm"][:], C["identf"][:])
                    P.inc("mm", ins)

                @blk.vector
                def _(v):
                    P.wait(v, "mm")
                    ins = v.tensor_copy(rw_fm[:, t0:t0 + 128], tr_ps)
                    P.inc("evV", ins)

        with nc.Block("p4rw") as blk:
            @blk.sync
            def _(s):
                P.wait(s, "evV")
                P.inc("st", s.dma_start(out=rwrows[:, :], in_=rw_fm[:, :]), 16)


def _p4_block(P, R, C, H, PS, tt, e, q0, TTW, bfbuf, acc, h_bf, w1buf, w2buf, bcrw, otmp,
              rws8, rwrow, b1c, b2_sb, rwrows):
    nc = P.nc
    NFT = 32
    NTT = T // TTW
    w1s = "w1d%d" % (tt % 2)
    w2s = "w2d%d" % (tt % 2)
    last = (tt == NTT - 1 and e == NEXP - 1)
    ne_tt, ne_e = (tt, e + 1) if e < NEXP - 1 else (tt + 1, 0)

    mm0 = P.val("mm")
    marks = {}
    m = mm0
    if e == 0:
        m += 1
        marks["b2"] = m
    m += 1
    marks["bc"] = m
    for ft in range(NFT):
        m += 1
        marks[("h", ft)] = m
    for fc in range(NFT):
        m += 1
        marks[("o", fc)] = m
    mm_end = m
    s0 = P.val("evS")
    smarks = {"bcrw": s0 + 1}
    for ft in range(NFT):
        smarks[("h", ft)] = s0 + 2 + ft
    s_end = s0 + 2 + NFT - 1
    v0 = P.val("evV")
    vmarks = {}
    v = v0
    if e == 0:
        v += 1
        vmarks["cast"] = v
        v += 1
        vmarks["b2"] = v
    v += 1
    vmarks["oev"] = v
    v_end = v

    with nc.Block("p4t%de%d" % (tt, e)) as blk:
        @blk.gpsimd
        def _(g):
            for s_ in range(2, NFT):
                g.wait_ge(P.sem("mm")[0], marks[("h", s_ - 2)])
                P.w1_marks[(tt, e, s_)] = P.inc(w1s, g.dma_start(
                    out=w1buf[:, s_ % 2, :, :],
                    in_=H["l1f_W1T"][e, :, s_ * 128:(s_ + 1) * 128].rearrange("(c p) f -> p c f", p=128)), 16)
            if not last:
                for s_ in range(2):
                    g.wait_ge(P.sem("mm")[0], marks[("h", NFT - 2 + s_)])
                    P.w1_marks[(ne_tt, ne_e, s_)] = P.inc("w1d%d" % (ne_tt % 2), g.dma_start(
                        out=w1buf[:, s_ % 2, :, :],
                        in_=H["l1f_W1T"][ne_e, :, s_ * 128:(s_ + 1) * 128].rearrange("(c p) f -> p c f", p=128)), 16)
            for fc in range(2, NFT):
                g.wait_ge(P.sem("mm")[0], marks[("o", fc - 2)])
                P.w2_marks[(tt, e, fc)] = P.inc(w2s, g.dma_start(
                    out=w2buf[:, fc % 2, :],
                    in_=H["l1f_W2T"][e, fc * 128:(fc + 1) * 128, :]), 16)
            if not last:
                for fc in range(2):
                    g.wait_ge(P.sem("mm")[0], marks[("o", NFT - 2 + fc)])
                    P.w2_marks[(ne_tt, ne_e, fc)] = P.inc("w2d%d" % (ne_tt % 2), g.dma_start(
                        out=w2buf[:, fc % 2, :],
                        in_=H["l1f_W2T"][ne_e, fc * 128:(fc + 1) * 128, :]), 16)
                nq0 = ne_tt * TTW
                if ne_e == 0:
                    P.rw8_marks[ne_tt] = P.inc("rwd", g.dma_start(
                        out=rws8[:, :], in_=rwrows[:, nq0:nq0 + TTW]), 16)
                P.rwe_marks[(ne_tt, ne_e)] = P.inc("rwd", g.dma_start(
                    out=rwrow[:, (tt * NEXP + e + 1) % 2, :],
                    in_=rwrows[ne_e, nq0:nq0 + TTW].rearrange("(one f) -> one f", one=1)), 16)

        @blk.vector
        def _(v):
            if e == 0:
                for c in range(NCH):
                    ins = v.tensor_copy(bfbuf[:, c, :], R[c][:, q0:q0 + TTW])
                P.inc("evV", ins)
                assert P.val("evV") == vmarks["cast"]
                v.wait_ge(P.sem("mm")[0], marks["b2"])
                for dt in range(NCH):
                    ins = v.tensor_copy(acc[:, dt, :], PS[:, dt, 0:TTW])
                P.inc("evV", ins)
                assert P.val("evV") == vmarks["b2"]
            v.wait_ge(P.sem("mm")[0], marks[("o", NFT - 1)])
            v.wait_ge(P.sem("evS")[0], smarks["bcrw"])
            for dt in range(NCH):
                P.fence(v, "fV", v.tensor_tensor(otmp[:], PS[:, dt, 0:TTW], bcrw[:], op=OP.mult))
                ins = v.tensor_tensor(acc[:, dt, :], acc[:, dt, :], otmp[:], op=OP.add)
                if dt < NCH - 1:
                    P.fence(v, "fV", ins)
            P.inc("evV", ins)
            assert P.val("evV") == vmarks["oev"]

        @blk.tensor
        def _(t):
            if e == 0:
                t.wait_ge(P.sem("rwd")[0], P.rw8_marks[tt])
                for dt in range(NCH):
                    ins = t.matmul(PS[:, dt, 0:TTW], b2_sb[:, dt * 128:(dt + 1) * 128],
                                   rws8[:, :], start=True, stop=True)
                P.inc("mm", ins)
                assert P.val("mm") == marks["b2"]
                t.wait_ge(P.sem("evV")[0], vmarks["b2"])
            t.wait_ge(P.sem("rwd")[0], P.rwe_marks[(tt, e)])
            ins = t.matmul(PS[:, 2, 0:TTW], C["ones1_bf"][:], rwrow[:, (tt * NEXP + e) % 2, :],
                           start=True, stop=True)
            P.inc("mm", ins)
            assert P.val("mm") == marks["bc"]
            for ft in range(NFT):
                t.wait_ge(P.sem(w1s)[0], P.w1_marks[(tt, e, ft)])
                if ft >= 2:
                    t.wait_ge(P.sem("evS")[0], smarks[("h", ft - 2)])
                for k in range(NCH):
                    ins = t.matmul(PS[:, ft % 2, 0:TTW],
                                   w1buf[:, ft % 2, k, :],
                                   bfbuf[:, k, :], start=(k == 0), stop=(k == NCH - 1))
                P.inc("mm", ins)
                assert P.val("mm") == marks[("h", ft)]
            t.wait_ge(P.sem("evS")[0], smarks[("h", NFT - 1)])
            for fc in range(NFT):
                t.wait_ge(P.sem(w2s)[0], P.w2_marks[(tt, e, fc)])
                for dt in range(NCH):
                    ins = t.matmul(PS[:, dt, 0:TTW], w2buf[:, fc % 2, dt * 128:(dt + 1) * 128],
                                   h_bf[:, fc, :], start=(fc == 0), stop=(fc == NFT - 1))
                P.inc("mm", ins)
                assert P.val("mm") == marks[("o", fc)]

        @blk.scalar
        def _(s):
            s.wait_ge(P.sem("mm")[0], marks["bc"])
            ins = s.activation(bcrw[:], PS[:, 2, 0:TTW], AF.Copy, bias=0.0, scale=1.0)
            P.inc("evS", ins)
            assert P.val("evS") == smarks["bcrw"]
            for ft in range(NFT):
                s.wait_ge(P.sem("mm")[0], marks[("h", ft)])
                ins = s.activation(h_bf[:, ft, :], PS[:, ft % 2, 0:TTW], AF.Relu,
                                   bias=b1c[:, e, ft:ft + 1], scale=1.0)
                P.inc("evS", ins)
                assert P.val("evS") == smarks[("h", ft)]

    assert P.val("mm") == mm_end
    assert P.val("evS") == s_end
    assert P.val("evV") == v_end


def build_program(phases, fused_in_hbm=True):
    P = Prog()
    nc = P.nc
    H = {}
    P.hbm = H
    P.w1_marks = {}
    P.w2_marks = {}
    P.rw8_marks = {}
    P.rwe_marks = {}

    def din(nm, shape, dt=F32):
        H[nm] = nc.dram_tensor(nm, shape, dt, kind="ExternalInput").ap()

    din("cc_ident", [128, 128])
    din("stream_in", [D, T])
    if 2 in phases:
        din("rankmask", [NEXP, 1])
        din("l0_Wr", [D, NEXP])
        din("l0_br", [NEXP])
        din("l0_W1", [NEXP, D, D])
        din("l0_W2", [NEXP, D, D])
        din("l0_b1", [NEXP, D])
        din("l0_b2", [NEXP, D])
    H["stream_out"] = nc.dram_tensor("stream_out", [D, T], F32, kind="ExternalOutput").ap()
    if 1 in phases:
        din("l0a_WKT", [D, D])
        din("l0a_WQT", [D, D])
        din("l0a_WVT", [D, D])
        din("l0a_WOT", [D, D])
        din("cc_trimask", [4, 128, 512])
    if 3 in phases:
        din("l1a_WinT", [D, 2 * D])
        din("l1a_WsT", [D, D])
        din("l1a_WoT", [D, D])
        din("l1a_b_in", [2 * D])
        din("l1a_b_s", [D])
        din("l1a_b_o", [D])
    if 4 in phases:
        din("l1f_WrT", [D, NEXP])
        din("l1f_br", [NEXP])
        din("l1f_W1T", [NEXP, D, 4096])
        din("l1f_W2T", [NEXP, 4096, D])
        din("l1f_b1", [NEXP, 4096])
        din("l1f_b2", [NEXP, D])
        din("l1f_gamma", [D])
        din("l1f_beta", [D])

    R = [P.sb([128, T], F32, "R%d" % c) for c in range(NCH)]
    C = make_consts(P)

    with nc.Block("ldR") as blk:
        @blk.sync
        def _(s):
            for c in range(NCH):
                P.inc("lds", s.dma_start(out=R[c][:], in_=H["stream_in"][c * 128:(c + 1) * 128, :]), 16)

        @blk.vector
        def _(v):
            P.wait(v, "lds")

    for ph in phases:
        if ph == 1:
            emit_phase1(P, R, C)
        elif ph == 2:
            emit_phase2(P, R, C)
        elif ph == 3:
            emit_phase3(P, R, C)
        elif ph == 4:
            emit_phase4(P, R, C)
        else:
            raise NotImplementedError("phase %d" % ph)

    if getattr(P, "skip_store", False):
        P.top.close()
        return P
    with nc.Block("stR") as blk:
        @blk.sync
        def _(s):
            for c in range(NCH):
                P.inc("st", s.dma_start(out=H["stream_out"][c * 128:(c + 1) * 128, :], in_=R[c][:]), 16)
            P.wait(s, "st")

    P.top.close()
    return P


def prep_weights(ins):
    w = {}
    w["cc_ident"] = np.eye(128, dtype=np.float32)
    w["l0a_WKT"] = np.ascontiguousarray(ins["L0_attn_W_K"].reshape(D, D).T)
    w["l0a_WQT"] = np.ascontiguousarray(ins["L0_attn_W_Q"].reshape(D, D).T)
    w["l0a_WVT"] = np.ascontiguousarray(ins["L0_attn_W_V"].reshape(D, D).T)
    w["l0a_WOT"] = np.ascontiguousarray(ins["L0_attn_W_O"].T)
    tri = np.zeros((4, 128, 512), dtype=np.float32)
    for a in range(4):
        for pr in range(128):
            q0 = pr + 128 * a
            if q0 < 512:
                tri[a, pr, q0:] = 1.0
    w["cc_trimask"] = tri
    w["l0_Wr"] = np.ascontiguousarray(ins["L0_ffn_Wr"])
    w["l0_br"] = np.ascontiguousarray(ins["L0_ffn_br"])
    w["l0_W1"] = np.ascontiguousarray(ins["L0_ffn_W1"])
    w["l0_W2"] = np.ascontiguousarray(ins["L0_ffn_W2"])
    w["l0_b1"] = np.ascontiguousarray(ins["L0_ffn_b1"])
    w["l0_b2"] = np.ascontiguousarray(ins["L0_ffn_b2"])
    w["l1a_WinT"] = np.ascontiguousarray(ins["L1_attn_W_in"].T)
    w["l1a_WsT"] = np.ascontiguousarray(ins["L1_attn_W_s"].T)
    w["l1a_WoT"] = np.ascontiguousarray(ins["L1_attn_W_o"].T)
    w["l1a_b_in"] = np.ascontiguousarray(ins["L1_attn_b_in"])
    w["l1a_b_s"] = np.ascontiguousarray(ins["L1_attn_b_s"])
    w["l1a_b_o"] = np.ascontiguousarray(ins["L1_attn_b_o"])
    w["l1f_WrT"] = np.ascontiguousarray(ins["L1_ffn_Wr"].T)
    w["l1f_br"] = np.ascontiguousarray(ins["L1_ffn_br"])
    w["l1f_W1T"] = np.ascontiguousarray(ins["L1_ffn_W1"].transpose(0, 2, 1))
    w["l1f_W2T"] = np.ascontiguousarray(ins["L1_ffn_W2"].transpose(0, 2, 1))
    w["l1f_b1"] = np.ascontiguousarray(ins["L1_ffn_b1"])
    w["l1f_b2"] = np.ascontiguousarray(ins["L1_ffn_b2"])
    w["l1f_gamma"] = np.ascontiguousarray(ins["L1_ffn_gamma"])
    w["l1f_beta"] = np.ascontiguousarray(ins["L1_ffn_beta"])
    return w


def run_phases(phases, stream_in_cores, weights):
    P = build_program(phases)
    in_maps = []
    for cid in range(NCORES):
        m = {"stream_in": stream_in_cores[cid]}
        if "rankmask" in P.hbm:
            m["rankmask"] = (np.arange(NEXP) < cid).astype(np.float32).reshape(NEXP, 1)
        for nm, arr in weights.items():
            if nm in P.hbm and nm != "stream_out":
                m[nm] = arr
        in_maps.append(m)
    res = run_bass_kernel_spmd(P.nc, in_maps, core_ids=list(range(NCORES)))
    return [res.results[c]["stream_out"] for c in range(NCORES)]


PHASE_GROUPS = [(1, 2, 3, 4)]


def kernel(**inputs):
    x = inputs["L0_attn_x"]
    w = prep_weights(inputs)
    streams = [np.ascontiguousarray(x[b].T).astype(np.float32) for b in range(NCORES)]
    for group in PHASE_GROUPS:
        streams = run_phases(group, streams, w)
    y = np.stack([o.T for o in streams], axis=0)
    return np.ascontiguousarray(y.astype(np.float32))


# version 04
# speedup vs baseline: 1.6530x; 1.1187x over previous
import numpy as np
from contextlib import ExitStack, contextmanager

import concourse.bass as bass
import concourse.mybir as mybir
from concourse.bass_utils import run_bass_kernel_spmd
from concourse.alu_op_type import AluOpType as OP

F32 = mybir.dt.float32
BF16 = mybir.dt.bfloat16
AF = mybir.ActivationFunctionType
AX = mybir.AxisListType.X

D = 1024
T = 4096
NCH = 8
NCORES = 8
EPS = 1e-5
NEXP = 8
P4_DEBUG = 0
P4_DEBUG_NE = 1
P4_DEBUG_NT = 16
_ign = 0


class Prog:
    def __init__(self):
        self.nc = bass.Bass("TRN2", target_bir_lowering=False)
        self.top = ExitStack()
        self.ctx = self.top
        self.sems = {}
        self.uid = 0

    def name(self, p="t"):
        self.uid += 1
        return "%s%d" % (p, self.uid)

    def sb(self, shape, dtype, name=None):
        return self.ctx.enter_context(self.nc.sbuf_tensor(name or self.name(), shape, dtype))

    def psum(self, shape, dtype=F32, name=None):
        return self.ctx.enter_context(self.nc.psum_tensor(name or self.name("p"), shape, dtype))

    def dram(self, shape, dtype=F32, name=None):
        return self.nc.dram_tensor(name or self.name("d"), shape, dtype).ap()

    @contextmanager
    def scope(self):
        old = self.ctx
        try:
            with ExitStack() as es:
                self.ctx = es
                yield
        finally:
            self.ctx = old

    def sem(self, nm):
        if nm not in self.sems:
            self.sems[nm] = [self.top.enter_context(self.nc.semaphore(nm)), 0]
        return self.sems[nm]

    def inc(self, nm, instr, k=1):
        s = self.sem(nm)
        instr.then_inc(s[0], k)
        s[1] += k
        return s[1]

    def val(self, nm):
        return self.sem(nm)[1]

    def wait(self, eng, nm, v=None):
        s = self.sem(nm)
        eng.wait_ge(s[0], s[1] if v is None else v)

    def fence(self, eng, nm, instr):
        v = self.inc(nm, instr)
        eng.wait_ge(self.sem(nm)[0], v)
        return instr


def make_consts(P):
    nc = P.nc
    c = {}
    c["identf"] = P.sb([128, 128], F32, "identf")
    c["identb"] = P.sb([128, 128], BF16, "identb")
    c["ones_col_bf"] = P.sb([128, 1], BF16, "onescolbf")
    c["ones1_f"] = P.sb([1, 128], F32, "ones1f")
    c["ones1_bf"] = P.sb([1, 128], BF16, "ones1bf")
    c["eps11"] = P.sb([1, 1], F32, "eps11")
    c["ones_row_bf"] = P.sb([1, 512], BF16, "onesrowbf")
    ident_hbm = P.hbm["cc_ident"]
    with nc.Block("consts") as blk:
        @blk.sync
        def _(s):
            P.inc("ldc", s.dma_start(out=c["identf"][:], in_=ident_hbm[:, :]), 16)

        @blk.gpsimd
        def _(g):
            P.inc("ldc", g.dma_start(out=c["identb"][:], in_=ident_hbm[:, :]), 16)
            g.memset(c["ones_col_bf"][:], 1.0)
            g.memset(c["eps11"][:], EPS)
            g.memset(c["ones1_f"][:], 1.0)
            g.memset(c["ones1_bf"][:], 1.0)
            g.memset(c["ones_row_bf"][:], 1.0)

        @blk.vector
        def _(v):
            P.wait(v, "ldc")
    return c


def ln_tile_block(P, PS, W, src, dst, bfbuf, C, gcols=None, bcols=None, res=None, tag="ln",
                  tmp1=None, tmp2=None):
    nc = P.nc
    with P.scope():
        if tmp1 is not None and tmp2 is not None:
            mu_row = tmp1[0:1, 0:W]
            r1 = tmp2[0:1, 0:W]
        else:
            mu_row = P.sb([1, W], F32)[:, :]
            r1 = P.sb([1, W], F32)[:, :]
        r2 = P.sb([1, W], F32)
        ex2_row = r1
        var_row = r1
        srow = r2
        rstd_row = r1
        tmp1 = tmp1[:, 0:W] if tmp1 is not None else P.sb([128, W], F32)[:, :]
        tmp2 = tmp2[:, 0:W] if tmp2 is not None else P.sb([128, W], F32)[:, :]
        psA = PS[0:1, 6, 0:W]
        psB = PS[0:1, 7, 0:W]
        bcmu = PS[:, 4, 0:W]
        bcrstd = PS[:, 5, 0:W]
        with nc.Block(tag) as blk:
            @blk.vector
            def _(v):
                for c in range(NCH):
                    ins = v.tensor_copy(bfbuf[:, c, 0:W], src[c])
                P.inc("evV", ins)
            mark_cast = P.val("evV")

            @blk.tensor
            def _(t):
                P.wait(t, "evV", mark_cast)
                for c in range(NCH):
                    ins = t.matmul(psA, C["ones_col_bf"][:], bfbuf[:, c, 0:W],
                                   start=(c == 0), stop=(c == NCH - 1))
                P.inc("mm", ins)
            markA = P.val("mm")

            @blk.scalar
            def _(s):
                P.wait(s, "mm", markA)
                for c in range(NCH):
                    ins = s.activation(bfbuf[:, c, 0:W], src[c], AF.Square, bias=0.0, scale=1.0)
                P.inc("evS", ins)
            mark_sq = P.val("evS")

            @blk.tensor
            def _(t):
                P.wait(t, "evS", mark_sq)
                for c in range(NCH):
                    ins = t.matmul(psB, C["ones_col_bf"][:], bfbuf[:, c, 0:W],
                                   start=(c == 0), stop=(c == NCH - 1))
                P.inc("mm", ins)
            markB = P.val("mm")

            @blk.vector
            def _(v):
                v.wait_ge(P.sem("mm")[0], markB)
                P.fence(v, "fV", v.tensor_scalar(mu_row, psA, 1.0 / D, None, op0=OP.mult))
                P.fence(v, "fV", v.tensor_scalar(ex2_row, psB, 1.0 / D, None, op0=OP.mult))
                P.fence(v, "fV", v.tensor_tensor(r2[:], mu_row, mu_row, op=OP.mult))
                ins = v.tensor_tensor(var_row, ex2_row, r2[:], op=OP.subtract)
                P.inc("evV", ins)
            mark_var = P.val("evV")

            @blk.scalar
            def _(s):
                s.wait_ge(P.sem("evV")[0], mark_var)
                ins = s.activation(srow[:], var_row, AF.Sqrt, bias=C["eps11"][:, 0:1], scale=1.0)
                P.inc("evS", ins)
            mark_sqrt = P.val("evS")

            @blk.vector
            def _(v):
                v.wait_ge(P.sem("evS")[0], mark_sqrt)
                ins = v.reciprocal(rstd_row, srow[:])
                P.inc("evV", ins)
            mark_rstd = P.val("evV")

            @blk.tensor
            def _(t):
                t.wait_ge(P.sem("evV")[0], mark_rstd)
                t.matmul(bcmu, C["ones1_f"][:], mu_row, start=True, stop=True)
                ins = t.matmul(bcrstd, C["ones1_f"][:], rstd_row, start=True, stop=True)
                P.inc("mm", ins)
            markC = P.val("mm")

            @blk.vector
            def _(v):
                v.wait_ge(P.sem("mm")[0], markC)
                for c in range(NCH):
                    lastc = (c == NCH - 1)
                    P.fence(v, "fV", v.tensor_tensor(tmp1, src[c], bcmu, op=OP.subtract))
                    if gcols is None and res is None:
                        ins = v.tensor_tensor(dst[c], tmp1, bcrstd, op=OP.mult)
                    else:
                        P.fence(v, "fV", v.tensor_tensor(tmp2, tmp1, bcrstd, op=OP.mult))
                        if gcols is not None:
                            y_ap = tmp1
                            P.fence(v, "fV", v.tensor_scalar(y_ap, tmp2, gcols[:, c:c + 1], bcols[:, c:c + 1],
                                                             op0=OP.mult, op1=OP.add))
                        else:
                            y_ap = tmp2
                        if res is not None:
                            ins = v.tensor_tensor(dst[c], res[c], y_ap, op=OP.add)
                        else:
                            ins = v.tensor_copy(dst[c], y_ap)
                    if not lastc:
                        P.fence(v, "fV", ins)
                P.inc("evV", ins)


def emit_phase1(P, R, C):
    nc = P.nc
    H = P.hbm
    NG = 8
    NQT = 8
    xm = {}

    with P.scope():
        PS = P.psum([128, 8, 512], F32, "ps1")
        K_g = P.sb([128, T], BF16, "p1K")
        Q_g = P.sb([128, T], BF16, "p1Q")
        V_g = P.sb([128, 32, 130], BF16, "p1V")
        Z_g = P.sb([128, T], BF16, "p1Z")
        xbf0 = P.sb([128, NCH, 512], BF16, "p1x0")
        xbf = [xbf0, xbf0]
        wk = P.sb([128, NCH, 128], BF16, "p1wk")
        wq = P.sb([128, NCH, 128], BF16, "p1wq")
        wv = P.sb([128, NCH, 128], BF16, "p1wv")
        wo_g = P.sb([128, 1024], BF16, "p1wo")
        E_sb = P.sb([128, 2, 512], BF16, "p1E")
        trim = P.sb([128, 4, 512], BF16, "p1tri")
        QS_t = P.sb([128, 512], BF16, "p1qs")
        row_a = P.sb([1, 512], F32, "p1ra")
        row_b = P.sb([1, 512], F32, "p1rb")
        srow_a = P.sb([1, 512], F32, "p1sa")
        srow_b = P.sb([1, 512], F32, "p1sb")
        negB = P.sb([1, 2, 512], BF16, "p1nb")
        maxK2 = P.sb([1, 2], F32, "p1mk")
        zc_sb = P.sb([64, 512], F32, "p1zc")
        lntmp1 = P.sb([128, 512], F32, "p1lt1")
        lntmp2 = P.sb([128, 512], F32, "p1lt2")

        with nc.Block("p1prolog") as blk:
            @blk.gpsimd
            def _(g):
                g.memset(V_g[:, :, 64:65], 1.0)
                g.memset(V_g[:, :, 129:130], 1.0)
                P.inc("ldg", g.dma_start(
                    out=trim[:, :, :],
                    in_=H["cc_trimask"].rearrange("a p f -> p a f")), 16)

        for g_ in range(NG):
            _p1_group(P, R, C, H, PS, g_, xm,
                      K_g, Q_g, V_g, Z_g, xbf, wk, wq, wv, wo_g, E_sb, trim,
                      QS_t, row_a, row_b, srow_a, srow_b, negB, maxK2, zc_sb)

        for tt in range(NQT):
            q0 = tt * 512
            rs = [R[c][:, q0:q0 + 512] for c in range(NCH)]
            ln_tile_block(P, PS, 512, rs, rs, xbf0, C, tag="p1ln%d" % tt,
                          tmp1=lntmp1, tmp2=lntmp2)


def _p1_group(P, R, C, H, PS, g_, xm,
              K_g, Q_g, V_g, Z_g, xbf, wk, wq, wv, wo_g, E_sb, trim,
              QS_t, row_a, row_b, srow_a, srow_b, negB, maxK2, zc_sb):
    nc = P.nc
    NQT = 8

    mm0 = P.val("mm")
    mmark = {}
    m = mm0
    for n in range(8):
        m += 1; mmark[("K", n)] = m
        m += 1; mmark[("Q", n)] = m
        for pt in range(4):
            m += 1; mmark[("V", n, pt)] = m
    for sl in range(8):
        m += 1; mmark[("ks", sl)] = m
    s0 = P.val("evS")
    smark = {}
    s = s0
    for n in range(8):
        s += 1; smark[("K", n)] = s
        s += 1; smark[("Q", n)] = s
    for sl in range(8):
        s += 1; smark[("ksq", sl)] = s
    v0 = P.val("evV")
    vmark = {}
    v = v0
    for n in range(8):
        for pt in range(4):
            v += 1; vmark[("V", n, pt)] = v
    for sl in range(8):
        v += 1; vmark[("mx", sl)] = v

    with nc.Block("p1g%dprod" % g_) as blk:
        @blk.gpsimd
        def _(g):
            P.inc("ldg", g.dma_start(
                out=wk[:, :, :],
                in_=H["l0a_WKT"][:, g_ * 128:(g_ + 1) * 128].rearrange("(c p) hd -> p c hd", p=128)), 16)
            P.inc("ldg", g.dma_start(
                out=wq[:, :, :],
                in_=H["l0a_WQT"][:, g_ * 128:(g_ + 1) * 128].rearrange("(c p) hd -> p c hd", p=128)), 16)
            P.inc("ldg", g.dma_start(
                out=wv[:, :, :],
                in_=H["l0a_WVT"][:, g_ * 128:(g_ + 1) * 128].rearrange("(c p) hd -> p c hd", p=128)), 16)
            P.inc("ldg", g.dma_start(
                out=wo_g[:, :], in_=H["l0a_WOT"][g_ * 128:(g_ + 1) * 128, :]), 16)
            for n in range(8):
                if n >= 1:
                    g.wait_ge(P.sem("mm")[0], mmark[("V", n - 1, 3)])
                xm[(g_, n)] = P.inc("xd1", g.dma_start(
                    out=xbf[n % 2][:, :, :],
                    in_=H["stream_in"][:, n * 512:(n + 1) * 512].rearrange("(c p) t -> p c t", p=128)), 16)

        @blk.tensor
        def _(t):
            P.wait(t, "ldg")
            for n in range(8):
                t.wait_ge(P.sem("xd1")[0], xm[(g_, n)])
                if n >= 1:
                    t.wait_ge(P.sem("evS")[0], smark[("Q", n - 1)])
                for k in range(NCH):
                    ins = t.matmul(PS[:, 0, :], wk[:, k, :], xbf[n % 2][:, k, :],
                                   start=(k == 0), stop=(k == NCH - 1))
                P.inc("mm", ins)
                for k in range(NCH):
                    ins = t.matmul(PS[:, 1, :], wq[:, k, :], xbf[n % 2][:, k, :],
                                   start=(k == 0), stop=(k == NCH - 1))
                P.inc("mm", ins)
                for pt in range(4):
                    if (n, pt) != (0, 0) and (n, pt) != (0, 1):
                        prev = (n, pt - 2) if pt >= 2 else (n - 1, pt + 2)
                        t.wait_ge(P.sem("evV")[0], vmark[("V", prev[0], prev[1])])
                    for k in range(NCH):
                        ins = t.matmul(PS[:, 2 + pt % 2, 0:128],
                                       xbf[n % 2][:, k, pt * 128:(pt + 1) * 128],
                                       wv[:, k, :], start=(k == 0), stop=(k == NCH - 1))
                    P.inc("mm", ins)
            for sl in range(8):
                t.wait_ge(P.sem("evS")[0], smark[("ksq", sl)])
                if sl >= 1:
                    t.wait_ge(P.sem("evV")[0], vmark[("mx", sl - 1)])
                t.matmul(PS[0:1, 6, :], C["ones_col_bf"][0:64, 0:1], QS_t[0:64, :],
                         start=True, stop=True)
                ins = t.matmul(PS[0:1, 7, :], C["ones_col_bf"][64:128, 0:1], QS_t[64:128, :],
                               start=True, stop=True)
                P.inc("mm", ins)

        @blk.scalar
        def _(s):
            for n in range(8):
                s.wait_ge(P.sem("mm")[0], mmark[("K", n)])
                ins = s.activation(K_g[:, n * 512:(n + 1) * 512], PS[:, 0, :], AF.Copy,
                                   bias=0.0, scale=1.0)
                P.inc("evS", ins)
                s.wait_ge(P.sem("mm")[0], mmark[("Q", n)])
                ins = s.activation(Q_g[:, n * 512:(n + 1) * 512], PS[:, 1, :], AF.Copy,
                                   bias=0.0, scale=1.0)
                P.inc("evS", ins)
            for sl in range(8):
                if sl == 0:
                    s.wait_ge(P.sem("evS")[0], smark[("Q", 7)])
                else:
                    s.wait_ge(P.sem("mm")[0], mmark[("ks", sl - 1)])
                ins = s.activation(QS_t[:, :], K_g[:, sl * 512:(sl + 1) * 512], AF.Square,
                                   bias=0.0, scale=1.0)
                P.inc("evS", ins)
                assert P.val("evS") == smark[("ksq", sl)]

        @blk.vector
        def _(v):
            for n in range(8):
                for pt in range(4):
                    v.wait_ge(P.sem("mm")[0], mmark[("V", n, pt)])
                    pc = n * 4 + pt
                    src = PS[:, 2 + pt % 2, 0:128].rearrange("p (h dh) -> p h dh", h=2)
                    dst = V_g[:, pc, 0:130].rearrange("p (h dhx) -> p h dhx", dhx=65)[:, :, 0:64]
                    ins = v.tensor_copy(dst, src)
                    P.inc("evV", ins)
                    assert P.val("evV") == vmark[("V", n, pt)]
            for sl in range(8):
                v.wait_ge(P.sem("mm")[0], mmark[("ks", sl)])
                P.fence(v, "fV", v.tensor_reduce(row_a[:, 0:1], PS[0:1, 6, :], axis=AX, op=OP.max))
                P.fence(v, "fV", v.tensor_reduce(row_b[:, 0:1], PS[0:1, 7, :], axis=AX, op=OP.max))
                if sl == 0:
                    P.fence(v, "fV", v.tensor_copy(maxK2[:, 0:1], row_a[:, 0:1]))
                    ins = v.tensor_copy(maxK2[:, 1:2], row_b[:, 0:1])
                else:
                    P.fence(v, "fV", v.tensor_tensor(maxK2[:, 0:1], maxK2[:, 0:1], row_a[:, 0:1], op=OP.max))
                    ins = v.tensor_tensor(maxK2[:, 1:2], maxK2[:, 1:2], row_b[:, 0:1], op=OP.max)
                P.inc("evV", ins)
                assert P.val("evV") == vmark[("mx", sl)]

    for qt in range(NQT):
        _p1_qtile(P, R, C, PS, g_, qt,
                  K_g, Q_g, V_g, Z_g, wo_g, E_sb, trim,
                  QS_t, row_a, row_b, srow_a, srow_b, negB, maxK2, zc_sb)


def _p1_qtile(P, R, C, PS, g_, qt,
              K_g, Q_g, V_g, Z_g, wo_g, E_sb, trim,
              QS_t, row_a, row_b, srow_a, srow_b, negB, maxK2, zc_sb):
    nc = P.nc
    qs = qt * 512
    npc = 4 * (qt + 1)

    mm0 = P.val("mm")
    s0 = P.val("evS")
    v0 = P.val("evV")
    mmark, smark, vmark = {}, {}, {}
    m = mm0
    m += 1; mmark["qrows"] = m
    for h in range(2):
        for j in range(npc):
            m += 1; mmark[("S", h, j)] = m
            m += 1; mmark[("A", h, j)] = m
        m += 1; mmark[("bc", h)] = m
    for dt in range(NCH):
        m += 1; mmark[("wo", dt)] = m
    s = s0
    s += 1; smark["qsq"] = s
    s += 1; smark["sqrt"] = s
    for h in range(2):
        for j in range(npc):
            s += 1; smark[("E", h, j)] = s
    v = v0
    v += 1; vmark["rows"] = v
    v += 1; vmark["negb"] = v
    for h in range(2):
        for j in range(max(0, npc - 4), npc):
            v += 1; vmark[("tm", h, j)] = v
        v += 1; vmark[("rl", h)] = v
        v += 1; vmark[("zn", h)] = v
    for dt in range(NCH):
        v += 1; vmark[("ra", dt)] = v

    with nc.Block("p1g%dq%d" % (g_, qt)) as blk:
        @blk.scalar
        def _(s):
            ins = s.activation(QS_t[:, :], Q_g[:, qs:qs + 512], AF.Square, bias=0.0, scale=1.0)
            P.inc("evS", ins)
            assert P.val("evS") == smark["qsq"]
            s.wait_ge(P.sem("evV")[0], vmark["rows"])
            s.activation(srow_a[:, :], row_a[:, :], AF.Sqrt, bias=0.0, scale=1.0)
            ins = s.activation(srow_b[:, :], row_b[:, :], AF.Sqrt, bias=0.0, scale=1.0)
            P.inc("evS", ins)
            assert P.val("evS") == smark["sqrt"]
            for h in range(2):
                for j in range(npc):
                    s.wait_ge(P.sem("mm")[0], mmark[("S", h, j)])
                    if (h, j) > (0, 1):
                        ph, pj = (h, j - 2) if j >= 2 else (h - 1, npc - 1 if j == 1 else npc - 2)
                        t_ = mmark[("A", ph, pj)]
                        s.wait_ge(P.sem("mm")[0], t_)
                    ins = s.activation(E_sb[:, j % 2, :], PS[:, j % 4, :], AF.Exp,
                                       bias=0.0, scale=0.125)
                    P.inc("evS", ins)
                    assert P.val("evS") == smark[("E", h, j)]

        @blk.tensor
        def _(t):
            t.wait_ge(P.sem("evS")[0], smark["qsq"])
            t.matmul(PS[0:1, 6, :], C["ones_col_bf"][0:64, 0:1], QS_t[0:64, :], start=True, stop=True)
            ins = t.matmul(PS[0:1, 7, :], C["ones_col_bf"][64:128, 0:1], QS_t[64:128, :], start=True, stop=True)
            P.inc("mm", ins)
            assert P.val("mm") == mmark["qrows"]
            t.wait_ge(P.sem("evV")[0], vmark["negb"])
            for h in range(2):
                hh = 64 * h
                for j in range(npc):
                    if j >= 4:
                        t.wait_ge(P.sem("evS")[0], smark[("E", h, j - 4)])
                    elif h == 1 and npc - 4 + j >= 0:
                        t.wait_ge(P.sem("evS")[0], smark[("E", 0, npc - 4 + j)])
                    t.matmul(PS[:, j % 4, :], K_g[hh:hh + 64, j * 128:(j + 1) * 128],
                             Q_g[hh:hh + 64, qs:qs + 512], start=True, stop=False)
                    ins = t.matmul(PS[:, j % 4, :], C["ones1_bf"][:],
                                   negB[:, h, :], start=False, stop=True)
                    P.inc("mm", ins)
                    assert P.val("mm") == mmark[("S", h, j)]
                    t.wait_ge(P.sem("evS")[0], smark[("E", h, j)])
                    if j >= npc - 4:
                        t.wait_ge(P.sem("evV")[0], vmark[("tm", h, j)])
                    ins = t.matmul(PS[0:65, 4, :], V_g[:, j, 65 * h:65 * h + 65],
                                   E_sb[:, j % 2, :], start=(j == 0), stop=(j == npc - 1))
                    P.inc("mm", ins)
                    assert P.val("mm") == mmark[("A", h, j)]
                t.wait_ge(P.sem("evV")[0], vmark[("rl", h)])
                ins = t.matmul(PS[:, 5, :], C["ones1_f"][:], row_a[:, :], start=True, stop=True)
                P.inc("mm", ins)
                assert P.val("mm") == mmark[("bc", h)]
            t.wait_ge(P.sem("evV")[0], vmark[("zn", 1)])
            for dt in range(NCH):
                if dt >= 2:
                    t.wait_ge(P.sem("evV")[0], vmark[("ra", dt - 2)])
                ins = t.matmul(PS[:, 6 + dt % 2, :], wo_g[:, dt * 128:(dt + 1) * 128],
                               Z_g[:, qs:qs + 512], start=True, stop=True)
                P.inc("mm", ins)
                assert P.val("mm") == mmark[("wo", dt)]

        @blk.vector
        def _(v):
            v.wait_ge(P.sem("mm")[0], mmark["qrows"])
            P.fence(v, "fV", v.tensor_scalar(row_a[:, :], PS[0:1, 6, :], maxK2[0:1, 0:1], None, op0=OP.mult))
            ins = v.tensor_scalar(row_b[:, :], PS[0:1, 7, :], maxK2[0:1, 1:2], None, op0=OP.mult)
            P.inc("evV", ins)
            assert P.val("evV") == vmark["rows"]
            v.wait_ge(P.sem("evS")[0], smark["sqrt"])
            P.fence(v, "fV", v.tensor_scalar(negB[:, 0, :], srow_a[:, :], -1.0, None, op0=OP.mult))
            ins = v.tensor_scalar(negB[:, 1, :], srow_b[:, :], -1.0, None, op0=OP.mult)
            P.inc("evV", ins)
            assert P.val("evV") == vmark["negb"]
            for h in range(2):
                for j in range(max(0, npc - 4), npc):
                    a = j - (npc - 4)
                    v.wait_ge(P.sem("evS")[0], smark[("E", h, j)])
                    ins = v.tensor_tensor(E_sb[:, j % 2, :], E_sb[:, j % 2, :], trim[:, a, :], op=OP.mult)
                    P.inc("evV", ins)
                    assert P.val("evV") == vmark[("tm", h, j)]
                v.wait_ge(P.sem("mm")[0], mmark[("A", h, npc - 1)])
                P.fence(v, "fV", v.reciprocal(row_a[:, :], PS[64:65, 4, :]))
                P.inc("evV", v.tensor_copy(zc_sb[:, :], PS[0:64, 4, :]))
                assert P.val("evV") == vmark[("rl", h)]
                v.wait_ge(P.sem("mm")[0], mmark[("bc", h)])
                ins = v.tensor_tensor(Z_g[64 * h:64 * h + 64, qs:qs + 512], zc_sb[:, :],
                                      PS[0:64, 5, :], op=OP.mult)
                P.inc("evV", ins)
                assert P.val("evV") == vmark[("zn", h)]
            for dt in range(NCH):
                v.wait_ge(P.sem("mm")[0], mmark[("wo", dt)])
                ins = v.tensor_tensor(R[dt][:, qs:qs + 512], R[dt][:, qs:qs + 512],
                                      PS[:, 6 + dt % 2, :], op=OP.add)
                P.inc("evV", ins)
                assert P.val("evV") == vmark[("ra", dt)]


def emit_phase3(P, R, C):
    nc = P.nc
    H = P.hbm
    TC = 256
    NCK = T // TC
    wim, wsm, wom = {}, {}, {}

    with P.scope():
        PS = P.psum([128, 8, 512], F32, "ps3")
        xbf = P.sb([128, NCH, TC], BF16, "p3x")
        gate = P.sb([128, NCH, TC], F32, "p3g")
        val_bf = P.sb([128, NCH, TC], BF16, "p3v")
        tanh_sb = P.sb([128, NCH, TC], F32, "p3t")
        a_sb = P.sb([128, NCH, TC], F32, "p3a")
        b_sb = P.sb([128, NCH, TC], F32, "p3b")
        h_sb = P.sb([128, NCH, TC], F32, "p3h")
        hstate = P.sb([128, NCH], F32, "p3hs")
        osb = P.sb([128, 2, TC], F32, "p3o")
        winbuf = P.sb([128, 2, NCH, 128], BF16, "p3wi")
        wsbuf = P.sb([128, 2, NCH, 128], BF16, "p3ws")
        wobuf = P.sb([128, 2, NCH, 128], BF16, "p3wo")
        bin_c = P.sb([128, 16], F32, "p3bin")
        binrow = P.sb([1, D], BF16, "p3binrow")
        bs_c = P.sb([128, NCH], F32, "p3bs")
        bo_c = P.sb([128, NCH], F32, "p3bo")
        borow = P.sb([1, D], BF16, "p3borow")

        with nc.Block("p3prolog") as blk:
            @blk.sync
            def _(s):
                for m in range(16):
                    P.inc("lds", s.dma_start(
                        out=bin_c[:, m:m + 1],
                        in_=H["l1a_b_in"][m * 128:(m + 1) * 128].rearrange("(p one) -> p one", one=1)), 16)
                for m in range(NCH):
                    P.inc("lds", s.dma_start(
                        out=bs_c[:, m:m + 1],
                        in_=H["l1a_b_s"][m * 128:(m + 1) * 128].rearrange("(p one) -> p one", one=1)), 16)
                    P.inc("lds", s.dma_start(
                        out=bo_c[:, m:m + 1],
                        in_=H["l1a_b_o"][m * 128:(m + 1) * 128].rearrange("(p one) -> p one", one=1)), 16)

            @blk.gpsimd
            def _(g):
                P.inc("ldg", g.dma_start(
                    out=binrow[:, :],
                    in_=H["l1a_b_in"][D:2 * D].rearrange("(one e) -> one e", one=1)), 16)
                P.inc("ldg", g.dma_start(
                    out=borow[:, :],
                    in_=H["l1a_b_o"].rearrange("(one e) -> one e", one=1)), 16)
                for m in range(2):
                    wim[(0, m)] = P.inc("w1d3", g.dma_start(
                        out=winbuf[:, m % 2, :, :],
                        in_=H["l1a_WinT"][:, m * 128:(m + 1) * 128].rearrange("(c p) e -> p c e", p=128)), 16)

            @blk.vector
            def _(v):
                P.wait(v, "lds")
                P.wait(v, "ldg")

        def p3_chunk(ct):
            q0 = ct * TC
            last = (ct == NCK - 1)
            mm0 = P.val("mm")
            mmk = lambda g_: mm0 + g_ + 1
            s0 = P.val("evS")
            smk = lambda g_: s0 + g_ + 1
            v0 = P.val("evV")
            vm = {"cast": v0 + 1, "hbf": v0 + 2}
            for m in range(NCH):
                vm[("stt", m)] = v0 + 3 + m

            with nc.Block("p3c%d" % ct) as blk:
                @blk.gpsimd
                def _(g):
                    for m in range(2, 16):
                        g.wait_ge(P.sem("mm")[0], mmk(m - 2))
                        wim[(ct, m)] = P.inc("w1d3", g.dma_start(
                            out=winbuf[:, m % 2, :, :],
                            in_=H["l1a_WinT"][:, m * 128:(m + 1) * 128].rearrange("(c p) e -> p c e", p=128)), 16)
                    for m in range(NCH):
                        if m >= 2:
                            g.wait_ge(P.sem("mm")[0], mmk(16 + m - 2))
                        wsm[(ct, m)] = P.inc("w2d3", g.dma_start(
                            out=wsbuf[:, m % 2, :, :],
                            in_=H["l1a_WsT"][:, m * 128:(m + 1) * 128].rearrange("(c p) e -> p c e", p=128)), 16)
                    for m in range(NCH):
                        if m >= 2:
                            g.wait_ge(P.sem("mm")[0], mmk(24 + m - 2))
                        wom[(ct, m)] = P.inc("w3d3", g.dma_start(
                            out=wobuf[:, m % 2, :, :],
                            in_=H["l1a_WoT"][:, m * 128:(m + 1) * 128].rearrange("(c p) e -> p c e", p=128)), 16)
                    if not last:
                        for m in range(2):
                            g.wait_ge(P.sem("mm")[0], mmk(14 + m))
                            wim[(ct + 1, m)] = P.inc("w1d3", g.dma_start(
                                out=winbuf[:, m % 2, :, :],
                                in_=H["l1a_WinT"][:, m * 128:(m + 1) * 128].rearrange("(c p) e -> p c e", p=128)), 16)

                @blk.vector
                def _(v):
                    for c in range(NCH):
                        ins = v.tensor_copy(xbf[:, c, :], R[c][:, q0:q0 + TC])
                    P.inc("evV", ins)
                    assert P.val("evV") == vm["cast"]
                    v.wait_ge(P.sem("evS")[0], smk(23))
                    for c in range(NCH):
                        v.tensor_scalar(a_sb[:, c, :], gate[:, c, :], -1.0, 1.0, op0=OP.mult, op1=OP.add)
                    for c in range(NCH):
                        ins = v.tensor_tensor(b_sb[:, c, :], gate[:, c, :], tanh_sb[:, c, :], op=OP.mult)
                    P.fence(v, "fV", ins)
                    for c in range(NCH):
                        if ct == 0:
                            ins = v.tensor_tensor_scan(h_sb[:, c, :], a_sb[:, c, :], b_sb[:, c, :],
                                                       0.0, op0=OP.mult, op1=OP.add)
                        else:
                            ins = v.tensor_tensor_scan(h_sb[:, c, :], a_sb[:, c, :], b_sb[:, c, :],
                                                       hstate[:, c:c + 1], op0=OP.mult, op1=OP.add)
                    P.fence(v, "fV", ins)
                    for c in range(NCH):
                        v.tensor_copy(hstate[:, c:c + 1], h_sb[:, c, TC - 1:TC])
                    for c in range(NCH):
                        ins = v.tensor_copy(val_bf[:, c, :], h_sb[:, c, :])
                    P.inc("evV", ins)
                    assert P.val("evV") == vm["hbf"]
                    for m in range(NCH):
                        v.wait_ge(P.sem("evS")[0], smk(24 + m))
                        ins = v.scalar_tensor_tensor(R[m][:, q0:q0 + TC], R[m][:, q0:q0 + TC], 2.0,
                                                     osb[:, m % 2, :], op0=OP.mult, op1=OP.add)
                        P.inc("evV", ins)
                        assert P.val("evV") == vm[("stt", m)]

                @blk.tensor
                def _(t):
                    t.wait_ge(P.sem("evV")[0], vm["cast"])
                    for m in range(16):
                        t.wait_ge(P.sem("w1d3")[0], wim[(ct, m)])
                        if m >= 2:
                            t.wait_ge(P.sem("evS")[0], smk(m - 2))
                        isval = (m >= NCH)
                        for k in range(NCH):
                            ins = t.matmul(PS[:, m % 2, 0:TC], winbuf[:, m % 2, k, :],
                                           xbf[:, k, :], start=(k == 0), stop=(k == NCH - 1 and not isval))
                        if isval:
                            ins = t.matmul(PS[:, m % 2, 0:TC],
                                           binrow[0:1, (m - NCH) * 128:(m - NCH + 1) * 128],
                                           C["ones_row_bf"][0:1, 0:TC], start=False, stop=True)
                        P.inc("mm", ins)
                        assert P.val("mm") == mmk(m)
                    t.wait_ge(P.sem("evS")[0], smk(15))
                    for m in range(NCH):
                        t.wait_ge(P.sem("w2d3")[0], wsm[(ct, m)])
                        t.wait_ge(P.sem("evS")[0], smk(16 + m - 2))
                        for k in range(NCH):
                            ins = t.matmul(PS[:, m % 2, 0:TC], wsbuf[:, m % 2, k, :],
                                           val_bf[:, k, :], start=(k == 0), stop=(k == NCH - 1))
                        P.inc("mm", ins)
                        assert P.val("mm") == mmk(16 + m)
                    t.wait_ge(P.sem("evV")[0], vm["hbf"])
                    for m in range(NCH):
                        t.wait_ge(P.sem("w3d3")[0], wom[(ct, m)])
                        t.wait_ge(P.sem("evS")[0], smk(24 + m - 2))
                        for k in range(NCH):
                            ins = t.matmul(PS[:, m % 2, 0:TC], wobuf[:, m % 2, k, :],
                                           val_bf[:, k, :], start=(k == 0), stop=False)
                        ins = t.matmul(PS[:, m % 2, 0:TC],
                                       borow[0:1, m * 128:(m + 1) * 128],
                                       C["ones_row_bf"][0:1, 0:TC], start=False, stop=True)
                        P.inc("mm", ins)
                        assert P.val("mm") == mmk(24 + m)

                @blk.scalar
                def _(s):
                    for m in range(16):
                        s.wait_ge(P.sem("mm")[0], mmk(m))
                        if m < NCH:
                            ins = s.activation(gate[:, m, :], PS[:, m % 2, 0:TC], AF.Sigmoid,
                                               bias=bin_c[:, m:m + 1], scale=1.0)
                        else:
                            ins = s.activation(val_bf[:, m - NCH, :], PS[:, m % 2, 0:TC], AF.Copy,
                                               bias=0.0, scale=1.0)
                        P.inc("evS", ins)
                        assert P.val("evS") == smk(m)
                    for m in range(NCH):
                        s.wait_ge(P.sem("mm")[0], mmk(16 + m))
                        ins = s.activation(tanh_sb[:, m, :], PS[:, m % 2, 0:TC], AF.Tanh,
                                           bias=bs_c[:, m:m + 1], scale=1.0)
                        P.inc("evS", ins)
                        assert P.val("evS") == smk(16 + m)
                    for m in range(NCH):
                        s.wait_ge(P.sem("mm")[0], mmk(24 + m))
                        if m >= 2:
                            s.wait_ge(P.sem("evV")[0], vm[("stt", m - 2)])
                        ins = s.activation(osb[:, m % 2, :], PS[:, m % 2, 0:TC], AF.Copy,
                                           bias=0.0, scale=1.0)
                        P.inc("evS", ins)
                        assert P.val("evS") == smk(24 + m)

        for ct in range(NCK):
            p3_chunk(ct)
        for tt in range(NCK):
            q0 = tt * TC
            rs = [R[c][:, q0:q0 + TC] for c in range(NCH)]
            ln_tile_block(P, PS, TC, rs, rs, xbf, C, tag="p3ln%d" % tt,
                          tmp1=gate[:, 0, :], tmp2=gate[:, 1, :])


def emit_phase2(P, R, C):
    nc = P.nc
    H = P.hbm
    TTW = 512
    NTT = T // TTW
    NFT = 8
    selrows = P.dram([NEXP, T], F32, "p2selrows")
    cc_in = P.dram([NEXP, 1], F32, "p2ccin")
    cc_out = P.dram([NEXP * NCORES, 1], F32, "p2ccout")
    w1m, w2m, sl8m, slem = {}, {}, {}, {}

    with P.scope():
        PS = P.psum([128, 8, 512], F32, "ps2")
        wr_sb = P.sb([128, NCH, NEXP], F32, "p2wr")
        br_row = P.sb([1, NEXP], F32, "p2br")
        b2_sb = P.sb([NEXP, 1024], BF16, "p2b2")
        b1c = P.sb([128, NEXP, NFT], F32, "p2b1c")
        rmask = P.sb([NEXP, 1], F32, "p2rmask")

        with nc.Block("p2prolog") as blk:
            @blk.sync
            def _(s):
                P.inc("lds", s.dma_start(
                    out=wr_sb[:, :, :],
                    in_=H["l0_Wr"].rearrange("(c p) e -> p c e", p=128)), 16)
                P.inc("lds", s.dma_start(
                    out=br_row[:, :],
                    in_=H["l0_br"].rearrange("(one e) -> one e", one=1)), 16)
                P.inc("lds", s.dma_start(out=rmask[:, :], in_=H["rankmask"][:, :]), 16)
                for e in range(NEXP):
                    for ft in range(NFT):
                        P.inc("lds", s.dma_start(
                            out=b1c[:, e, ft:ft + 1],
                            in_=H["l0_b1"][e, ft * 128:(ft + 1) * 128].rearrange("(p one) -> p one", one=1)), 16)

            @blk.gpsimd
            def _(g):
                P.inc("ldg", g.dma_start(out=b2_sb[:, :], in_=H["l0_b2"][:, :]), 16)

        with P.scope():
            mask_fm = P.sb([NEXP, T], F32, "p2maskfm")
            pos_fm = P.sb([NEXP, T], F32, "p2posfm")
            kept = mask_fm
            sel_fm = pos_fm
            gatec = P.sb([128, 32], F32, "p2gate")
            counts = P.sb([NEXP, 1], F32, "p2counts")
            call_sb = P.sb([NEXP, NEXP], F32, "p2call")
            offs = P.sb([NEXP, 1], F32, "p2offs")
            lsc = {}
            for nm, sh in [("lcopy", [128, NEXP]), ("m1", [128, 1]), ("negm1", [128, 1]),
                           ("mask_tm", [128, NEXP]), ("eexp", [128, NEXP]),
                           ("den", [128, 1]), ("ktm", [128, NEXP]), ("stm", [128, NEXP])]:
                lsc[nm] = P.sb(sh, F32, "p2" + nm)
            lg_ps = PS[:, 0, 0:NEXP]
            tr_ps = PS[0:8, 1, 0:128]
            tb_ps = PS[:, 2, 0:NEXP]

            for t32 in range(32):
                t0 = t32 * 128
                with nc.Block("p2r%d" % t32) as blk:
                    @blk.tensor
                    def _(t):
                        if t32 == 0:
                            P.wait(t, "lds")
                            P.wait(t, "ldg")
                        for c in range(NCH):
                            t.matmul(lg_ps, R[c][:, t0:t0 + 128], wr_sb[:, c, :],
                                     start=(c == 0), stop=False)
                        ins = t.matmul(lg_ps, C["ones1_f"][:], br_row[:, :], start=False, stop=True)
                        P.inc("mm", ins)

                    @blk.vector
                    def _(v):
                        P.wait(v, "mm")
                        P.fence(v, "fV", v.tensor_copy(lsc["lcopy"][:], lg_ps))
                        P.fence(v, "fV", v.tensor_reduce(lsc["m1"][:], lsc["lcopy"][:], axis=AX, op=OP.max))
                        v.tensor_reduce(lsc["negm1"][:], lsc["lcopy"][:], axis=AX, op=OP.max, negate=True)
                        ins = v.tensor_scalar(lsc["mask_tm"][:], lsc["lcopy"][:], lsc["m1"][:, 0:1], None, op0=OP.is_equal)
                        P.inc("evV", ins)

                    @blk.scalar
                    def _(s):
                        P.wait(s, "evV")
                        ins = s.activation(lsc["eexp"][:], lsc["lcopy"][:], AF.Exp,
                                           bias=lsc["negm1"][:, 0:1], scale=1.0)
                        P.inc("evS", ins)

                    @blk.vector
                    def _(v):
                        P.wait(v, "evS")
                        P.fence(v, "fV", v.tensor_reduce(lsc["den"][:], lsc["eexp"][:], axis=AX, op=OP.add))
                        ins = v.reciprocal(gatec[:, t32:t32 + 1], lsc["den"][:])
                        P.inc("evV", ins)

                    @blk.tensor
                    def _(t):
                        P.wait(t, "evV")
                        ins = t.transpose(tr_ps, lsc["mask_tm"][:], C["identf"][:])
                        P.inc("mm", ins)

                    @blk.vector
                    def _(v):
                        P.wait(v, "mm")
                        ins = v.tensor_copy(mask_fm[:, t0:t0 + 128], tr_ps)
                        P.inc("evV", ins)

            with nc.Block("p2cc") as blk:
                @blk.vector
                def _(v):
                    ins = v.tensor_reduce(counts[:, 0:1], mask_fm[:, :], axis=AX, op=OP.add)
                    P.inc("evV", ins)

                @blk.sync
                def _(s):
                    P.wait(s, "evV")
                    P.inc("st", s.dma_start(out=cc_in[:, :], in_=counts[:, :]), 16)

                @blk.gpsimd
                def _(g):
                    P.wait(g, "st")
                    P.inc("cc", g.collective_compute(
                        "AllGather", OP.bypass,
                        replica_groups=[list(range(NCORES))],
                        ins=[cc_in[:, :].opt()],
                        outs=[cc_out[:, :].opt()]), 1)
                    P.wait(g, "cc")
                    P.inc("ldg", g.dma_start(
                        out=call_sb[:, :],
                        in_=cc_out.rearrange("(r e) one -> r (e one)", e=NEXP)), 16)

            with nc.Block("p2off") as blk:
                @blk.tensor
                def _(t):
                    P.wait(t, "ldg")
                    ins = t.matmul(PS[0:8, 3, 0:1], call_sb[:, :], rmask[:, :], start=True, stop=True)
                    P.inc("mm", ins)

                @blk.vector
                def _(v):
                    P.wait(v, "mm")
                    P.fence(v, "fV", v.tensor_copy(offs[:, 0:1], PS[0:8, 3, 0:1]))
                    P.fence(v, "fV", v.tensor_tensor_scan(pos_fm[:, :], mask_fm[:, :], mask_fm[:, :],
                                                          offs[:, 0:1], op0=OP.add, op1=OP.max))
                    P.fence(v, "fV", v.tensor_scalar(pos_fm[:, :], pos_fm[:, :], float(T * NCORES // NEXP), None, op0=OP.is_lt))
                    ins = v.tensor_tensor(kept[:, :], mask_fm[:, :], pos_fm[:, :], op=OP.mult)
                    P.inc("evV", ins)

            for t32 in range(32):
                t0 = t32 * 128
                with nc.Block("p2s%d" % t32) as blk:
                    @blk.tensor
                    def _(t):
                        ins = t.transpose(PS[:, 2, 0:NEXP], kept[:, t0:t0 + 128], C["identf"][0:8, 0:8])
                        P.inc("mm", ins)

                    @blk.vector
                    def _(v):
                        P.wait(v, "mm")
                        P.fence(v, "fV", v.tensor_copy(lsc["ktm"][:], tb_ps))
                        ins = v.tensor_scalar(lsc["stm"][:], lsc["ktm"][:], gatec[:, t32:t32 + 1], None, op0=OP.mult)
                        P.inc("evV", ins)

                    @blk.tensor
                    def _(t):
                        P.wait(t, "evV")
                        ins = t.transpose(tr_ps, lsc["stm"][:], C["identf"][:])
                        P.inc("mm", ins)

                    @blk.vector
                    def _(v):
                        P.wait(v, "mm")
                        ins = v.tensor_copy(sel_fm[:, t0:t0 + 128], tr_ps)
                        P.inc("evV", ins)

            with nc.Block("p2selst") as blk:
                @blk.sync
                def _(s):
                    P.wait(s, "evV")
                    P.inc("st", s.dma_start(out=selrows[:, :], in_=sel_fm[:, :]), 16)

        bfbuf = P.sb([128, NCH, TTW], BF16, "p2bf")
        acc = P.sb([128, NCH, TTW], F32, "p2acc")
        h_bf = P.sb([128, NFT, TTW], BF16, "p2h")
        w1buf = P.sb([128, 2, NCH, 128], BF16, "p2w1")
        w2buf = P.sb([128, 2, 1024], BF16, "p2w2")
        bcsel = P.sb([128, TTW], F32, "p2bcsel")
        otmp = P.sb([128, TTW], F32, "p2otmp")
        sls8 = P.sb([NEXP, TTW], BF16, "p2sls8")
        selrow = P.sb([1, 2, TTW], BF16, "p2selrow")
        with nc.Block("p2pref") as blk:
            @blk.gpsimd
            def _(g):
                for s_ in range(2):
                    w1m[(0, 0, s_)] = P.inc("w1d2", g.dma_start(
                        out=w1buf[:, s_ % 2, :, :],
                        in_=H["l0_W1"][0, :, s_ * 128:(s_ + 1) * 128].rearrange("(c p) f -> p c f", p=128)), 16)
                for fc in range(2):
                    w2m[(0, 0, fc)] = P.inc("w2d2", g.dma_start(
                        out=w2buf[:, fc % 2, :],
                        in_=H["l0_W2"][0, fc * 128:(fc + 1) * 128, :]), 16)
                P.wait(g, "st")
                sl8m[0] = P.inc("sld2", g.dma_start(out=sls8[:, :], in_=selrows[:, 0:TTW]), 16)
                slem[(0, 0)] = P.inc("sld2", g.dma_start(
                    out=selrow[:, 0, :], in_=selrows[0, 0:TTW].rearrange("(one f) -> one f", one=1)), 16)
        def p2_block(tt, e, q0):
            last = (tt == NTT - 1 and e == NEXP - 1)
            ne_tt, ne_e = (tt, e + 1) if e < NEXP - 1 else (tt + 1, 0)
            mm0 = P.val("mm")
            marks = {}
            m = mm0
            if e == 0:
                m += 1
                marks["b2"] = m
            m += 1
            marks["bc"] = m
            for ft in range(NFT):
                m += 1
                marks[("h", ft)] = m
            for fc in range(NFT):
                m += 1
                marks[("o", fc)] = m
            s0 = P.val("evS")
            smarks = {"bcsel": s0 + 1}
            for ft in range(NFT):
                smarks[("h", ft)] = s0 + 2 + ft
            v0 = P.val("evV")
            vmarks = {}
            v = v0
            if e == 0:
                v += 1
                vmarks["b2"] = v
            v += 1
            vmarks["cast"] = v
            v += 1
            vmarks["oev"] = v

            with nc.Block("p2t%de%d" % (tt, e)) as blk:
                @blk.gpsimd
                def _(g):
                    for s_ in range(2, NFT):
                        g.wait_ge(P.sem("mm")[0], marks[("h", s_ - 2)])
                        w1m[(tt, e, s_)] = P.inc("w1d2", g.dma_start(
                            out=w1buf[:, s_ % 2, :, :],
                            in_=H["l0_W1"][e, :, s_ * 128:(s_ + 1) * 128].rearrange("(c p) f -> p c f", p=128)), 16)
                    if not last:
                        for s_ in range(2):
                            g.wait_ge(P.sem("mm")[0], marks[("h", NFT - 2 + s_)])
                            w1m[(ne_tt, ne_e, s_)] = P.inc("w1d2", g.dma_start(
                                out=w1buf[:, s_ % 2, :, :],
                                in_=H["l0_W1"][ne_e, :, s_ * 128:(s_ + 1) * 128].rearrange("(c p) f -> p c f", p=128)), 16)
                    for fc in range(2, NFT):
                        g.wait_ge(P.sem("mm")[0], marks[("o", fc - 2)])
                        w2m[(tt, e, fc)] = P.inc("w2d2", g.dma_start(
                            out=w2buf[:, fc % 2, :],
                            in_=H["l0_W2"][e, fc * 128:(fc + 1) * 128, :]), 16)
                    if not last:
                        for fc in range(2):
                            g.wait_ge(P.sem("mm")[0], marks[("o", NFT - 2 + fc)])
                            w2m[(ne_tt, ne_e, fc)] = P.inc("w2d2", g.dma_start(
                                out=w2buf[:, fc % 2, :],
                                in_=H["l0_W2"][ne_e, fc * 128:(fc + 1) * 128, :]), 16)
                        nq0 = ne_tt * TTW
                        if ne_e == 0:
                            sl8m[ne_tt] = P.inc("sld2", g.dma_start(
                                out=sls8[:, :], in_=selrows[:, nq0:nq0 + TTW]), 16)
                        slem[(ne_tt, ne_e)] = P.inc("sld2", g.dma_start(
                            out=selrow[:, (tt * NEXP + e + 1) % 2, :],
                            in_=selrows[ne_e, nq0:nq0 + TTW].rearrange("(one f) -> one f", one=1)), 16)

                @blk.vector
                def _(v):
                    if e == 0:
                        v.wait_ge(P.sem("mm")[0], marks["b2"])
                        for dt in range(NCH):
                            ins = v.tensor_copy(acc[:, dt, :], PS[:, dt, 0:TTW])
                        P.inc("evV", ins)
                        assert P.val("evV") == vmarks["b2"]
                    v.wait_ge(P.sem("evS")[0], smarks["bcsel"])
                    for c in range(NCH):
                        ins = v.tensor_tensor(bfbuf[:, c, :], R[c][:, q0:q0 + TTW], bcsel[:, :], op=OP.mult)
                    P.inc("evV", ins)
                    assert P.val("evV") == vmarks["cast"]
                    v.wait_ge(P.sem("mm")[0], marks[("o", NFT - 1)])
                    for dt in range(NCH):
                        P.fence(v, "fV", v.tensor_tensor(otmp[:], PS[:, dt, 0:TTW], bcsel[:, :], op=OP.mult))
                        ins = v.tensor_tensor(acc[:, dt, :], acc[:, dt, :], otmp[:], op=OP.add)
                        if dt < NCH - 1:
                            P.fence(v, "fV", ins)
                    P.inc("evV", ins)
                    assert P.val("evV") == vmarks["oev"]

                @blk.tensor
                def _(t):
                    if e == 0:
                        t.wait_ge(P.sem("sld2")[0], sl8m[tt])
                        for dt in range(NCH):
                            ins = t.matmul(PS[:, dt, 0:TTW], b2_sb[:, dt * 128:(dt + 1) * 128],
                                           sls8[:, :], start=True, stop=True)
                        P.inc("mm", ins)
                        assert P.val("mm") == marks["b2"]
                        t.wait_ge(P.sem("evV")[0], vmarks["b2"])
                    t.wait_ge(P.sem("sld2")[0], slem[(tt, e)])
                    ins = t.matmul(PS[:, 2, 0:TTW], C["ones1_bf"][:], selrow[:, (tt * NEXP + e) % 2, :],
                                   start=True, stop=True)
                    P.inc("mm", ins)
                    assert P.val("mm") == marks["bc"]
                    t.wait_ge(P.sem("evV")[0], vmarks["cast"])
                    for ft in range(NFT):
                        t.wait_ge(P.sem("w1d2")[0], w1m[(tt, e, ft)])
                        if ft >= 2:
                            t.wait_ge(P.sem("evS")[0], smarks[("h", ft - 2)])
                        for k in range(NCH):
                            ins = t.matmul(PS[:, ft % 2, 0:TTW],
                                           w1buf[:, ft % 2, k, :],
                                           bfbuf[:, k, :], start=(k == 0), stop=(k == NCH - 1))
                        P.inc("mm", ins)
                        assert P.val("mm") == marks[("h", ft)]
                    t.wait_ge(P.sem("evS")[0], smarks[("h", NFT - 1)])
                    for fc in range(NFT):
                        t.wait_ge(P.sem("w2d2")[0], w2m[(tt, e, fc)])
                        for dt in range(NCH):
                            ins = t.matmul(PS[:, dt, 0:TTW], w2buf[:, fc % 2, dt * 128:(dt + 1) * 128],
                                           h_bf[:, fc, :], start=(fc == 0), stop=(fc == NFT - 1))
                        P.inc("mm", ins)
                        assert P.val("mm") == marks[("o", fc)]

                @blk.scalar
                def _(s):
                    s.wait_ge(P.sem("mm")[0], marks["bc"])
                    ins = s.activation(bcsel[:, :], PS[:, 2, 0:TTW], AF.Copy, bias=0.0, scale=1.0)
                    P.inc("evS", ins)
                    assert P.val("evS") == smarks["bcsel"]
                    for ft in range(NFT):
                        s.wait_ge(P.sem("mm")[0], marks[("h", ft)])
                        ins = s.activation(h_bf[:, ft, :], PS[:, ft % 2, 0:TTW], AF.Relu,
                                           bias=b1c[:, e, ft:ft + 1], scale=1.0)
                        P.inc("evS", ins)
                        assert P.val("evS") == smarks[("h", ft)]

        for tt in range(NTT):
            q0 = tt * TTW
            for e in range(NEXP):
                p2_block(tt, e, q0)
            srcs = [acc[:, c, :] for c in range(NCH)]
            rs = [R[c][:, q0:q0 + TTW] for c in range(NCH)]
            with nc.Block("p2res%d" % tt) as blk:
                @blk.vector
                def _(v):
                    for c in range(NCH):
                        ins = v.tensor_tensor(acc[:, c, :], acc[:, c, :], rs[c], op=OP.add)
                    P.inc("evV", ins)
            ln_tile_block(P, PS, TTW, srcs, rs, bfbuf, C, tag="p2ln%d" % tt,
                          tmp1=otmp, tmp2=bcsel)


def emit_phase4(P, R, C):
    nc = P.nc
    H = P.hbm
    TTW = 512
    NTT = T // TTW
    NFT = 32
    rwrows = P.dram([NEXP, T], F32, "p4rwrows")
    P.p4cache = (P.dram([NEXP, D, 4096], BF16, "p4w1c"), P.dram([NEXP, 4096, D], BF16, "p4w2c"))
    P.wc1_marks = {}
    P.wc2_marks = {}

    with P.scope():
        PS = P.psum([128, 8, 512], F32, "ps4")
        wr_sb = P.sb([128, NCH, NEXP], F32, "p4wr")
        br_row = P.sb([1, NEXP], F32, "p4br")
        b2_sb = P.sb([NEXP, 1024], BF16, "p4b2")
        b1c = P.sb([128, NEXP, NFT], F32, "p4b1c")
        gcols = P.sb([128, NCH], F32, "p4g")
        bcols = P.sb([128, NCH], F32, "p4b")

        with nc.Block("p4prolog") as blk:
            @blk.sync
            def _(s):
                P.inc("lds", s.dma_start(
                    out=wr_sb[:, :, :],
                    in_=H["l1f_WrT"].rearrange("(c p) e -> p c e", p=128)), 16)
                P.inc("lds", s.dma_start(
                    out=br_row[:, :],
                    in_=H["l1f_br"].rearrange("(one e) -> one e", one=1)), 16)
                for c in range(NCH):
                    P.inc("lds", s.dma_start(
                        out=gcols[:, c:c + 1],
                        in_=H["l1f_gamma"][c * 128:(c + 1) * 128].rearrange("(p one) -> p one", one=1)), 16)
                    P.inc("lds", s.dma_start(
                        out=bcols[:, c:c + 1],
                        in_=H["l1f_beta"][c * 128:(c + 1) * 128].rearrange("(p one) -> p one", one=1)), 16)
                for e in range(NEXP):
                    for ft in range(NFT):
                        P.inc("lds", s.dma_start(
                            out=b1c[:, e, ft:ft + 1],
                            in_=H["l1f_b1"][e, ft * 128:(ft + 1) * 128].rearrange("(p one) -> p one", one=1)), 16)

            @blk.gpsimd
            def _(g):
                P.inc("ldg", g.dma_start(out=b2_sb[:, :], in_=H["l1f_b2"][:, :]), 16)

        with P.scope():
            rw_fm = P.sb([NEXP, T], F32, "p4rwfm")
            lsc = {}
            for nm, sh in [("lcopy", [128, NEXP]), ("m1", [128, 1]), ("negm1", [128, 1]),
                           ("is1", [128, NEXP]), ("tmp8", [128, NEXP]), ("m2", [128, 1]),
                           ("keep", [128, NEXP]), ("eexp", [128, NEXP]), ("ek", [128, NEXP]),
                           ("den", [128, 1]), ("rden", [128, 1]), ("rw_tm", [128, NEXP])]:
                lsc[nm] = P.sb(sh, F32, "p4" + nm)
            _p4_router(P, R, C, H, PS, wr_sb, br_row, rw_fm, lsc, rwrows)
            if P4_DEBUG == 1:
                P.skip_store = True
                with nc.Block("dbg1") as blk:
                    @blk.sync
                    def _(s):
                        P.inc("st", s.dma_start(out=H["stream_out"][0:8, :], in_=rw_fm[:, :]), 16)
                        P.wait(s, "st")
                return

        bfbuf = P.sb([128, NCH, TTW], BF16, "p4bf")
        acc = P.sb([128, NCH, TTW], F32, "p4acc")
        h_bf = P.sb([128, NFT, TTW], BF16, "p4h")
        w1buf = P.sb([128, 2, NCH, 128], BF16, "p4w1")
        w2buf = P.sb([128, 2, 1024], BF16, "p4w2")
        bcrw = P.sb([128, TTW], F32, "p4bcrw")
        otmp = P.sb([128, TTW], F32, "p4otmp")
        rws8 = P.sb([NEXP, TTW], BF16, "p4rws8")
        rwrow = P.sb([1, 2, TTW], BF16, "p4rwrow")
        with nc.Block("p4pref") as blk:
            @blk.gpsimd
            def _(g):
                for s_ in range(2):
                    P.w1_marks[(0, 0, s_)] = P.inc("w1d0", g.dma_start(
                        out=w1buf[:, s_ % 2, :, :],
                        in_=H["l1f_W1T"][0, :, s_ * 128:(s_ + 1) * 128].rearrange("(c p) f -> p c f", p=128)), 16)
                for fc in range(2):
                    P.w2_marks[(0, 0, fc)] = P.inc("w2d0", g.dma_start(
                        out=w2buf[:, fc % 2, :],
                        in_=H["l1f_W2T"][0, fc * 128:(fc + 1) * 128, :]), 16)
                P.wait(g, "st")
                P.rw8_marks[0] = P.inc("rwd", g.dma_start(out=rws8[:, :], in_=rwrows[:, 0:TTW]), 16)
                P.rwe_marks[(0, 0)] = P.inc("rwd", g.dma_start(
                    out=rwrow[:, 0, :], in_=rwrows[0, 0:TTW].rearrange("(one f) -> one f", one=1)), 16)
        for tt in range(1 if P4_DEBUG in (2, 3) else (P4_DEBUG_NT if P4_DEBUG == 4 else NTT)):
            q0 = tt * TTW
            for e in range(NEXP if P4_DEBUG != 3 else P4_DEBUG_NE):
                _p4_block(P, R, C, H, PS, tt, e, q0, TTW,
                          bfbuf, acc, h_bf, w1buf, w2buf, bcrw, otmp, rws8, rwrow, b1c, b2_sb, rwrows)
            if P4_DEBUG == 3:
                P.skip_store = True
                with nc.Block("dbg3") as blk:
                    @blk.gpsimd
                    def _(g):
                        for ft in range(16):
                            P.inc("st", g.dma_start(out=H["stream_out"][0:128, ft * TTW:(ft + 1) * TTW], in_=h_bf[:, ft, :]), 16)
                        for ft in range(16, NFT):
                            P.inc("st", g.dma_start(out=H["stream_out"][384:512, (ft - 16) * TTW:(ft - 15) * TTW], in_=h_bf[:, ft, :]), 16)
                        P.inc("st", g.dma_start(out=H["stream_out"][128:256, 0:TTW], in_=bcrw[:, :]), 16)
                        for c in range(NCH):
                            P.inc("st", g.dma_start(out=H["stream_out"][256:384, c * TTW:(c + 1) * TTW], in_=acc[:, c, :]), 16)
                        P.wait(g, "st")
                return
            srcs = [acc[:, c, :] for c in range(NCH)]
            rs = [R[c][:, q0:q0 + TTW] for c in range(NCH)]
            with nc.Block("p4res%d" % tt) as blk:
                @blk.vector
                def _(v):
                    for c in range(NCH):
                        ins = v.tensor_tensor(acc[:, c, :], acc[:, c, :], rs[c], op=OP.add)
                    P.inc("evV", ins)
            if P4_DEBUG == 2:
                P.skip_store = True
                with nc.Block("dbg2") as blk:
                    @blk.sync
                    def _(s):
                        for c in range(NCH):
                            P.inc("st", s.dma_start(out=H["stream_out"][c * 128:(c + 1) * 128, 0:TTW], in_=acc[:, c, :]), 16)
                        P.wait(s, "st")
                return
            ln_tile_block(P, PS, TTW, srcs, rs, bfbuf, C,
                          gcols=gcols, bcols=bcols, res=rs, tag="p4ln%d" % tt,
                          tmp1=otmp, tmp2=bcrw)
        if P4_DEBUG == 4:
            return
        for tt in range(NTT):
            q0 = tt * TTW
            rs = [R[c][:, q0:q0 + TTW] for c in range(NCH)]
            ln_tile_block(P, PS, TTW, rs, rs, bfbuf, C, tag="p4lno%d" % tt,
                          tmp1=otmp, tmp2=bcrw)


def _p4_router(P, R, C, H, PS, wr_sb, br_row, rw_fm, lsc, rwrows):
    nc = P.nc
    NEXP_ = NEXP
    lg_ps = PS[:, 0, 0:NEXP]
    tr_ps = PS[0:8, 1, 0:128]
    if True:
        for t32 in range(32):
            t0 = t32 * 128
            with nc.Block("p4r%d" % t32) as blk:
                @blk.tensor
                def _(t):
                    if t32 == 0:
                        P.wait(t, "lds")
                        P.wait(t, "ldg")
                    for c in range(NCH):
                        t.matmul(lg_ps, R[c][:, t0:t0 + 128], wr_sb[:, c, :],
                                 start=(c == 0), stop=False)
                    ins = t.matmul(lg_ps, C["ones1_f"][:], br_row[:, :], start=False, stop=True)
                    P.inc("mm", ins)

                @blk.vector
                def _(v):
                    P.wait(v, "mm")
                    P.fence(v, "fV", v.tensor_copy(lsc["lcopy"][:], lg_ps))
                    P.fence(v, "fV", v.tensor_reduce(lsc["m1"][:], lsc["lcopy"][:], axis=AX, op=OP.max))
                    v.tensor_reduce(lsc["negm1"][:], lsc["lcopy"][:], axis=AX, op=OP.max, negate=True)
                    P.fence(v, "fV", v.tensor_scalar(lsc["is1"][:], lsc["lcopy"][:], lsc["m1"][:, 0:1], None, op0=OP.is_equal))
                    P.fence(v, "fV", v.scalar_tensor_tensor(lsc["tmp8"][:], lsc["is1"][:], -1e30, lsc["lcopy"][:],
                                                            op0=OP.mult, op1=OP.add))
                    P.fence(v, "fV", v.tensor_reduce(lsc["m2"][:], lsc["tmp8"][:], axis=AX, op=OP.max))
                    ins = v.tensor_scalar(lsc["keep"][:], lsc["lcopy"][:], lsc["m2"][:, 0:1], None, op0=OP.is_ge)
                    P.inc("evV", ins)

                @blk.scalar
                def _(s):
                    P.wait(s, "evV")
                    ins = s.activation(lsc["eexp"][:], lsc["lcopy"][:], AF.Exp,
                                       bias=lsc["negm1"][:, 0:1], scale=1.0)
                    P.inc("evS", ins)

                @blk.vector
                def _(v):
                    P.wait(v, "evS")
                    P.fence(v, "fV", v.tensor_tensor(lsc["ek"][:], lsc["eexp"][:], lsc["keep"][:], op=OP.mult))
                    P.fence(v, "fV", v.tensor_reduce(lsc["den"][:], lsc["ek"][:], axis=AX, op=OP.add))
                    P.fence(v, "fV", v.reciprocal(lsc["rden"][:], lsc["den"][:]))
                    ins = v.tensor_scalar(lsc["rw_tm"][:], lsc["ek"][:], lsc["rden"][:, 0:1], None, op0=OP.mult)
                    P.inc("evV", ins)

                @blk.tensor
                def _(t):
                    P.wait(t, "evV")
                    ins = t.transpose(tr_ps, lsc["rw_tm"][:], C["identf"][:])
                    P.inc("mm", ins)

                @blk.vector
                def _(v):
                    P.wait(v, "mm")
                    ins = v.tensor_copy(rw_fm[:, t0:t0 + 128], tr_ps)
                    P.inc("evV", ins)

        with nc.Block("p4rw") as blk:
            @blk.sync
            def _(s):
                P.wait(s, "evV")
                P.inc("st", s.dma_start(out=rwrows[:, :], in_=rw_fm[:, :]), 16)


def _p4_block(P, R, C, H, PS, tt, e, q0, TTW, bfbuf, acc, h_bf, w1buf, w2buf, bcrw, otmp,
              rws8, rwrow, b1c, b2_sb, rwrows):
    nc = P.nc
    NFT = 32
    NTT = T // TTW
    w1s = "w1d%d" % (tt % 2)
    w2s = "w2d%d" % (tt % 2)
    last = (tt == NTT - 1 and e == NEXP - 1)
    ne_tt, ne_e = (tt, e + 1) if e < NEXP - 1 else (tt + 1, 0)

    mm0 = P.val("mm")
    marks = {}
    m = mm0
    if e == 0:
        m += 1
        marks["b2"] = m
    m += 1
    marks["bc"] = m
    for ft in range(NFT):
        m += 1
        marks[("h", ft)] = m
    for fc in range(NFT):
        m += 1
        marks[("o", fc)] = m
    mm_end = m
    s0 = P.val("evS")
    smarks = {"bcrw": s0 + 1}
    for ft in range(NFT):
        smarks[("h", ft)] = s0 + 2 + ft
    s_end = s0 + 2 + NFT - 1
    v0 = P.val("evV")
    vmarks = {}
    v = v0
    if e == 0:
        v += 1
        vmarks["cast"] = v
        v += 1
        vmarks["b2"] = v
    v += 1
    vmarks["oev"] = v
    v_end = v

    w1c, w2c = P.p4cache
    if tt == 0:
        wc1_0 = P.val("wc1")
        wc2_0 = P.val("wc2")
        for s_ in range(NFT):
            P.wc1_marks[(e, s_)] = wc1_0 + 16 * (s_ + 1)
            P.wc2_marks[(e, s_)] = wc2_0 + 16 * (s_ + 1)

    def w1src(ee, s_, from_cache):
        if from_cache:
            return w1c[ee, :, s_ * 128:(s_ + 1) * 128].rearrange("(c p) f -> p c f", p=128)
        return H["l1f_W1T"][ee, :, s_ * 128:(s_ + 1) * 128].rearrange("(c p) f -> p c f", p=128)

    def w2src(ee, fc, from_cache):
        if from_cache:
            return w2c[ee, fc * 128:(fc + 1) * 128, :]
        return H["l1f_W2T"][ee, fc * 128:(fc + 1) * 128, :]

    with nc.Block("p4t%de%d" % (tt, e)) as blk:
        @blk.gpsimd
        def _(g):
            if tt == 0:
                for s_ in range(2, NFT):
                    g.wait_ge(P.sem("mm")[0], marks[("h", s_ - 2)])
                    g.wait_ge(P.sem("wc1")[0], P.wc1_marks[(e, s_ - 2)])
                    P.w1_marks[(tt, e, s_)] = P.inc(w1s, g.dma_start(
                        out=w1buf[:, s_ % 2, :, :], in_=w1src(e, s_, False)), 16)
                if not last:
                    if ne_tt == 0:
                        for s_ in range(2):
                            g.wait_ge(P.sem("mm")[0], marks[("h", NFT - 2 + s_)])
                            g.wait_ge(P.sem("wc1")[0], P.wc1_marks[(e, NFT - 2 + s_)])
                            P.w1_marks[(ne_tt, ne_e, s_)] = P.inc("w1d%d" % (ne_tt % 2), g.dma_start(
                                out=w1buf[:, s_ % 2, :, :], in_=w1src(ne_e, s_, False)), 16)
                for fc in range(2, NFT):
                    g.wait_ge(P.sem("mm")[0], marks[("o", fc - 2)])
                    g.wait_ge(P.sem("wc2")[0], P.wc2_marks[(e, fc - 2)])
                    P.w2_marks[(tt, e, fc)] = P.inc(w2s, g.dma_start(
                        out=w2buf[:, fc % 2, :], in_=w2src(e, fc, False)), 16)
                if not last and ne_tt == 0:
                    for fc in range(2):
                        g.wait_ge(P.sem("mm")[0], marks[("o", NFT - 2 + fc)])
                        g.wait_ge(P.sem("wc2")[0], P.wc2_marks[(e, NFT - 2 + fc)])
                        P.w2_marks[(ne_tt, ne_e, fc)] = P.inc("w2d%d" % (ne_tt % 2), g.dma_start(
                            out=w2buf[:, fc % 2, :], in_=w2src(ne_e, fc, False)), 16)
            if not last:
                nq0 = ne_tt * TTW
                if ne_e == 0:
                    P.rw8_marks[ne_tt] = P.inc("rwd", g.dma_start(
                        out=rws8[:, :], in_=rwrows[:, nq0:nq0 + TTW]), 16)
                P.rwe_marks[(ne_tt, ne_e)] = P.inc("rwd", g.dma_start(
                    out=rwrow[:, (tt * NEXP + e + 1) % 2, :],
                    in_=rwrows[ne_e, nq0:nq0 + TTW].rearrange("(one f) -> one f", one=1)), 16)

        @blk.sync
        def _(sy):
            if tt == 0:
                for s_ in range(NFT):
                    sy.wait_ge(P.sem(w1s)[0], P.w1_marks[(tt, e, s_)])
                    mk = P.inc("wc1", sy.dma_start(
                        out=w1src(e, s_, True), in_=w1buf[:, s_ % 2, :, :]), 16)
                    assert mk == P.wc1_marks[(e, s_)]
                for fc in range(NFT):
                    sy.wait_ge(P.sem(w2s)[0], P.w2_marks[(tt, e, fc)])
                    mk = P.inc("wc2", sy.dma_start(
                        out=w2src(e, fc, True), in_=w2buf[:, fc % 2, :]), 16)
                    assert mk == P.wc2_marks[(e, fc)]
                if not last and ne_tt == 1:
                    sy.wait_ge(P.sem("wc1")[0], P.wc1_marks[(e, NFT - 1)])
                    sy.wait_ge(P.sem("wc2")[0], P.wc2_marks[(e, NFT - 1)])
                    for s_ in range(2):
                        sy.wait_ge(P.sem("mm")[0], marks[("h", NFT - 2 + s_)])
                        P.w1_marks[(ne_tt, ne_e, s_)] = P.inc("w1d%d" % (ne_tt % 2), sy.dma_start(
                            out=w1buf[:, s_ % 2, :, :], in_=w1src(ne_e, s_, True)), 16)
                    for fc in range(2):
                        sy.wait_ge(P.sem("mm")[0], marks[("o", NFT - 2 + fc)])
                        P.w2_marks[(ne_tt, ne_e, fc)] = P.inc("w2d%d" % (ne_tt % 2), sy.dma_start(
                            out=w2buf[:, fc % 2, :], in_=w2src(ne_e, fc, True)), 16)
            else:
                if tt == 1 and e == 0:
                    P.wait(sy, "wc1")
                    P.wait(sy, "wc2")
                for s_ in range(2, NFT):
                    sy.wait_ge(P.sem("mm")[0], marks[("h", s_ - 2)])
                    P.w1_marks[(tt, e, s_)] = P.inc(w1s, sy.dma_start(
                        out=w1buf[:, s_ % 2, :, :], in_=w1src(e, s_, True)), 16)
                if not last:
                    for s_ in range(2):
                        sy.wait_ge(P.sem("mm")[0], marks[("h", NFT - 2 + s_)])
                        P.w1_marks[(ne_tt, ne_e, s_)] = P.inc("w1d%d" % (ne_tt % 2), sy.dma_start(
                            out=w1buf[:, s_ % 2, :, :], in_=w1src(ne_e, s_, True)), 16)
                for fc in range(2, NFT):
                    sy.wait_ge(P.sem("mm")[0], marks[("o", fc - 2)])
                    P.w2_marks[(tt, e, fc)] = P.inc(w2s, sy.dma_start(
                        out=w2buf[:, fc % 2, :], in_=w2src(e, fc, True)), 16)
                if not last:
                    for fc in range(2):
                        sy.wait_ge(P.sem("mm")[0], marks[("o", NFT - 2 + fc)])
                        P.w2_marks[(ne_tt, ne_e, fc)] = P.inc("w2d%d" % (ne_tt % 2), sy.dma_start(
                            out=w2buf[:, fc % 2, :], in_=w2src(ne_e, fc, True)), 16)

        @blk.vector
        def _(v):
            if e == 0:
                for c in range(NCH):
                    ins = v.tensor_copy(bfbuf[:, c, :], R[c][:, q0:q0 + TTW])
                P.inc("evV", ins)
                assert P.val("evV") == vmarks["cast"]
                v.wait_ge(P.sem("mm")[0], marks["b2"])
                for dt in range(NCH):
                    ins = v.tensor_copy(acc[:, dt, :], PS[:, dt, 0:TTW])
                P.inc("evV", ins)
                assert P.val("evV") == vmarks["b2"]
            v.wait_ge(P.sem("mm")[0], marks[("o", NFT - 1)])
            v.wait_ge(P.sem("evS")[0], smarks["bcrw"])
            for dt in range(NCH):
                P.fence(v, "fV", v.tensor_tensor(otmp[:], PS[:, dt, 0:TTW], bcrw[:], op=OP.mult))
                ins = v.tensor_tensor(acc[:, dt, :], acc[:, dt, :], otmp[:], op=OP.add)
                if dt < NCH - 1:
                    P.fence(v, "fV", ins)
            P.inc("evV", ins)
            assert P.val("evV") == vmarks["oev"]

        @blk.tensor
        def _(t):
            if e == 0:
                t.wait_ge(P.sem("rwd")[0], P.rw8_marks[tt])
                for dt in range(NCH):
                    ins = t.matmul(PS[:, dt, 0:TTW], b2_sb[:, dt * 128:(dt + 1) * 128],
                                   rws8[:, :], start=True, stop=True)
                P.inc("mm", ins)
                assert P.val("mm") == marks["b2"]
                t.wait_ge(P.sem("evV")[0], vmarks["b2"])
            t.wait_ge(P.sem("rwd")[0], P.rwe_marks[(tt, e)])
            ins = t.matmul(PS[:, 2, 0:TTW], C["ones1_bf"][:], rwrow[:, (tt * NEXP + e) % 2, :],
                           start=True, stop=True)
            P.inc("mm", ins)
            assert P.val("mm") == marks["bc"]
            for ft in range(NFT):
                t.wait_ge(P.sem(w1s)[0], P.w1_marks[(tt, e, ft)])
                if ft >= 2:
                    t.wait_ge(P.sem("evS")[0], smarks[("h", ft - 2)])
                for k in range(NCH):
                    ins = t.matmul(PS[:, ft % 2, 0:TTW],
                                   w1buf[:, ft % 2, k, :],
                                   bfbuf[:, k, :], start=(k == 0), stop=(k == NCH - 1))
                P.inc("mm", ins)
                assert P.val("mm") == marks[("h", ft)]
            t.wait_ge(P.sem("evS")[0], smarks[("h", NFT - 1)])
            for fc in range(NFT):
                t.wait_ge(P.sem(w2s)[0], P.w2_marks[(tt, e, fc)])
                for dt in range(NCH):
                    ins = t.matmul(PS[:, dt, 0:TTW], w2buf[:, fc % 2, dt * 128:(dt + 1) * 128],
                                   h_bf[:, fc, :], start=(fc == 0), stop=(fc == NFT - 1))
                P.inc("mm", ins)
                assert P.val("mm") == marks[("o", fc)]

        @blk.scalar
        def _(s):
            s.wait_ge(P.sem("mm")[0], marks["bc"])
            ins = s.activation(bcrw[:], PS[:, 2, 0:TTW], AF.Copy, bias=0.0, scale=1.0)
            P.inc("evS", ins)
            assert P.val("evS") == smarks["bcrw"]
            for ft in range(NFT):
                s.wait_ge(P.sem("mm")[0], marks[("h", ft)])
                ins = s.activation(h_bf[:, ft, :], PS[:, ft % 2, 0:TTW], AF.Relu,
                                   bias=b1c[:, e, ft:ft + 1], scale=1.0)
                P.inc("evS", ins)
                assert P.val("evS") == smarks[("h", ft)]

    assert P.val("mm") == mm_end
    assert P.val("evS") == s_end
    assert P.val("evV") == v_end


def build_program(phases, fused_in_hbm=True):
    P = Prog()
    nc = P.nc
    H = {}
    P.hbm = H
    P.w1_marks = {}
    P.w2_marks = {}
    P.rw8_marks = {}
    P.rwe_marks = {}

    def din(nm, shape, dt=F32):
        H[nm] = nc.dram_tensor(nm, shape, dt, kind="ExternalInput").ap()

    din("cc_ident", [128, 128])
    din("stream_in", [D, T])
    if 2 in phases:
        din("rankmask", [NEXP, 1])
        din("l0_Wr", [D, NEXP])
        din("l0_br", [NEXP])
        din("l0_W1", [NEXP, D, D])
        din("l0_W2", [NEXP, D, D])
        din("l0_b1", [NEXP, D])
        din("l0_b2", [NEXP, D])
    H["stream_out"] = nc.dram_tensor("stream_out", [D, T], F32, kind="ExternalOutput").ap()
    if 1 in phases:
        din("l0a_WKT", [D, D])
        din("l0a_WQT", [D, D])
        din("l0a_WVT", [D, D])
        din("l0a_WOT", [D, D])
        din("cc_trimask", [4, 128, 512])
    if 3 in phases:
        din("l1a_WinT", [D, 2 * D])
        din("l1a_WsT", [D, D])
        din("l1a_WoT", [D, D])
        din("l1a_b_in", [2 * D])
        din("l1a_b_s", [D])
        din("l1a_b_o", [D])
    if 4 in phases:
        din("l1f_WrT", [D, NEXP])
        din("l1f_br", [NEXP])
        din("l1f_W1T", [NEXP, D, 4096])
        din("l1f_W2T", [NEXP, 4096, D])
        din("l1f_b1", [NEXP, 4096])
        din("l1f_b2", [NEXP, D])
        din("l1f_gamma", [D])
        din("l1f_beta", [D])

    R = [P.sb([128, T], F32, "R%d" % c) for c in range(NCH)]
    C = make_consts(P)

    with nc.Block("ldR") as blk:
        @blk.sync
        def _(s):
            for c in range(NCH):
                P.inc("lds", s.dma_start(out=R[c][:], in_=H["stream_in"][c * 128:(c + 1) * 128, :]), 16)

        @blk.vector
        def _(v):
            P.wait(v, "lds")

    for ph in phases:
        if ph == 1:
            emit_phase1(P, R, C)
        elif ph == 2:
            emit_phase2(P, R, C)
        elif ph == 3:
            emit_phase3(P, R, C)
        elif ph == 4:
            emit_phase4(P, R, C)
        else:
            raise NotImplementedError("phase %d" % ph)

    if getattr(P, "skip_store", False):
        P.top.close()
        return P
    with nc.Block("stR") as blk:
        @blk.sync
        def _(s):
            for c in range(NCH):
                P.inc("st", s.dma_start(out=H["stream_out"][c * 128:(c + 1) * 128, :], in_=R[c][:]), 16)
            P.wait(s, "st")

    P.top.close()
    return P


def prep_weights(ins):
    w = {}
    w["cc_ident"] = np.eye(128, dtype=np.float32)
    w["l0a_WKT"] = np.ascontiguousarray(ins["L0_attn_W_K"].reshape(D, D).T)
    w["l0a_WQT"] = np.ascontiguousarray(ins["L0_attn_W_Q"].reshape(D, D).T)
    w["l0a_WVT"] = np.ascontiguousarray(ins["L0_attn_W_V"].reshape(D, D).T)
    w["l0a_WOT"] = np.ascontiguousarray(ins["L0_attn_W_O"].T)
    tri = np.zeros((4, 128, 512), dtype=np.float32)
    for a in range(4):
        for pr in range(128):
            q0 = pr + 128 * a
            if q0 < 512:
                tri[a, pr, q0:] = 1.0
    w["cc_trimask"] = tri
    w["l0_Wr"] = np.ascontiguousarray(ins["L0_ffn_Wr"])
    w["l0_br"] = np.ascontiguousarray(ins["L0_ffn_br"])
    w["l0_W1"] = np.ascontiguousarray(ins["L0_ffn_W1"])
    w["l0_W2"] = np.ascontiguousarray(ins["L0_ffn_W2"])
    w["l0_b1"] = np.ascontiguousarray(ins["L0_ffn_b1"])
    w["l0_b2"] = np.ascontiguousarray(ins["L0_ffn_b2"])
    w["l1a_WinT"] = np.ascontiguousarray(ins["L1_attn_W_in"].T)
    w["l1a_WsT"] = np.ascontiguousarray(ins["L1_attn_W_s"].T)
    w["l1a_WoT"] = np.ascontiguousarray(ins["L1_attn_W_o"].T)
    w["l1a_b_in"] = np.ascontiguousarray(ins["L1_attn_b_in"])
    w["l1a_b_s"] = np.ascontiguousarray(ins["L1_attn_b_s"])
    w["l1a_b_o"] = np.ascontiguousarray(ins["L1_attn_b_o"])
    w["l1f_WrT"] = np.ascontiguousarray(ins["L1_ffn_Wr"].T)
    w["l1f_br"] = np.ascontiguousarray(ins["L1_ffn_br"])
    w["l1f_W1T"] = np.ascontiguousarray(ins["L1_ffn_W1"].transpose(0, 2, 1))
    w["l1f_W2T"] = np.ascontiguousarray(ins["L1_ffn_W2"].transpose(0, 2, 1))
    w["l1f_b1"] = np.ascontiguousarray(ins["L1_ffn_b1"])
    w["l1f_b2"] = np.ascontiguousarray(ins["L1_ffn_b2"])
    w["l1f_gamma"] = np.ascontiguousarray(ins["L1_ffn_gamma"])
    w["l1f_beta"] = np.ascontiguousarray(ins["L1_ffn_beta"])
    return w


def run_phases(phases, stream_in_cores, weights):
    P = build_program(phases)
    in_maps = []
    for cid in range(NCORES):
        m = {"stream_in": stream_in_cores[cid]}
        if "rankmask" in P.hbm:
            m["rankmask"] = (np.arange(NEXP) < cid).astype(np.float32).reshape(NEXP, 1)
        for nm, arr in weights.items():
            if nm in P.hbm and nm != "stream_out":
                m[nm] = arr
        in_maps.append(m)
    res = run_bass_kernel_spmd(P.nc, in_maps, core_ids=list(range(NCORES)))
    return [res.results[c]["stream_out"] for c in range(NCORES)]


PHASE_GROUPS = [(1, 2, 3, 4)]


def kernel(**inputs):
    x = inputs["L0_attn_x"]
    w = prep_weights(inputs)
    streams = [np.ascontiguousarray(x[b].T).astype(np.float32) for b in range(NCORES)]
    for group in PHASE_GROUPS:
        streams = run_phases(group, streams, w)
    y = np.stack([o.T for o in streams], axis=0)
    return np.ascontiguousarray(y.astype(np.float32))


# version 05
# speedup vs baseline: 1.6695x; 1.0100x over previous
import numpy as np
from contextlib import ExitStack, contextmanager

import concourse.bass as bass
import concourse.mybir as mybir
from concourse.bass_utils import run_bass_kernel_spmd
from concourse.alu_op_type import AluOpType as OP

F32 = mybir.dt.float32
BF16 = mybir.dt.bfloat16
AF = mybir.ActivationFunctionType
AX = mybir.AxisListType.X

D = 1024
T = 4096
NCH = 8
NCORES = 8
EPS = 1e-5
NEXP = 8
P4_DEBUG = 0
P4_DEBUG_NE = 1
P4_DEBUG_NT = 16
_ign = 0


class Prog:
    def __init__(self):
        self.nc = bass.Bass("TRN2", target_bir_lowering=False)
        self.top = ExitStack()
        self.ctx = self.top
        self.sems = {}
        self.uid = 0

    def name(self, p="t"):
        self.uid += 1
        return "%s%d" % (p, self.uid)

    def sb(self, shape, dtype, name=None):
        return self.ctx.enter_context(self.nc.sbuf_tensor(name or self.name(), shape, dtype))

    def psum(self, shape, dtype=F32, name=None):
        return self.ctx.enter_context(self.nc.psum_tensor(name or self.name("p"), shape, dtype))

    def dram(self, shape, dtype=F32, name=None):
        return self.nc.dram_tensor(name or self.name("d"), shape, dtype).ap()

    @contextmanager
    def scope(self):
        old = self.ctx
        try:
            with ExitStack() as es:
                self.ctx = es
                yield
        finally:
            self.ctx = old

    def sem(self, nm):
        if nm not in self.sems:
            self.sems[nm] = [self.top.enter_context(self.nc.semaphore(nm)), 0]
        return self.sems[nm]

    def inc(self, nm, instr, k=1):
        s = self.sem(nm)
        instr.then_inc(s[0], k)
        s[1] += k
        return s[1]

    def val(self, nm):
        return self.sem(nm)[1]

    def wait(self, eng, nm, v=None):
        s = self.sem(nm)
        eng.wait_ge(s[0], s[1] if v is None else v)

    def fence(self, eng, nm, instr):
        v = self.inc(nm, instr)
        eng.wait_ge(self.sem(nm)[0], v)
        return instr


def make_consts(P):
    nc = P.nc
    c = {}
    c["identf"] = P.sb([128, 128], F32, "identf")
    c["identb"] = P.sb([128, 128], BF16, "identb")
    c["ones_col_bf"] = P.sb([128, 1], BF16, "onescolbf")
    c["ones1_f"] = P.sb([1, 128], F32, "ones1f")
    c["ones1_bf"] = P.sb([1, 128], BF16, "ones1bf")
    c["eps11"] = P.sb([1, 1], F32, "eps11")
    c["ones_row_bf"] = P.sb([1, 512], BF16, "onesrowbf")
    ident_hbm = P.hbm["cc_ident"]
    with nc.Block("consts") as blk:
        @blk.sync
        def _(s):
            P.inc("ldc", s.dma_start(out=c["identf"][:], in_=ident_hbm[:, :]), 16)

        @blk.gpsimd
        def _(g):
            P.inc("ldc", g.dma_start(out=c["identb"][:], in_=ident_hbm[:, :]), 16)
            g.memset(c["ones_col_bf"][:], 1.0)
            g.memset(c["eps11"][:], EPS)
            g.memset(c["ones1_f"][:], 1.0)
            g.memset(c["ones1_bf"][:], 1.0)
            g.memset(c["ones_row_bf"][:], 1.0)

        @blk.vector
        def _(v):
            P.wait(v, "ldc")
    return c


def ln_tile_block(P, PS, W, src, dst, bfbuf, C, gcols=None, bcols=None, res=None, tag="ln",
                  tmp1=None, tmp2=None):
    nc = P.nc
    with P.scope():
        if tmp1 is not None and tmp2 is not None:
            mu_row = tmp1[0:1, 0:W]
            r1 = tmp2[0:1, 0:W]
        else:
            mu_row = P.sb([1, W], F32)[:, :]
            r1 = P.sb([1, W], F32)[:, :]
        r2 = P.sb([1, W], F32)
        ex2_row = r1
        var_row = r1
        srow = r2
        rstd_row = r1
        tmp1 = tmp1[:, 0:W] if tmp1 is not None else P.sb([128, W], F32)[:, :]
        tmp2 = tmp2[:, 0:W] if tmp2 is not None else P.sb([128, W], F32)[:, :]
        psA = PS[0:1, 6, 0:W]
        psB = PS[0:1, 7, 0:W]
        bcmu = PS[:, 4, 0:W]
        bcrstd = PS[:, 5, 0:W]
        with nc.Block(tag) as blk:
            @blk.vector
            def _(v):
                for c in range(NCH):
                    ins = v.tensor_copy(bfbuf[:, c, 0:W], src[c])
                P.inc("evV", ins)
            mark_cast = P.val("evV")

            @blk.tensor
            def _(t):
                P.wait(t, "evV", mark_cast)
                for c in range(NCH):
                    ins = t.matmul(psA, C["ones_col_bf"][:], bfbuf[:, c, 0:W],
                                   start=(c == 0), stop=(c == NCH - 1))
                P.inc("mm", ins)
            markA = P.val("mm")

            @blk.scalar
            def _(s):
                P.wait(s, "mm", markA)
                for c in range(NCH):
                    ins = s.activation(bfbuf[:, c, 0:W], src[c], AF.Square, bias=0.0, scale=1.0)
                P.inc("evS", ins)
            mark_sq = P.val("evS")

            @blk.tensor
            def _(t):
                P.wait(t, "evS", mark_sq)
                for c in range(NCH):
                    ins = t.matmul(psB, C["ones_col_bf"][:], bfbuf[:, c, 0:W],
                                   start=(c == 0), stop=(c == NCH - 1))
                P.inc("mm", ins)
            markB = P.val("mm")

            @blk.vector
            def _(v):
                v.wait_ge(P.sem("mm")[0], markB)
                P.fence(v, "fV", v.tensor_scalar(mu_row, psA, 1.0 / D, None, op0=OP.mult))
                P.fence(v, "fV", v.tensor_scalar(ex2_row, psB, 1.0 / D, None, op0=OP.mult))
                P.fence(v, "fV", v.tensor_tensor(r2[:], mu_row, mu_row, op=OP.mult))
                ins = v.tensor_tensor(var_row, ex2_row, r2[:], op=OP.subtract)
                P.inc("evV", ins)
            mark_var = P.val("evV")

            @blk.scalar
            def _(s):
                s.wait_ge(P.sem("evV")[0], mark_var)
                ins = s.activation(srow[:], var_row, AF.Sqrt, bias=C["eps11"][:, 0:1], scale=1.0)
                P.inc("evS", ins)
            mark_sqrt = P.val("evS")

            @blk.vector
            def _(v):
                v.wait_ge(P.sem("evS")[0], mark_sqrt)
                ins = v.reciprocal(rstd_row, srow[:])
                P.inc("evV", ins)
            mark_rstd = P.val("evV")

            @blk.tensor
            def _(t):
                t.wait_ge(P.sem("evV")[0], mark_rstd)
                t.matmul(bcmu, C["ones1_f"][:], mu_row, start=True, stop=True)
                ins = t.matmul(bcrstd, C["ones1_f"][:], rstd_row, start=True, stop=True)
                P.inc("mm", ins)
            markC = P.val("mm")

            @blk.vector
            def _(v):
                v.wait_ge(P.sem("mm")[0], markC)
                for c in range(NCH):
                    lastc = (c == NCH - 1)
                    P.fence(v, "fV", v.tensor_tensor(tmp1, src[c], bcmu, op=OP.subtract))
                    if gcols is None and res is None:
                        ins = v.tensor_tensor(dst[c], tmp1, bcrstd, op=OP.mult)
                    else:
                        P.fence(v, "fV", v.tensor_tensor(tmp2, tmp1, bcrstd, op=OP.mult))
                        if gcols is not None:
                            y_ap = tmp1
                            P.fence(v, "fV", v.tensor_scalar(y_ap, tmp2, gcols[:, c:c + 1], bcols[:, c:c + 1],
                                                             op0=OP.mult, op1=OP.add))
                        else:
                            y_ap = tmp2
                        if res is not None:
                            ins = v.tensor_tensor(dst[c], res[c], y_ap, op=OP.add)
                        else:
                            ins = v.tensor_copy(dst[c], y_ap)
                    if not lastc:
                        P.fence(v, "fV", ins)
                P.inc("evV", ins)


def emit_phase1(P, R, C):
    nc = P.nc
    H = P.hbm
    NG = 8
    NQT = 8
    xm = {}

    with P.scope():
        PS = P.psum([128, 8, 512], F32, "ps1")
        K_g = P.sb([128, T], BF16, "p1K")
        Q_g = P.sb([128, T], BF16, "p1Q")
        V_g = P.sb([128, 32, 130], BF16, "p1V")
        Z_g = P.sb([128, T], BF16, "p1Z")
        xbf0 = P.sb([128, NCH, 512], BF16, "p1x0")
        xbf = [xbf0, xbf0]
        wk = P.sb([128, NCH, 128], BF16, "p1wk")
        wq = P.sb([128, NCH, 128], BF16, "p1wq")
        wv = P.sb([128, NCH, 128], BF16, "p1wv")
        wo_g = P.sb([128, 1024], BF16, "p1wo")
        E_sb = P.sb([128, 2, 512], BF16, "p1E")
        trim = P.sb([128, 4, 512], BF16, "p1tri")
        QS_t = P.sb([128, 512], BF16, "p1qs")
        row_a = P.sb([1, 512], F32, "p1ra")
        row_b = P.sb([1, 512], F32, "p1rb")
        srow_a = P.sb([1, 512], F32, "p1sa")
        srow_b = P.sb([1, 512], F32, "p1sb")
        negB = P.sb([1, 2, 512], BF16, "p1nb")
        maxK2 = P.sb([1, 2], F32, "p1mk")
        zc_sb = P.sb([64, 512], F32, "p1zc")
        lntmp1 = P.sb([128, 512], F32, "p1lt1")
        lntmp2 = P.sb([128, 512], F32, "p1lt2")

        with nc.Block("p1prolog") as blk:
            @blk.gpsimd
            def _(g):
                g.memset(V_g[:, :, 64:65], 1.0)
                g.memset(V_g[:, :, 129:130], 1.0)
                P.inc("ldg", g.dma_start(
                    out=trim[:, :, :],
                    in_=H["cc_trimask"].rearrange("a p f -> p a f")), 16)

        for g_ in range(NG):
            _p1_group(P, R, C, H, PS, g_, xm,
                      K_g, Q_g, V_g, Z_g, xbf, wk, wq, wv, wo_g, E_sb, trim,
                      QS_t, row_a, row_b, srow_a, srow_b, negB, maxK2, zc_sb)

        for tt in range(NQT):
            q0 = tt * 512
            rs = [R[c][:, q0:q0 + 512] for c in range(NCH)]
            ln_tile_block(P, PS, 512, rs, rs, xbf0, C, tag="p1ln%d" % tt,
                          tmp1=lntmp1, tmp2=lntmp2)


def _p1_group(P, R, C, H, PS, g_, xm,
              K_g, Q_g, V_g, Z_g, xbf, wk, wq, wv, wo_g, E_sb, trim,
              QS_t, row_a, row_b, srow_a, srow_b, negB, maxK2, zc_sb):
    nc = P.nc
    NQT = 8

    mm0 = P.val("mm")
    mmark = {}
    m = mm0
    for n in range(8):
        m += 1; mmark[("K", n)] = m
        m += 1; mmark[("Q", n)] = m
        for pt in range(4):
            m += 1; mmark[("V", n, pt)] = m
    for sl in range(8):
        m += 1; mmark[("ks", sl)] = m
    s0 = P.val("evS")
    smark = {}
    s = s0
    for n in range(8):
        s += 1; smark[("K", n)] = s
        s += 1; smark[("Q", n)] = s
    for sl in range(8):
        s += 1; smark[("ksq", sl)] = s
    v0 = P.val("evV")
    vmark = {}
    v = v0
    for n in range(8):
        for pt in range(4):
            v += 1; vmark[("V", n, pt)] = v
    for sl in range(8):
        v += 1; vmark[("mx", sl)] = v

    with nc.Block("p1g%dprod" % g_) as blk:
        @blk.gpsimd
        def _(g):
            P.inc("ldg", g.dma_start(
                out=wk[:, :, :],
                in_=H["l0a_WKT"][:, g_ * 128:(g_ + 1) * 128].rearrange("(c p) hd -> p c hd", p=128)), 16)
            P.inc("ldg", g.dma_start(
                out=wq[:, :, :],
                in_=H["l0a_WQT"][:, g_ * 128:(g_ + 1) * 128].rearrange("(c p) hd -> p c hd", p=128)), 16)
            P.inc("ldg", g.dma_start(
                out=wv[:, :, :],
                in_=H["l0a_WVT"][:, g_ * 128:(g_ + 1) * 128].rearrange("(c p) hd -> p c hd", p=128)), 16)
            P.inc("ldg", g.dma_start(
                out=wo_g[:, :], in_=H["l0a_WOT"][g_ * 128:(g_ + 1) * 128, :]), 16)
            for n in range(8):
                if n >= 1:
                    g.wait_ge(P.sem("mm")[0], mmark[("V", n - 1, 3)])
                xm[(g_, n)] = P.inc("xd1", g.dma_start(
                    out=xbf[n % 2][:, :, :],
                    in_=H["stream_in"][:, n * 512:(n + 1) * 512].rearrange("(c p) t -> p c t", p=128)), 16)

        @blk.tensor
        def _(t):
            P.wait(t, "ldg")
            for n in range(8):
                t.wait_ge(P.sem("xd1")[0], xm[(g_, n)])
                if n >= 1:
                    t.wait_ge(P.sem("evS")[0], smark[("Q", n - 1)])
                for k in range(NCH):
                    ins = t.matmul(PS[:, 0, :], wk[:, k, :], xbf[n % 2][:, k, :],
                                   start=(k == 0), stop=(k == NCH - 1))
                P.inc("mm", ins)
                for k in range(NCH):
                    ins = t.matmul(PS[:, 1, :], wq[:, k, :], xbf[n % 2][:, k, :],
                                   start=(k == 0), stop=(k == NCH - 1))
                P.inc("mm", ins)
                for pt in range(4):
                    if (n, pt) != (0, 0) and (n, pt) != (0, 1):
                        prev = (n, pt - 2) if pt >= 2 else (n - 1, pt + 2)
                        t.wait_ge(P.sem("evV")[0], vmark[("V", prev[0], prev[1])])
                    for k in range(NCH):
                        ins = t.matmul(PS[:, 2 + pt % 2, 0:128],
                                       xbf[n % 2][:, k, pt * 128:(pt + 1) * 128],
                                       wv[:, k, :], start=(k == 0), stop=(k == NCH - 1))
                    P.inc("mm", ins)
            for sl in range(8):
                t.wait_ge(P.sem("evS")[0], smark[("ksq", sl)])
                if sl >= 1:
                    t.wait_ge(P.sem("evV")[0], vmark[("mx", sl - 1)])
                t.matmul(PS[0:1, 6, :], C["ones_col_bf"][0:64, 0:1], QS_t[0:64, :],
                         start=True, stop=True)
                ins = t.matmul(PS[0:1, 7, :], C["ones_col_bf"][64:128, 0:1], QS_t[64:128, :],
                               start=True, stop=True)
                P.inc("mm", ins)

        @blk.scalar
        def _(s):
            for n in range(8):
                s.wait_ge(P.sem("mm")[0], mmark[("K", n)])
                ins = s.activation(K_g[:, n * 512:(n + 1) * 512], PS[:, 0, :], AF.Copy,
                                   bias=0.0, scale=1.0)
                P.inc("evS", ins)
                s.wait_ge(P.sem("mm")[0], mmark[("Q", n)])
                ins = s.activation(Q_g[:, n * 512:(n + 1) * 512], PS[:, 1, :], AF.Copy,
                                   bias=0.0, scale=1.0)
                P.inc("evS", ins)
            for sl in range(8):
                if sl == 0:
                    s.wait_ge(P.sem("evS")[0], smark[("Q", 7)])
                else:
                    s.wait_ge(P.sem("mm")[0], mmark[("ks", sl - 1)])
                ins = s.activation(QS_t[:, :], K_g[:, sl * 512:(sl + 1) * 512], AF.Square,
                                   bias=0.0, scale=1.0)
                P.inc("evS", ins)
                assert P.val("evS") == smark[("ksq", sl)]

        @blk.vector
        def _(v):
            for n in range(8):
                for pt in range(4):
                    v.wait_ge(P.sem("mm")[0], mmark[("V", n, pt)])
                    pc = n * 4 + pt
                    src = PS[:, 2 + pt % 2, 0:128].rearrange("p (h dh) -> p h dh", h=2)
                    dst = V_g[:, pc, 0:130].rearrange("p (h dhx) -> p h dhx", dhx=65)[:, :, 0:64]
                    ins = v.tensor_copy(dst, src)
                    P.inc("evV", ins)
                    assert P.val("evV") == vmark[("V", n, pt)]
            for sl in range(8):
                v.wait_ge(P.sem("mm")[0], mmark[("ks", sl)])
                P.fence(v, "fV", v.tensor_reduce(row_a[:, 0:1], PS[0:1, 6, :], axis=AX, op=OP.max))
                P.fence(v, "fV", v.tensor_reduce(row_b[:, 0:1], PS[0:1, 7, :], axis=AX, op=OP.max))
                if sl == 0:
                    P.fence(v, "fV", v.tensor_copy(maxK2[:, 0:1], row_a[:, 0:1]))
                    ins = v.tensor_copy(maxK2[:, 1:2], row_b[:, 0:1])
                else:
                    P.fence(v, "fV", v.tensor_tensor(maxK2[:, 0:1], maxK2[:, 0:1], row_a[:, 0:1], op=OP.max))
                    ins = v.tensor_tensor(maxK2[:, 1:2], maxK2[:, 1:2], row_b[:, 0:1], op=OP.max)
                P.inc("evV", ins)
                assert P.val("evV") == vmark[("mx", sl)]

    for qt in range(NQT):
        _p1_qtile(P, R, C, PS, g_, qt,
                  K_g, Q_g, V_g, Z_g, wo_g, E_sb, trim,
                  QS_t, row_a, row_b, srow_a, srow_b, negB, maxK2, zc_sb)


def _p1_qtile(P, R, C, PS, g_, qt,
              K_g, Q_g, V_g, Z_g, wo_g, E_sb, trim,
              QS_t, row_a, row_b, srow_a, srow_b, negB, maxK2, zc_sb):
    nc = P.nc
    qs = qt * 512
    npc = 4 * (qt + 1)

    mm0 = P.val("mm")
    s0 = P.val("evS")
    v0 = P.val("evV")
    mmark, smark, vmark = {}, {}, {}
    m = mm0
    m += 1; mmark["qrows"] = m
    for h in range(2):
        for j in range(npc):
            m += 1; mmark[("S", h, j)] = m
            m += 1; mmark[("A", h, j)] = m
        m += 1; mmark[("bc", h)] = m
    for dt in range(NCH):
        m += 1; mmark[("wo", dt)] = m
    s = s0
    s += 1; smark["qsq"] = s
    s += 1; smark["sqrt"] = s
    for h in range(2):
        for j in range(npc):
            s += 1; smark[("E", h, j)] = s
    v = v0
    v += 1; vmark["rows"] = v
    v += 1; vmark["negb"] = v
    for h in range(2):
        for j in range(max(0, npc - 4), npc):
            v += 1; vmark[("tm", h, j)] = v
        v += 1; vmark[("rl", h)] = v
        v += 1; vmark[("zn", h)] = v
    for dt in range(NCH):
        v += 1; vmark[("ra", dt)] = v

    with nc.Block("p1g%dq%d" % (g_, qt)) as blk:
        @blk.scalar
        def _(s):
            ins = s.activation(QS_t[:, :], Q_g[:, qs:qs + 512], AF.Square, bias=0.0, scale=1.0)
            P.inc("evS", ins)
            assert P.val("evS") == smark["qsq"]
            s.wait_ge(P.sem("evV")[0], vmark["rows"])
            s.activation(srow_a[:, :], row_a[:, :], AF.Sqrt, bias=0.0, scale=1.0)
            ins = s.activation(srow_b[:, :], row_b[:, :], AF.Sqrt, bias=0.0, scale=1.0)
            P.inc("evS", ins)
            assert P.val("evS") == smark["sqrt"]
            for h in range(2):
                for j in range(npc):
                    s.wait_ge(P.sem("mm")[0], mmark[("S", h, j)])
                    if (h, j) > (0, 1):
                        ph, pj = (h, j - 2) if j >= 2 else (h - 1, npc - 1 if j == 1 else npc - 2)
                        t_ = mmark[("A", ph, pj)]
                        s.wait_ge(P.sem("mm")[0], t_)
                    ins = s.activation(E_sb[:, j % 2, :], PS[:, j % 4, :], AF.Exp,
                                       bias=0.0, scale=0.125)
                    P.inc("evS", ins)
                    assert P.val("evS") == smark[("E", h, j)]

        @blk.tensor
        def _(t):
            t.wait_ge(P.sem("evS")[0], smark["qsq"])
            t.matmul(PS[0:1, 6, :], C["ones_col_bf"][0:64, 0:1], QS_t[0:64, :], start=True, stop=True)
            ins = t.matmul(PS[0:1, 7, :], C["ones_col_bf"][64:128, 0:1], QS_t[64:128, :], start=True, stop=True)
            P.inc("mm", ins)
            assert P.val("mm") == mmark["qrows"]
            t.wait_ge(P.sem("evV")[0], vmark["negb"])
            for h in range(2):
                hh = 64 * h
                for j in range(npc):
                    if j >= 4:
                        t.wait_ge(P.sem("evS")[0], smark[("E", h, j - 4)])
                    elif h == 1 and npc - 4 + j >= 0:
                        t.wait_ge(P.sem("evS")[0], smark[("E", 0, npc - 4 + j)])
                    t.matmul(PS[:, j % 4, :], K_g[hh:hh + 64, j * 128:(j + 1) * 128],
                             Q_g[hh:hh + 64, qs:qs + 512], start=True, stop=False)
                    ins = t.matmul(PS[:, j % 4, :], C["ones1_bf"][:],
                                   negB[:, h, :], start=False, stop=True)
                    P.inc("mm", ins)
                    assert P.val("mm") == mmark[("S", h, j)]
                    t.wait_ge(P.sem("evS")[0], smark[("E", h, j)])
                    if j >= npc - 4:
                        t.wait_ge(P.sem("evV")[0], vmark[("tm", h, j)])
                    ins = t.matmul(PS[0:65, 4, :], V_g[:, j, 65 * h:65 * h + 65],
                                   E_sb[:, j % 2, :], start=(j == 0), stop=(j == npc - 1))
                    P.inc("mm", ins)
                    assert P.val("mm") == mmark[("A", h, j)]
                t.wait_ge(P.sem("evV")[0], vmark[("rl", h)])
                ins = t.matmul(PS[:, 5, :], C["ones1_f"][:], row_a[:, :], start=True, stop=True)
                P.inc("mm", ins)
                assert P.val("mm") == mmark[("bc", h)]
            t.wait_ge(P.sem("evV")[0], vmark[("zn", 1)])
            for dt in range(NCH):
                if dt >= 2:
                    t.wait_ge(P.sem("evV")[0], vmark[("ra", dt - 2)])
                ins = t.matmul(PS[:, 6 + dt % 2, :], wo_g[:, dt * 128:(dt + 1) * 128],
                               Z_g[:, qs:qs + 512], start=True, stop=True)
                P.inc("mm", ins)
                assert P.val("mm") == mmark[("wo", dt)]

        @blk.vector
        def _(v):
            v.wait_ge(P.sem("mm")[0], mmark["qrows"])
            P.fence(v, "fV", v.tensor_scalar(row_a[:, :], PS[0:1, 6, :], maxK2[0:1, 0:1], None, op0=OP.mult))
            ins = v.tensor_scalar(row_b[:, :], PS[0:1, 7, :], maxK2[0:1, 1:2], None, op0=OP.mult)
            P.inc("evV", ins)
            assert P.val("evV") == vmark["rows"]
            v.wait_ge(P.sem("evS")[0], smark["sqrt"])
            P.fence(v, "fV", v.tensor_scalar(negB[:, 0, :], srow_a[:, :], -1.0, None, op0=OP.mult))
            ins = v.tensor_scalar(negB[:, 1, :], srow_b[:, :], -1.0, None, op0=OP.mult)
            P.inc("evV", ins)
            assert P.val("evV") == vmark["negb"]
            for h in range(2):
                for j in range(max(0, npc - 4), npc):
                    a = j - (npc - 4)
                    v.wait_ge(P.sem("evS")[0], smark[("E", h, j)])
                    ins = v.tensor_tensor(E_sb[:, j % 2, :], E_sb[:, j % 2, :], trim[:, a, :], op=OP.mult)
                    P.inc("evV", ins)
                    assert P.val("evV") == vmark[("tm", h, j)]
                v.wait_ge(P.sem("mm")[0], mmark[("A", h, npc - 1)])
                P.fence(v, "fV", v.reciprocal(row_a[:, :], PS[64:65, 4, :]))
                P.inc("evV", v.tensor_copy(zc_sb[:, :], PS[0:64, 4, :]))
                assert P.val("evV") == vmark[("rl", h)]
                v.wait_ge(P.sem("mm")[0], mmark[("bc", h)])
                ins = v.tensor_tensor(Z_g[64 * h:64 * h + 64, qs:qs + 512], zc_sb[:, :],
                                      PS[0:64, 5, :], op=OP.mult)
                P.inc("evV", ins)
                assert P.val("evV") == vmark[("zn", h)]
            for dt in range(NCH):
                v.wait_ge(P.sem("mm")[0], mmark[("wo", dt)])
                ins = v.tensor_tensor(R[dt][:, qs:qs + 512], R[dt][:, qs:qs + 512],
                                      PS[:, 6 + dt % 2, :], op=OP.add)
                P.inc("evV", ins)
                assert P.val("evV") == vmark[("ra", dt)]


def emit_phase3(P, R, C):
    nc = P.nc
    H = P.hbm
    TC = 256
    NCK = T // TC
    wim, wsm, wom = {}, {}, {}
    wcim, wcsm, wcom = {}, {}, {}
    winc = P.dram([D, 2 * D], BF16, "p3winc")
    wsc = P.dram([D, D], BF16, "p3wsc")
    woc = P.dram([D, D], BF16, "p3woc")

    with P.scope():
        PS = P.psum([128, 8, 512], F32, "ps3")
        xbf = P.sb([128, NCH, TC], BF16, "p3x")
        gate = P.sb([128, NCH, TC], F32, "p3g")
        val_bf = P.sb([128, NCH, TC], BF16, "p3v")
        tanh_sb = P.sb([128, NCH, TC], F32, "p3t")
        a_sb = P.sb([128, NCH, TC], F32, "p3a")
        b_sb = P.sb([128, NCH, TC], F32, "p3b")
        h_sb = P.sb([128, NCH, TC], F32, "p3h")
        hstate = P.sb([128, NCH], F32, "p3hs")
        osb = P.sb([128, 2, TC], F32, "p3o")
        winbuf = P.sb([128, 2, NCH, 128], BF16, "p3wi")
        wsbuf = P.sb([128, 2, NCH, 128], BF16, "p3ws")
        wobuf = P.sb([128, 2, NCH, 128], BF16, "p3wo")
        bin_c = P.sb([128, 16], F32, "p3bin")
        binrow = P.sb([1, D], BF16, "p3binrow")
        bs_c = P.sb([128, NCH], F32, "p3bs")
        bo_c = P.sb([128, NCH], F32, "p3bo")
        borow = P.sb([1, D], BF16, "p3borow")

        with nc.Block("p3prolog") as blk:
            @blk.sync
            def _(s):
                for m in range(16):
                    P.inc("lds", s.dma_start(
                        out=bin_c[:, m:m + 1],
                        in_=H["l1a_b_in"][m * 128:(m + 1) * 128].rearrange("(p one) -> p one", one=1)), 16)
                for m in range(NCH):
                    P.inc("lds", s.dma_start(
                        out=bs_c[:, m:m + 1],
                        in_=H["l1a_b_s"][m * 128:(m + 1) * 128].rearrange("(p one) -> p one", one=1)), 16)
                    P.inc("lds", s.dma_start(
                        out=bo_c[:, m:m + 1],
                        in_=H["l1a_b_o"][m * 128:(m + 1) * 128].rearrange("(p one) -> p one", one=1)), 16)

            @blk.gpsimd
            def _(g):
                P.inc("ldg", g.dma_start(
                    out=binrow[:, :],
                    in_=H["l1a_b_in"][D:2 * D].rearrange("(one e) -> one e", one=1)), 16)
                P.inc("ldg", g.dma_start(
                    out=borow[:, :],
                    in_=H["l1a_b_o"].rearrange("(one e) -> one e", one=1)), 16)
                for m in range(2):
                    wim[(0, m)] = P.inc("w1d3", g.dma_start(
                        out=winbuf[:, m % 2, :, :],
                        in_=H["l1a_WinT"][:, m * 128:(m + 1) * 128].rearrange("(c p) e -> p c e", p=128)), 16)

            @blk.vector
            def _(v):
                P.wait(v, "lds")
                P.wait(v, "ldg")

        def p3_chunk(ct):
            q0 = ct * TC
            last = (ct == NCK - 1)
            mm0 = P.val("mm")
            mmk = lambda g_: mm0 + g_ + 1
            s0 = P.val("evS")
            smk = lambda g_: s0 + g_ + 1
            v0 = P.val("evV")
            vm = {"cast": v0 + 1, "hbf": v0 + 2}
            for m in range(NCH):
                vm[("stt", m)] = v0 + 3 + m

            if ct == 0:
                b1_, b2_, b3_ = P.val("wci3"), P.val("wcs3"), P.val("wco3")
                for m in range(16):
                    wcim[m] = b1_ + 16 * (m + 1)
                for m in range(NCH):
                    wcsm[m] = b2_ + 16 * (m + 1)
                    wcom[m] = b3_ + 16 * (m + 1)

            def wisrc(m, cch):
                if cch:
                    return winc[:, m * 128:(m + 1) * 128].rearrange("(c p) e -> p c e", p=128)
                return H["l1a_WinT"][:, m * 128:(m + 1) * 128].rearrange("(c p) e -> p c e", p=128)

            def wssrc(m, cch):
                if cch:
                    return wsc[:, m * 128:(m + 1) * 128].rearrange("(c p) e -> p c e", p=128)
                return H["l1a_WsT"][:, m * 128:(m + 1) * 128].rearrange("(c p) e -> p c e", p=128)

            def wosrc(m, cch):
                if cch:
                    return woc[:, m * 128:(m + 1) * 128].rearrange("(c p) e -> p c e", p=128)
                return H["l1a_WoT"][:, m * 128:(m + 1) * 128].rearrange("(c p) e -> p c e", p=128)

            with nc.Block("p3c%d" % ct) as blk:
                @blk.gpsimd
                def _(g):
                    if ct == 0:
                        for m in range(2, 16):
                            g.wait_ge(P.sem("mm")[0], mmk(m - 2))
                            g.wait_ge(P.sem("wci3")[0], wcim[m - 2])
                            wim[(ct, m)] = P.inc("w1d3", g.dma_start(
                                out=winbuf[:, m % 2, :, :], in_=wisrc(m, False)), 16)
                        for m in range(NCH):
                            if m >= 2:
                                g.wait_ge(P.sem("mm")[0], mmk(16 + m - 2))
                                g.wait_ge(P.sem("wcs3")[0], wcsm[m - 2])
                            wsm[(ct, m)] = P.inc("w2d3", g.dma_start(
                                out=wsbuf[:, m % 2, :, :], in_=wssrc(m, False)), 16)
                        for m in range(NCH):
                            if m >= 2:
                                g.wait_ge(P.sem("mm")[0], mmk(24 + m - 2))
                                g.wait_ge(P.sem("wco3")[0], wcom[m - 2])
                            wom[(ct, m)] = P.inc("w3d3", g.dma_start(
                                out=wobuf[:, m % 2, :, :], in_=wosrc(m, False)), 16)

                @blk.sync
                def _(sy):
                    if ct == 0:
                        for m in range(16):
                            sy.wait_ge(P.sem("w1d3")[0], wim[(ct, m)])
                            mk = P.inc("wci3", sy.dma_start(out=wisrc(m, True), in_=winbuf[:, m % 2, :, :]), 16)
                            assert mk == wcim[m]
                        for m in range(NCH):
                            sy.wait_ge(P.sem("w2d3")[0], wsm[(ct, m)])
                            mk = P.inc("wcs3", sy.dma_start(out=wssrc(m, True), in_=wsbuf[:, m % 2, :, :]), 16)
                            assert mk == wcsm[m]
                        for m in range(NCH):
                            sy.wait_ge(P.sem("w3d3")[0], wom[(ct, m)])
                            mk = P.inc("wco3", sy.dma_start(out=wosrc(m, True), in_=wobuf[:, m % 2, :, :]), 16)
                            assert mk == wcom[m]
                        if not last:
                            sy.wait_ge(P.sem("wci3")[0], wcim[15])
                            for m in range(2):
                                sy.wait_ge(P.sem("mm")[0], mmk(14 + m))
                                wim[(ct + 1, m)] = P.inc("w1d3", sy.dma_start(
                                    out=winbuf[:, m % 2, :, :], in_=wisrc(m, True)), 16)
                    else:
                        for m in range(2, 16):
                            sy.wait_ge(P.sem("mm")[0], mmk(m - 2))
                            wim[(ct, m)] = P.inc("w1d3", sy.dma_start(
                                out=winbuf[:, m % 2, :, :], in_=wisrc(m, True)), 16)
                        for m in range(NCH):
                            if m >= 2:
                                sy.wait_ge(P.sem("mm")[0], mmk(16 + m - 2))
                            wsm[(ct, m)] = P.inc("w2d3", sy.dma_start(
                                out=wsbuf[:, m % 2, :, :], in_=wssrc(m, True)), 16)
                        for m in range(NCH):
                            if m >= 2:
                                sy.wait_ge(P.sem("mm")[0], mmk(24 + m - 2))
                            wom[(ct, m)] = P.inc("w3d3", sy.dma_start(
                                out=wobuf[:, m % 2, :, :], in_=wosrc(m, True)), 16)
                        if not last:
                            for m in range(2):
                                sy.wait_ge(P.sem("mm")[0], mmk(14 + m))
                                wim[(ct + 1, m)] = P.inc("w1d3", sy.dma_start(
                                    out=winbuf[:, m % 2, :, :], in_=wisrc(m, True)), 16)

                @blk.vector
                def _(v):
                    for c in range(NCH):
                        ins = v.tensor_copy(xbf[:, c, :], R[c][:, q0:q0 + TC])
                    P.inc("evV", ins)
                    assert P.val("evV") == vm["cast"]
                    v.wait_ge(P.sem("evS")[0], smk(23))
                    for c in range(NCH):
                        v.tensor_scalar(a_sb[:, c, :], gate[:, c, :], -1.0, 1.0, op0=OP.mult, op1=OP.add)
                    for c in range(NCH):
                        ins = v.tensor_tensor(b_sb[:, c, :], gate[:, c, :], tanh_sb[:, c, :], op=OP.mult)
                    P.fence(v, "fV", ins)
                    for c in range(NCH):
                        if ct == 0:
                            ins = v.tensor_tensor_scan(h_sb[:, c, :], a_sb[:, c, :], b_sb[:, c, :],
                                                       0.0, op0=OP.mult, op1=OP.add)
                        else:
                            ins = v.tensor_tensor_scan(h_sb[:, c, :], a_sb[:, c, :], b_sb[:, c, :],
                                                       hstate[:, c:c + 1], op0=OP.mult, op1=OP.add)
                    P.fence(v, "fV", ins)
                    for c in range(NCH):
                        v.tensor_copy(hstate[:, c:c + 1], h_sb[:, c, TC - 1:TC])
                    for c in range(NCH):
                        ins = v.tensor_copy(val_bf[:, c, :], h_sb[:, c, :])
                    P.inc("evV", ins)
                    assert P.val("evV") == vm["hbf"]
                    for m in range(NCH):
                        v.wait_ge(P.sem("evS")[0], smk(24 + m))
                        ins = v.scalar_tensor_tensor(R[m][:, q0:q0 + TC], R[m][:, q0:q0 + TC], 2.0,
                                                     osb[:, m % 2, :], op0=OP.mult, op1=OP.add)
                        P.inc("evV", ins)
                        assert P.val("evV") == vm[("stt", m)]

                @blk.tensor
                def _(t):
                    t.wait_ge(P.sem("evV")[0], vm["cast"])
                    for m in range(16):
                        t.wait_ge(P.sem("w1d3")[0], wim[(ct, m)])
                        if m >= 2:
                            t.wait_ge(P.sem("evS")[0], smk(m - 2))
                        isval = (m >= NCH)
                        for k in range(NCH):
                            ins = t.matmul(PS[:, m % 2, 0:TC], winbuf[:, m % 2, k, :],
                                           xbf[:, k, :], start=(k == 0), stop=(k == NCH - 1 and not isval))
                        if isval:
                            ins = t.matmul(PS[:, m % 2, 0:TC],
                                           binrow[0:1, (m - NCH) * 128:(m - NCH + 1) * 128],
                                           C["ones_row_bf"][0:1, 0:TC], start=False, stop=True)
                        P.inc("mm", ins)
                        assert P.val("mm") == mmk(m)
                    t.wait_ge(P.sem("evS")[0], smk(15))
                    for m in range(NCH):
                        t.wait_ge(P.sem("w2d3")[0], wsm[(ct, m)])
                        t.wait_ge(P.sem("evS")[0], smk(16 + m - 2))
                        for k in range(NCH):
                            ins = t.matmul(PS[:, m % 2, 0:TC], wsbuf[:, m % 2, k, :],
                                           val_bf[:, k, :], start=(k == 0), stop=(k == NCH - 1))
                        P.inc("mm", ins)
                        assert P.val("mm") == mmk(16 + m)
                    t.wait_ge(P.sem("evV")[0], vm["hbf"])
                    for m in range(NCH):
                        t.wait_ge(P.sem("w3d3")[0], wom[(ct, m)])
                        t.wait_ge(P.sem("evS")[0], smk(24 + m - 2))
                        for k in range(NCH):
                            ins = t.matmul(PS[:, m % 2, 0:TC], wobuf[:, m % 2, k, :],
                                           val_bf[:, k, :], start=(k == 0), stop=False)
                        ins = t.matmul(PS[:, m % 2, 0:TC],
                                       borow[0:1, m * 128:(m + 1) * 128],
                                       C["ones_row_bf"][0:1, 0:TC], start=False, stop=True)
                        P.inc("mm", ins)
                        assert P.val("mm") == mmk(24 + m)

                @blk.scalar
                def _(s):
                    for m in range(16):
                        s.wait_ge(P.sem("mm")[0], mmk(m))
                        if m < NCH:
                            ins = s.activation(gate[:, m, :], PS[:, m % 2, 0:TC], AF.Sigmoid,
                                               bias=bin_c[:, m:m + 1], scale=1.0)
                        else:
                            ins = s.activation(val_bf[:, m - NCH, :], PS[:, m % 2, 0:TC], AF.Copy,
                                               bias=0.0, scale=1.0)
                        P.inc("evS", ins)
                        assert P.val("evS") == smk(m)
                    for m in range(NCH):
                        s.wait_ge(P.sem("mm")[0], mmk(16 + m))
                        ins = s.activation(tanh_sb[:, m, :], PS[:, m % 2, 0:TC], AF.Tanh,
                                           bias=bs_c[:, m:m + 1], scale=1.0)
                        P.inc("evS", ins)
                        assert P.val("evS") == smk(16 + m)
                    for m in range(NCH):
                        s.wait_ge(P.sem("mm")[0], mmk(24 + m))
                        if m >= 2:
                            s.wait_ge(P.sem("evV")[0], vm[("stt", m - 2)])
                        ins = s.activation(osb[:, m % 2, :], PS[:, m % 2, 0:TC], AF.Copy,
                                           bias=0.0, scale=1.0)
                        P.inc("evS", ins)
                        assert P.val("evS") == smk(24 + m)

        for ct in range(NCK):
            p3_chunk(ct)
        for tt in range(NCK):
            q0 = tt * TC
            rs = [R[c][:, q0:q0 + TC] for c in range(NCH)]
            ln_tile_block(P, PS, TC, rs, rs, xbf, C, tag="p3ln%d" % tt,
                          tmp1=gate[:, 0, :], tmp2=gate[:, 1, :])


def emit_phase2(P, R, C):
    nc = P.nc
    H = P.hbm
    TTW = 512
    NTT = T // TTW
    NFT = 8
    selrows = P.dram([NEXP, T], F32, "p2selrows")
    cc_in = P.dram([NEXP, 1], F32, "p2ccin")
    cc_out = P.dram([NEXP * NCORES, 1], F32, "p2ccout")
    w1c2 = P.dram([NEXP, D, D], BF16, "p2w1c")
    w2c2 = P.dram([NEXP, D, D], BF16, "p2w2c")
    w1m, w2m, sl8m, slem = {}, {}, {}, {}
    wcm1, wcm2 = {}, {}

    with P.scope():
        PS = P.psum([128, 8, 512], F32, "ps2")
        wr_sb = P.sb([128, NCH, NEXP], F32, "p2wr")
        br_row = P.sb([1, NEXP], F32, "p2br")
        b2_sb = P.sb([NEXP, 1024], BF16, "p2b2")
        b1c = P.sb([128, NEXP, NFT], F32, "p2b1c")
        rmask = P.sb([NEXP, 1], F32, "p2rmask")

        with nc.Block("p2prolog") as blk:
            @blk.sync
            def _(s):
                P.inc("lds", s.dma_start(
                    out=wr_sb[:, :, :],
                    in_=H["l0_Wr"].rearrange("(c p) e -> p c e", p=128)), 16)
                P.inc("lds", s.dma_start(
                    out=br_row[:, :],
                    in_=H["l0_br"].rearrange("(one e) -> one e", one=1)), 16)
                P.inc("lds", s.dma_start(out=rmask[:, :], in_=H["rankmask"][:, :]), 16)
                for e in range(NEXP):
                    for ft in range(NFT):
                        P.inc("lds", s.dma_start(
                            out=b1c[:, e, ft:ft + 1],
                            in_=H["l0_b1"][e, ft * 128:(ft + 1) * 128].rearrange("(p one) -> p one", one=1)), 16)

            @blk.gpsimd
            def _(g):
                P.inc("ldg", g.dma_start(out=b2_sb[:, :], in_=H["l0_b2"][:, :]), 16)

        with P.scope():
            mask_fm = P.sb([NEXP, T], F32, "p2maskfm")
            pos_fm = P.sb([NEXP, T], F32, "p2posfm")
            kept = mask_fm
            sel_fm = pos_fm
            gatec = P.sb([128, 32], F32, "p2gate")
            counts = P.sb([NEXP, 1], F32, "p2counts")
            call_sb = P.sb([NEXP, NEXP], F32, "p2call")
            offs = P.sb([NEXP, 1], F32, "p2offs")
            lsc = {}
            for nm, sh in [("lcopy", [128, NEXP]), ("m1", [128, 1]), ("negm1", [128, 1]),
                           ("mask_tm", [128, NEXP]), ("eexp", [128, NEXP]),
                           ("den", [128, 1]), ("ktm", [128, NEXP]), ("stm", [128, NEXP])]:
                lsc[nm] = P.sb(sh, F32, "p2" + nm)
            lg_ps = PS[:, 0, 0:NEXP]
            tr_ps = PS[0:8, 1, 0:128]
            tb_ps = PS[:, 2, 0:NEXP]

            for t32 in range(32):
                t0 = t32 * 128
                with nc.Block("p2r%d" % t32) as blk:
                    @blk.tensor
                    def _(t):
                        if t32 == 0:
                            P.wait(t, "lds")
                            P.wait(t, "ldg")
                        for c in range(NCH):
                            t.matmul(lg_ps, R[c][:, t0:t0 + 128], wr_sb[:, c, :],
                                     start=(c == 0), stop=False)
                        ins = t.matmul(lg_ps, C["ones1_f"][:], br_row[:, :], start=False, stop=True)
                        P.inc("mm", ins)

                    @blk.vector
                    def _(v):
                        P.wait(v, "mm")
                        P.fence(v, "fV", v.tensor_copy(lsc["lcopy"][:], lg_ps))
                        P.fence(v, "fV", v.tensor_reduce(lsc["m1"][:], lsc["lcopy"][:], axis=AX, op=OP.max))
                        v.tensor_reduce(lsc["negm1"][:], lsc["lcopy"][:], axis=AX, op=OP.max, negate=True)
                        ins = v.tensor_scalar(lsc["mask_tm"][:], lsc["lcopy"][:], lsc["m1"][:, 0:1], None, op0=OP.is_equal)
                        P.inc("evV", ins)

                    @blk.scalar
                    def _(s):
                        P.wait(s, "evV")
                        ins = s.activation(lsc["eexp"][:], lsc["lcopy"][:], AF.Exp,
                                           bias=lsc["negm1"][:, 0:1], scale=1.0)
                        P.inc("evS", ins)

                    @blk.vector
                    def _(v):
                        P.wait(v, "evS")
                        P.fence(v, "fV", v.tensor_reduce(lsc["den"][:], lsc["eexp"][:], axis=AX, op=OP.add))
                        ins = v.reciprocal(gatec[:, t32:t32 + 1], lsc["den"][:])
                        P.inc("evV", ins)

                    @blk.tensor
                    def _(t):
                        P.wait(t, "evV")
                        ins = t.transpose(tr_ps, lsc["mask_tm"][:], C["identf"][:])
                        P.inc("mm", ins)

                    @blk.vector
                    def _(v):
                        P.wait(v, "mm")
                        ins = v.tensor_copy(mask_fm[:, t0:t0 + 128], tr_ps)
                        P.inc("evV", ins)

            with nc.Block("p2cc") as blk:
                @blk.vector
                def _(v):
                    ins = v.tensor_reduce(counts[:, 0:1], mask_fm[:, :], axis=AX, op=OP.add)
                    P.inc("evV", ins)

                @blk.sync
                def _(s):
                    P.wait(s, "evV")
                    P.inc("st", s.dma_start(out=cc_in[:, :], in_=counts[:, :]), 16)

                @blk.gpsimd
                def _(g):
                    P.wait(g, "st")
                    P.inc("cc", g.collective_compute(
                        "AllGather", OP.bypass,
                        replica_groups=[list(range(NCORES))],
                        ins=[cc_in[:, :].opt()],
                        outs=[cc_out[:, :].opt()]), 1)
                    P.wait(g, "cc")
                    P.inc("ldg", g.dma_start(
                        out=call_sb[:, :],
                        in_=cc_out.rearrange("(r e) one -> r (e one)", e=NEXP)), 16)

            with nc.Block("p2off") as blk:
                @blk.tensor
                def _(t):
                    P.wait(t, "ldg")
                    ins = t.matmul(PS[0:8, 3, 0:1], call_sb[:, :], rmask[:, :], start=True, stop=True)
                    P.inc("mm", ins)

                @blk.vector
                def _(v):
                    P.wait(v, "mm")
                    P.fence(v, "fV", v.tensor_copy(offs[:, 0:1], PS[0:8, 3, 0:1]))
                    P.fence(v, "fV", v.tensor_tensor_scan(pos_fm[:, :], mask_fm[:, :], mask_fm[:, :],
                                                          offs[:, 0:1], op0=OP.add, op1=OP.max))
                    P.fence(v, "fV", v.tensor_scalar(pos_fm[:, :], pos_fm[:, :], float(T * NCORES // NEXP), None, op0=OP.is_lt))
                    ins = v.tensor_tensor(kept[:, :], mask_fm[:, :], pos_fm[:, :], op=OP.mult)
                    P.inc("evV", ins)

            for t32 in range(32):
                t0 = t32 * 128
                with nc.Block("p2s%d" % t32) as blk:
                    @blk.tensor
                    def _(t):
                        ins = t.transpose(PS[:, 2, 0:NEXP], kept[:, t0:t0 + 128], C["identf"][0:8, 0:8])
                        P.inc("mm", ins)

                    @blk.vector
                    def _(v):
                        P.wait(v, "mm")
                        P.fence(v, "fV", v.tensor_copy(lsc["ktm"][:], tb_ps))
                        ins = v.tensor_scalar(lsc["stm"][:], lsc["ktm"][:], gatec[:, t32:t32 + 1], None, op0=OP.mult)
                        P.inc("evV", ins)

                    @blk.tensor
                    def _(t):
                        P.wait(t, "evV")
                        ins = t.transpose(tr_ps, lsc["stm"][:], C["identf"][:])
                        P.inc("mm", ins)

                    @blk.vector
                    def _(v):
                        P.wait(v, "mm")
                        ins = v.tensor_copy(sel_fm[:, t0:t0 + 128], tr_ps)
                        P.inc("evV", ins)

            with nc.Block("p2selst") as blk:
                @blk.sync
                def _(s):
                    P.wait(s, "evV")
                    P.inc("st", s.dma_start(out=selrows[:, :], in_=sel_fm[:, :]), 16)

        bfbuf = P.sb([128, NCH, TTW], BF16, "p2bf")
        acc = P.sb([128, NCH, TTW], F32, "p2acc")
        h_bf = P.sb([128, NFT, TTW], BF16, "p2h")
        w1buf = P.sb([128, 2, NCH, 128], BF16, "p2w1")
        w2buf = P.sb([128, 2, 1024], BF16, "p2w2")
        bcsel = P.sb([128, TTW], F32, "p2bcsel")
        otmp = P.sb([128, TTW], F32, "p2otmp")
        sls8 = P.sb([NEXP, TTW], BF16, "p2sls8")
        selrow = P.sb([1, 2, TTW], BF16, "p2selrow")
        with nc.Block("p2pref") as blk:
            @blk.gpsimd
            def _(g):
                for s_ in range(2):
                    w1m[(0, 0, s_)] = P.inc("w1d2", g.dma_start(
                        out=w1buf[:, s_ % 2, :, :],
                        in_=H["l0_W1"][0, :, s_ * 128:(s_ + 1) * 128].rearrange("(c p) f -> p c f", p=128)), 16)
                for fc in range(2):
                    w2m[(0, 0, fc)] = P.inc("w2d2", g.dma_start(
                        out=w2buf[:, fc % 2, :],
                        in_=H["l0_W2"][0, fc * 128:(fc + 1) * 128, :]), 16)
                P.wait(g, "st")
                sl8m[0] = P.inc("sld2", g.dma_start(out=sls8[:, :], in_=selrows[:, 0:TTW]), 16)
                slem[(0, 0)] = P.inc("sld2", g.dma_start(
                    out=selrow[:, 0, :], in_=selrows[0, 0:TTW].rearrange("(one f) -> one f", one=1)), 16)
        def p2_block(tt, e, q0):
            last = (tt == NTT - 1 and e == NEXP - 1)
            ne_tt, ne_e = (tt, e + 1) if e < NEXP - 1 else (tt + 1, 0)
            mm0 = P.val("mm")
            marks = {}
            m = mm0
            if e == 0:
                m += 1
                marks["b2"] = m
            m += 1
            marks["bc"] = m
            for ft in range(NFT):
                m += 1
                marks[("h", ft)] = m
            for fc in range(NFT):
                m += 1
                marks[("o", fc)] = m
            s0 = P.val("evS")
            smarks = {"bcsel": s0 + 1}
            for ft in range(NFT):
                smarks[("h", ft)] = s0 + 2 + ft
            v0 = P.val("evV")
            vmarks = {}
            v = v0
            if e == 0:
                v += 1
                vmarks["b2"] = v
            v += 1
            vmarks["cast"] = v
            v += 1
            vmarks["oev"] = v

            if tt == 0:
                wc1_0 = P.val("wc1p2")
                wc2_0 = P.val("wc2p2")
                for s_ in range(NFT):
                    wcm1[(e, s_)] = wc1_0 + 16 * (s_ + 1)
                    wcm2[(e, s_)] = wc2_0 + 16 * (s_ + 1)

            def w1src(ee, s_, cch):
                if cch:
                    return w1c2[ee, :, s_ * 128:(s_ + 1) * 128].rearrange("(c p) f -> p c f", p=128)
                return H["l0_W1"][ee, :, s_ * 128:(s_ + 1) * 128].rearrange("(c p) f -> p c f", p=128)

            def w2src(ee, fc, cch):
                if cch:
                    return w2c2[ee, fc * 128:(fc + 1) * 128, :]
                return H["l0_W2"][ee, fc * 128:(fc + 1) * 128, :]

            with nc.Block("p2t%de%d" % (tt, e)) as blk:
                @blk.gpsimd
                def _(g):
                    if tt == 0:
                        for s_ in range(2, NFT):
                            g.wait_ge(P.sem("mm")[0], marks[("h", s_ - 2)])
                            g.wait_ge(P.sem("wc1p2")[0], wcm1[(e, s_ - 2)])
                            w1m[(tt, e, s_)] = P.inc("w1d2", g.dma_start(
                                out=w1buf[:, s_ % 2, :, :], in_=w1src(e, s_, False)), 16)
                        if not last and ne_tt == 0:
                            for s_ in range(2):
                                g.wait_ge(P.sem("mm")[0], marks[("h", NFT - 2 + s_)])
                                g.wait_ge(P.sem("wc1p2")[0], wcm1[(e, NFT - 2 + s_)])
                                w1m[(ne_tt, ne_e, s_)] = P.inc("w1d2", g.dma_start(
                                    out=w1buf[:, s_ % 2, :, :], in_=w1src(ne_e, s_, False)), 16)
                        for fc in range(2, NFT):
                            g.wait_ge(P.sem("mm")[0], marks[("o", fc - 2)])
                            g.wait_ge(P.sem("wc2p2")[0], wcm2[(e, fc - 2)])
                            w2m[(tt, e, fc)] = P.inc("w2d2", g.dma_start(
                                out=w2buf[:, fc % 2, :], in_=w2src(e, fc, False)), 16)
                        if not last and ne_tt == 0:
                            for fc in range(2):
                                g.wait_ge(P.sem("mm")[0], marks[("o", NFT - 2 + fc)])
                                g.wait_ge(P.sem("wc2p2")[0], wcm2[(e, NFT - 2 + fc)])
                                w2m[(ne_tt, ne_e, fc)] = P.inc("w2d2", g.dma_start(
                                    out=w2buf[:, fc % 2, :], in_=w2src(ne_e, fc, False)), 16)
                    if not last:
                        nq0 = ne_tt * TTW
                        if ne_e == 0:
                            sl8m[ne_tt] = P.inc("sld2", g.dma_start(
                                out=sls8[:, :], in_=selrows[:, nq0:nq0 + TTW]), 16)
                        slem[(ne_tt, ne_e)] = P.inc("sld2", g.dma_start(
                            out=selrow[:, (tt * NEXP + e + 1) % 2, :],
                            in_=selrows[ne_e, nq0:nq0 + TTW].rearrange("(one f) -> one f", one=1)), 16)

                @blk.sync
                def _(sy):
                    if tt == 0:
                        for s_ in range(NFT):
                            sy.wait_ge(P.sem("w1d2")[0], w1m[(tt, e, s_)])
                            mk = P.inc("wc1p2", sy.dma_start(
                                out=w1src(e, s_, True), in_=w1buf[:, s_ % 2, :, :]), 16)
                            assert mk == wcm1[(e, s_)]
                        for fc in range(NFT):
                            sy.wait_ge(P.sem("w2d2")[0], w2m[(tt, e, fc)])
                            mk = P.inc("wc2p2", sy.dma_start(
                                out=w2src(e, fc, True), in_=w2buf[:, fc % 2, :]), 16)
                            assert mk == wcm2[(e, fc)]
                        if not last and ne_tt == 1:
                            sy.wait_ge(P.sem("wc1p2")[0], wcm1[(e, NFT - 1)])
                            sy.wait_ge(P.sem("wc2p2")[0], wcm2[(e, NFT - 1)])
                            for s_ in range(2):
                                sy.wait_ge(P.sem("mm")[0], marks[("h", NFT - 2 + s_)])
                                w1m[(ne_tt, ne_e, s_)] = P.inc("w1d2", sy.dma_start(
                                    out=w1buf[:, s_ % 2, :, :], in_=w1src(ne_e, s_, True)), 16)
                            for fc in range(2):
                                sy.wait_ge(P.sem("mm")[0], marks[("o", NFT - 2 + fc)])
                                w2m[(ne_tt, ne_e, fc)] = P.inc("w2d2", sy.dma_start(
                                    out=w2buf[:, fc % 2, :], in_=w2src(ne_e, fc, True)), 16)
                    else:
                        if tt == 1 and e == 0:
                            P.wait(sy, "wc1p2")
                            P.wait(sy, "wc2p2")
                        for s_ in range(2, NFT):
                            sy.wait_ge(P.sem("mm")[0], marks[("h", s_ - 2)])
                            w1m[(tt, e, s_)] = P.inc("w1d2", sy.dma_start(
                                out=w1buf[:, s_ % 2, :, :], in_=w1src(e, s_, True)), 16)
                        if not last:
                            for s_ in range(2):
                                sy.wait_ge(P.sem("mm")[0], marks[("h", NFT - 2 + s_)])
                                w1m[(ne_tt, ne_e, s_)] = P.inc("w1d2", sy.dma_start(
                                    out=w1buf[:, s_ % 2, :, :], in_=w1src(ne_e, s_, True)), 16)
                        for fc in range(2, NFT):
                            sy.wait_ge(P.sem("mm")[0], marks[("o", fc - 2)])
                            w2m[(tt, e, fc)] = P.inc("w2d2", sy.dma_start(
                                out=w2buf[:, fc % 2, :], in_=w2src(e, fc, True)), 16)
                        if not last:
                            for fc in range(2):
                                sy.wait_ge(P.sem("mm")[0], marks[("o", NFT - 2 + fc)])
                                w2m[(ne_tt, ne_e, fc)] = P.inc("w2d2", sy.dma_start(
                                    out=w2buf[:, fc % 2, :], in_=w2src(ne_e, fc, True)), 16)

                @blk.vector
                def _(v):
                    if e == 0:
                        v.wait_ge(P.sem("mm")[0], marks["b2"])
                        for dt in range(NCH):
                            ins = v.tensor_copy(acc[:, dt, :], PS[:, dt, 0:TTW])
                        P.inc("evV", ins)
                        assert P.val("evV") == vmarks["b2"]
                    v.wait_ge(P.sem("evS")[0], smarks["bcsel"])
                    for c in range(NCH):
                        ins = v.tensor_tensor(bfbuf[:, c, :], R[c][:, q0:q0 + TTW], bcsel[:, :], op=OP.mult)
                    P.inc("evV", ins)
                    assert P.val("evV") == vmarks["cast"]
                    v.wait_ge(P.sem("mm")[0], marks[("o", NFT - 1)])
                    for dt in range(NCH):
                        P.fence(v, "fV", v.tensor_tensor(otmp[:], PS[:, dt, 0:TTW], bcsel[:, :], op=OP.mult))
                        ins = v.tensor_tensor(acc[:, dt, :], acc[:, dt, :], otmp[:], op=OP.add)
                        if dt < NCH - 1:
                            P.fence(v, "fV", ins)
                    P.inc("evV", ins)
                    assert P.val("evV") == vmarks["oev"]

                @blk.tensor
                def _(t):
                    if e == 0:
                        t.wait_ge(P.sem("sld2")[0], sl8m[tt])
                        for dt in range(NCH):
                            ins = t.matmul(PS[:, dt, 0:TTW], b2_sb[:, dt * 128:(dt + 1) * 128],
                                           sls8[:, :], start=True, stop=True)
                        P.inc("mm", ins)
                        assert P.val("mm") == marks["b2"]
                        t.wait_ge(P.sem("evV")[0], vmarks["b2"])
                    t.wait_ge(P.sem("sld2")[0], slem[(tt, e)])
                    ins = t.matmul(PS[:, 2, 0:TTW], C["ones1_bf"][:], selrow[:, (tt * NEXP + e) % 2, :],
                                   start=True, stop=True)
                    P.inc("mm", ins)
                    assert P.val("mm") == marks["bc"]
                    t.wait_ge(P.sem("evV")[0], vmarks["cast"])
                    for ft in range(NFT):
                        t.wait_ge(P.sem("w1d2")[0], w1m[(tt, e, ft)])
                        if ft >= 2:
                            t.wait_ge(P.sem("evS")[0], smarks[("h", ft - 2)])
                        for k in range(NCH):
                            ins = t.matmul(PS[:, ft % 2, 0:TTW],
                                           w1buf[:, ft % 2, k, :],
                                           bfbuf[:, k, :], start=(k == 0), stop=(k == NCH - 1))
                        P.inc("mm", ins)
                        assert P.val("mm") == marks[("h", ft)]
                    t.wait_ge(P.sem("evS")[0], smarks[("h", NFT - 1)])
                    for fc in range(NFT):
                        t.wait_ge(P.sem("w2d2")[0], w2m[(tt, e, fc)])
                        for dt in range(NCH):
                            ins = t.matmul(PS[:, dt, 0:TTW], w2buf[:, fc % 2, dt * 128:(dt + 1) * 128],
                                           h_bf[:, fc, :], start=(fc == 0), stop=(fc == NFT - 1))
                        P.inc("mm", ins)
                        assert P.val("mm") == marks[("o", fc)]

                @blk.scalar
                def _(s):
                    s.wait_ge(P.sem("mm")[0], marks["bc"])
                    ins = s.activation(bcsel[:, :], PS[:, 2, 0:TTW], AF.Copy, bias=0.0, scale=1.0)
                    P.inc("evS", ins)
                    assert P.val("evS") == smarks["bcsel"]
                    for ft in range(NFT):
                        s.wait_ge(P.sem("mm")[0], marks[("h", ft)])
                        ins = s.activation(h_bf[:, ft, :], PS[:, ft % 2, 0:TTW], AF.Relu,
                                           bias=b1c[:, e, ft:ft + 1], scale=1.0)
                        P.inc("evS", ins)
                        assert P.val("evS") == smarks[("h", ft)]

        for tt in range(NTT):
            q0 = tt * TTW
            for e in range(NEXP):
                p2_block(tt, e, q0)
            srcs = [acc[:, c, :] for c in range(NCH)]
            rs = [R[c][:, q0:q0 + TTW] for c in range(NCH)]
            with nc.Block("p2res%d" % tt) as blk:
                @blk.vector
                def _(v):
                    for c in range(NCH):
                        ins = v.tensor_tensor(acc[:, c, :], acc[:, c, :], rs[c], op=OP.add)
                    P.inc("evV", ins)
            ln_tile_block(P, PS, TTW, srcs, rs, bfbuf, C, tag="p2ln%d" % tt,
                          tmp1=otmp, tmp2=bcsel)


def emit_phase4(P, R, C):
    nc = P.nc
    H = P.hbm
    TTW = 512
    NTT = T // TTW
    NFT = 32
    rwrows = P.dram([NEXP, T], F32, "p4rwrows")
    P.p4cache = (P.dram([NEXP, D, 4096], BF16, "p4w1c"), P.dram([NEXP, 4096, D], BF16, "p4w2c"))
    P.wc1_marks = {}
    P.wc2_marks = {}

    with P.scope():
        PS = P.psum([128, 8, 512], F32, "ps4")
        wr_sb = P.sb([128, NCH, NEXP], F32, "p4wr")
        br_row = P.sb([1, NEXP], F32, "p4br")
        b2_sb = P.sb([NEXP, 1024], BF16, "p4b2")
        b1c = P.sb([128, NEXP, NFT], F32, "p4b1c")
        gcols = P.sb([128, NCH], F32, "p4g")
        bcols = P.sb([128, NCH], F32, "p4b")

        with nc.Block("p4prolog") as blk:
            @blk.sync
            def _(s):
                P.inc("lds", s.dma_start(
                    out=wr_sb[:, :, :],
                    in_=H["l1f_WrT"].rearrange("(c p) e -> p c e", p=128)), 16)
                P.inc("lds", s.dma_start(
                    out=br_row[:, :],
                    in_=H["l1f_br"].rearrange("(one e) -> one e", one=1)), 16)
                for c in range(NCH):
                    P.inc("lds", s.dma_start(
                        out=gcols[:, c:c + 1],
                        in_=H["l1f_gamma"][c * 128:(c + 1) * 128].rearrange("(p one) -> p one", one=1)), 16)
                    P.inc("lds", s.dma_start(
                        out=bcols[:, c:c + 1],
                        in_=H["l1f_beta"][c * 128:(c + 1) * 128].rearrange("(p one) -> p one", one=1)), 16)
                for e in range(NEXP):
                    for ft in range(NFT):
                        P.inc("lds", s.dma_start(
                            out=b1c[:, e, ft:ft + 1],
                            in_=H["l1f_b1"][e, ft * 128:(ft + 1) * 128].rearrange("(p one) -> p one", one=1)), 16)

            @blk.gpsimd
            def _(g):
                P.inc("ldg", g.dma_start(out=b2_sb[:, :], in_=H["l1f_b2"][:, :]), 16)

        with P.scope():
            rw_fm = P.sb([NEXP, T], F32, "p4rwfm")
            lsc = {}
            for nm, sh in [("lcopy", [128, NEXP]), ("m1", [128, 1]), ("negm1", [128, 1]),
                           ("is1", [128, NEXP]), ("tmp8", [128, NEXP]), ("m2", [128, 1]),
                           ("keep", [128, NEXP]), ("eexp", [128, NEXP]), ("ek", [128, NEXP]),
                           ("den", [128, 1]), ("rden", [128, 1]), ("rw_tm", [128, NEXP])]:
                lsc[nm] = P.sb(sh, F32, "p4" + nm)
            _p4_router(P, R, C, H, PS, wr_sb, br_row, rw_fm, lsc, rwrows)
            if P4_DEBUG == 1:
                P.skip_store = True
                with nc.Block("dbg1") as blk:
                    @blk.sync
                    def _(s):
                        P.inc("st", s.dma_start(out=H["stream_out"][0:8, :], in_=rw_fm[:, :]), 16)
                        P.wait(s, "st")
                return

        bfbuf = P.sb([128, NCH, TTW], BF16, "p4bf")
        acc = P.sb([128, NCH, TTW], F32, "p4acc")
        h_bf = P.sb([128, NFT, TTW], BF16, "p4h")
        w1buf = P.sb([128, 2, NCH, 128], BF16, "p4w1")
        w2buf = P.sb([128, 2, 1024], BF16, "p4w2")
        bcrw = P.sb([128, TTW], F32, "p4bcrw")
        otmp = P.sb([128, TTW], F32, "p4otmp")
        rws8 = P.sb([NEXP, TTW], BF16, "p4rws8")
        rwrow = P.sb([1, 2, TTW], BF16, "p4rwrow")
        with nc.Block("p4pref") as blk:
            @blk.gpsimd
            def _(g):
                for s_ in range(2):
                    P.w1_marks[(0, 0, s_)] = P.inc("w1d0", g.dma_start(
                        out=w1buf[:, s_ % 2, :, :],
                        in_=H["l1f_W1T"][0, :, s_ * 128:(s_ + 1) * 128].rearrange("(c p) f -> p c f", p=128)), 16)
                for fc in range(2):
                    P.w2_marks[(0, 0, fc)] = P.inc("w2d0", g.dma_start(
                        out=w2buf[:, fc % 2, :],
                        in_=H["l1f_W2T"][0, fc * 128:(fc + 1) * 128, :]), 16)
                P.wait(g, "st")
                P.rw8_marks[0] = P.inc("rwd", g.dma_start(out=rws8[:, :], in_=rwrows[:, 0:TTW]), 16)
                P.rwe_marks[(0, 0)] = P.inc("rwd", g.dma_start(
                    out=rwrow[:, 0, :], in_=rwrows[0, 0:TTW].rearrange("(one f) -> one f", one=1)), 16)
        for tt in range(1 if P4_DEBUG in (2, 3) else (P4_DEBUG_NT if P4_DEBUG == 4 else NTT)):
            q0 = tt * TTW
            for e in range(NEXP if P4_DEBUG != 3 else P4_DEBUG_NE):
                _p4_block(P, R, C, H, PS, tt, e, q0, TTW,
                          bfbuf, acc, h_bf, w1buf, w2buf, bcrw, otmp, rws8, rwrow, b1c, b2_sb, rwrows)
            if P4_DEBUG == 3:
                P.skip_store = True
                with nc.Block("dbg3") as blk:
                    @blk.gpsimd
                    def _(g):
                        for ft in range(16):
                            P.inc("st", g.dma_start(out=H["stream_out"][0:128, ft * TTW:(ft + 1) * TTW], in_=h_bf[:, ft, :]), 16)
                        for ft in range(16, NFT):
                            P.inc("st", g.dma_start(out=H["stream_out"][384:512, (ft - 16) * TTW:(ft - 15) * TTW], in_=h_bf[:, ft, :]), 16)
                        P.inc("st", g.dma_start(out=H["stream_out"][128:256, 0:TTW], in_=bcrw[:, :]), 16)
                        for c in range(NCH):
                            P.inc("st", g.dma_start(out=H["stream_out"][256:384, c * TTW:(c + 1) * TTW], in_=acc[:, c, :]), 16)
                        P.wait(g, "st")
                return
            srcs = [acc[:, c, :] for c in range(NCH)]
            rs = [R[c][:, q0:q0 + TTW] for c in range(NCH)]
            with nc.Block("p4res%d" % tt) as blk:
                @blk.vector
                def _(v):
                    for c in range(NCH):
                        ins = v.tensor_tensor(acc[:, c, :], acc[:, c, :], rs[c], op=OP.add)
                    P.inc("evV", ins)
            if P4_DEBUG == 2:
                P.skip_store = True
                with nc.Block("dbg2") as blk:
                    @blk.sync
                    def _(s):
                        for c in range(NCH):
                            P.inc("st", s.dma_start(out=H["stream_out"][c * 128:(c + 1) * 128, 0:TTW], in_=acc[:, c, :]), 16)
                        P.wait(s, "st")
                return
            ln_tile_block(P, PS, TTW, srcs, rs, bfbuf, C,
                          gcols=gcols, bcols=bcols, res=rs, tag="p4ln%d" % tt,
                          tmp1=otmp, tmp2=bcrw)
        if P4_DEBUG == 4:
            return
        for tt in range(NTT):
            q0 = tt * TTW
            rs = [R[c][:, q0:q0 + TTW] for c in range(NCH)]
            ln_tile_block(P, PS, TTW, rs, rs, bfbuf, C, tag="p4lno%d" % tt,
                          tmp1=otmp, tmp2=bcrw)


def _p4_router(P, R, C, H, PS, wr_sb, br_row, rw_fm, lsc, rwrows):
    nc = P.nc
    NEXP_ = NEXP
    lg_ps = PS[:, 0, 0:NEXP]
    tr_ps = PS[0:8, 1, 0:128]
    if True:
        for t32 in range(32):
            t0 = t32 * 128
            with nc.Block("p4r%d" % t32) as blk:
                @blk.tensor
                def _(t):
                    if t32 == 0:
                        P.wait(t, "lds")
                        P.wait(t, "ldg")
                    for c in range(NCH):
                        t.matmul(lg_ps, R[c][:, t0:t0 + 128], wr_sb[:, c, :],
                                 start=(c == 0), stop=False)
                    ins = t.matmul(lg_ps, C["ones1_f"][:], br_row[:, :], start=False, stop=True)
                    P.inc("mm", ins)

                @blk.vector
                def _(v):
                    P.wait(v, "mm")
                    P.fence(v, "fV", v.tensor_copy(lsc["lcopy"][:], lg_ps))
                    P.fence(v, "fV", v.tensor_reduce(lsc["m1"][:], lsc["lcopy"][:], axis=AX, op=OP.max))
                    v.tensor_reduce(lsc["negm1"][:], lsc["lcopy"][:], axis=AX, op=OP.max, negate=True)
                    P.fence(v, "fV", v.tensor_scalar(lsc["is1"][:], lsc["lcopy"][:], lsc["m1"][:, 0:1], None, op0=OP.is_equal))
                    P.fence(v, "fV", v.scalar_tensor_tensor(lsc["tmp8"][:], lsc["is1"][:], -1e30, lsc["lcopy"][:],
                                                            op0=OP.mult, op1=OP.add))
                    P.fence(v, "fV", v.tensor_reduce(lsc["m2"][:], lsc["tmp8"][:], axis=AX, op=OP.max))
                    ins = v.tensor_scalar(lsc["keep"][:], lsc["lcopy"][:], lsc["m2"][:, 0:1], None, op0=OP.is_ge)
                    P.inc("evV", ins)

                @blk.scalar
                def _(s):
                    P.wait(s, "evV")
                    ins = s.activation(lsc["eexp"][:], lsc["lcopy"][:], AF.Exp,
                                       bias=lsc["negm1"][:, 0:1], scale=1.0)
                    P.inc("evS", ins)

                @blk.vector
                def _(v):
                    P.wait(v, "evS")
                    P.fence(v, "fV", v.tensor_tensor(lsc["ek"][:], lsc["eexp"][:], lsc["keep"][:], op=OP.mult))
                    P.fence(v, "fV", v.tensor_reduce(lsc["den"][:], lsc["ek"][:], axis=AX, op=OP.add))
                    P.fence(v, "fV", v.reciprocal(lsc["rden"][:], lsc["den"][:]))
                    ins = v.tensor_scalar(lsc["rw_tm"][:], lsc["ek"][:], lsc["rden"][:, 0:1], None, op0=OP.mult)
                    P.inc("evV", ins)

                @blk.tensor
                def _(t):
                    P.wait(t, "evV")
                    ins = t.transpose(tr_ps, lsc["rw_tm"][:], C["identf"][:])
                    P.inc("mm", ins)

                @blk.vector
                def _(v):
                    P.wait(v, "mm")
                    ins = v.tensor_copy(rw_fm[:, t0:t0 + 128], tr_ps)
                    P.inc("evV", ins)

        with nc.Block("p4rw") as blk:
            @blk.sync
            def _(s):
                P.wait(s, "evV")
                P.inc("st", s.dma_start(out=rwrows[:, :], in_=rw_fm[:, :]), 16)


def _p4_block(P, R, C, H, PS, tt, e, q0, TTW, bfbuf, acc, h_bf, w1buf, w2buf, bcrw, otmp,
              rws8, rwrow, b1c, b2_sb, rwrows):
    nc = P.nc
    NFT = 32
    NTT = T // TTW
    w1s = "w1d%d" % (tt % 2)
    w2s = "w2d%d" % (tt % 2)
    last = (tt == NTT - 1 and e == NEXP - 1)
    ne_tt, ne_e = (tt, e + 1) if e < NEXP - 1 else (tt + 1, 0)

    mm0 = P.val("mm")
    marks = {}
    m = mm0
    if e == 0:
        m += 1
        marks["b2"] = m
    m += 1
    marks["bc"] = m
    for ft in range(NFT):
        m += 1
        marks[("h", ft)] = m
    for fc in range(NFT):
        m += 1
        marks[("o", fc)] = m
    mm_end = m
    s0 = P.val("evS")
    smarks = {"bcrw": s0 + 1}
    for ft in range(NFT):
        smarks[("h", ft)] = s0 + 2 + ft
    s_end = s0 + 2 + NFT - 1
    v0 = P.val("evV")
    vmarks = {}
    v = v0
    if e == 0:
        v += 1
        vmarks["cast"] = v
        v += 1
        vmarks["b2"] = v
    v += 1
    vmarks["oev"] = v
    v_end = v

    w1c, w2c = P.p4cache
    if tt == 0:
        wc1_0 = P.val("wc1")
        wc2_0 = P.val("wc2")
        for s_ in range(NFT):
            P.wc1_marks[(e, s_)] = wc1_0 + 16 * (s_ + 1)
            P.wc2_marks[(e, s_)] = wc2_0 + 16 * (s_ + 1)

    def w1src(ee, s_, from_cache):
        if from_cache:
            return w1c[ee, :, s_ * 128:(s_ + 1) * 128].rearrange("(c p) f -> p c f", p=128)
        return H["l1f_W1T"][ee, :, s_ * 128:(s_ + 1) * 128].rearrange("(c p) f -> p c f", p=128)

    def w2src(ee, fc, from_cache):
        if from_cache:
            return w2c[ee, fc * 128:(fc + 1) * 128, :]
        return H["l1f_W2T"][ee, fc * 128:(fc + 1) * 128, :]

    with nc.Block("p4t%de%d" % (tt, e)) as blk:
        @blk.gpsimd
        def _(g):
            if tt == 0:
                for s_ in range(2, NFT):
                    g.wait_ge(P.sem("mm")[0], marks[("h", s_ - 2)])
                    g.wait_ge(P.sem("wc1")[0], P.wc1_marks[(e, s_ - 2)])
                    P.w1_marks[(tt, e, s_)] = P.inc(w1s, g.dma_start(
                        out=w1buf[:, s_ % 2, :, :], in_=w1src(e, s_, False)), 16)
                if not last:
                    if ne_tt == 0:
                        for s_ in range(2):
                            g.wait_ge(P.sem("mm")[0], marks[("h", NFT - 2 + s_)])
                            g.wait_ge(P.sem("wc1")[0], P.wc1_marks[(e, NFT - 2 + s_)])
                            P.w1_marks[(ne_tt, ne_e, s_)] = P.inc("w1d%d" % (ne_tt % 2), g.dma_start(
                                out=w1buf[:, s_ % 2, :, :], in_=w1src(ne_e, s_, False)), 16)
                for fc in range(2, NFT):
                    g.wait_ge(P.sem("mm")[0], marks[("o", fc - 2)])
                    g.wait_ge(P.sem("wc2")[0], P.wc2_marks[(e, fc - 2)])
                    P.w2_marks[(tt, e, fc)] = P.inc(w2s, g.dma_start(
                        out=w2buf[:, fc % 2, :], in_=w2src(e, fc, False)), 16)
                if not last and ne_tt == 0:
                    for fc in range(2):
                        g.wait_ge(P.sem("mm")[0], marks[("o", NFT - 2 + fc)])
                        g.wait_ge(P.sem("wc2")[0], P.wc2_marks[(e, NFT - 2 + fc)])
                        P.w2_marks[(ne_tt, ne_e, fc)] = P.inc("w2d%d" % (ne_tt % 2), g.dma_start(
                            out=w2buf[:, fc % 2, :], in_=w2src(ne_e, fc, False)), 16)
            if not last:
                nq0 = ne_tt * TTW
                if ne_e == 0:
                    P.rw8_marks[ne_tt] = P.inc("rwd", g.dma_start(
                        out=rws8[:, :], in_=rwrows[:, nq0:nq0 + TTW]), 16)
                P.rwe_marks[(ne_tt, ne_e)] = P.inc("rwd", g.dma_start(
                    out=rwrow[:, (tt * NEXP + e + 1) % 2, :],
                    in_=rwrows[ne_e, nq0:nq0 + TTW].rearrange("(one f) -> one f", one=1)), 16)

        @blk.sync
        def _(sy):
            if tt == 0:
                for s_ in range(NFT):
                    sy.wait_ge(P.sem(w1s)[0], P.w1_marks[(tt, e, s_)])
                    mk = P.inc("wc1", sy.dma_start(
                        out=w1src(e, s_, True), in_=w1buf[:, s_ % 2, :, :]), 16)
                    assert mk == P.wc1_marks[(e, s_)]
                for fc in range(NFT):
                    sy.wait_ge(P.sem(w2s)[0], P.w2_marks[(tt, e, fc)])
                    mk = P.inc("wc2", sy.dma_start(
                        out=w2src(e, fc, True), in_=w2buf[:, fc % 2, :]), 16)
                    assert mk == P.wc2_marks[(e, fc)]
                if not last and ne_tt == 1:
                    sy.wait_ge(P.sem("wc1")[0], P.wc1_marks[(e, NFT - 1)])
                    sy.wait_ge(P.sem("wc2")[0], P.wc2_marks[(e, NFT - 1)])
                    for s_ in range(2):
                        sy.wait_ge(P.sem("mm")[0], marks[("h", NFT - 2 + s_)])
                        P.w1_marks[(ne_tt, ne_e, s_)] = P.inc("w1d%d" % (ne_tt % 2), sy.dma_start(
                            out=w1buf[:, s_ % 2, :, :], in_=w1src(ne_e, s_, True)), 16)
                    for fc in range(2):
                        sy.wait_ge(P.sem("mm")[0], marks[("o", NFT - 2 + fc)])
                        P.w2_marks[(ne_tt, ne_e, fc)] = P.inc("w2d%d" % (ne_tt % 2), sy.dma_start(
                            out=w2buf[:, fc % 2, :], in_=w2src(ne_e, fc, True)), 16)
            else:
                if tt == 1 and e == 0:
                    P.wait(sy, "wc1")
                    P.wait(sy, "wc2")
                for s_ in range(2, NFT):
                    sy.wait_ge(P.sem("mm")[0], marks[("h", s_ - 2)])
                    P.w1_marks[(tt, e, s_)] = P.inc(w1s, sy.dma_start(
                        out=w1buf[:, s_ % 2, :, :], in_=w1src(e, s_, True)), 16)
                if not last:
                    for s_ in range(2):
                        sy.wait_ge(P.sem("mm")[0], marks[("h", NFT - 2 + s_)])
                        P.w1_marks[(ne_tt, ne_e, s_)] = P.inc("w1d%d" % (ne_tt % 2), sy.dma_start(
                            out=w1buf[:, s_ % 2, :, :], in_=w1src(ne_e, s_, True)), 16)
                for fc in range(2, NFT):
                    sy.wait_ge(P.sem("mm")[0], marks[("o", fc - 2)])
                    P.w2_marks[(tt, e, fc)] = P.inc(w2s, sy.dma_start(
                        out=w2buf[:, fc % 2, :], in_=w2src(e, fc, True)), 16)
                if not last:
                    for fc in range(2):
                        sy.wait_ge(P.sem("mm")[0], marks[("o", NFT - 2 + fc)])
                        P.w2_marks[(ne_tt, ne_e, fc)] = P.inc("w2d%d" % (ne_tt % 2), sy.dma_start(
                            out=w2buf[:, fc % 2, :], in_=w2src(ne_e, fc, True)), 16)

        @blk.vector
        def _(v):
            if e == 0:
                for c in range(NCH):
                    ins = v.tensor_copy(bfbuf[:, c, :], R[c][:, q0:q0 + TTW])
                P.inc("evV", ins)
                assert P.val("evV") == vmarks["cast"]
                v.wait_ge(P.sem("mm")[0], marks["b2"])
                for dt in range(NCH):
                    ins = v.tensor_copy(acc[:, dt, :], PS[:, dt, 0:TTW])
                P.inc("evV", ins)
                assert P.val("evV") == vmarks["b2"]
            v.wait_ge(P.sem("mm")[0], marks[("o", NFT - 1)])
            v.wait_ge(P.sem("evS")[0], smarks["bcrw"])
            for dt in range(NCH):
                P.fence(v, "fV", v.tensor_tensor(otmp[:], PS[:, dt, 0:TTW], bcrw[:], op=OP.mult))
                ins = v.tensor_tensor(acc[:, dt, :], acc[:, dt, :], otmp[:], op=OP.add)
                if dt < NCH - 1:
                    P.fence(v, "fV", ins)
            P.inc("evV", ins)
            assert P.val("evV") == vmarks["oev"]

        @blk.tensor
        def _(t):
            if e == 0:
                t.wait_ge(P.sem("rwd")[0], P.rw8_marks[tt])
                for dt in range(NCH):
                    ins = t.matmul(PS[:, dt, 0:TTW], b2_sb[:, dt * 128:(dt + 1) * 128],
                                   rws8[:, :], start=True, stop=True)
                P.inc("mm", ins)
                assert P.val("mm") == marks["b2"]
                t.wait_ge(P.sem("evV")[0], vmarks["b2"])
            t.wait_ge(P.sem("rwd")[0], P.rwe_marks[(tt, e)])
            ins = t.matmul(PS[:, 2, 0:TTW], C["ones1_bf"][:], rwrow[:, (tt * NEXP + e) % 2, :],
                           start=True, stop=True)
            P.inc("mm", ins)
            assert P.val("mm") == marks["bc"]
            for ft in range(NFT):
                t.wait_ge(P.sem(w1s)[0], P.w1_marks[(tt, e, ft)])
                if ft >= 2:
                    t.wait_ge(P.sem("evS")[0], smarks[("h", ft - 2)])
                for k in range(NCH):
                    ins = t.matmul(PS[:, ft % 2, 0:TTW],
                                   w1buf[:, ft % 2, k, :],
                                   bfbuf[:, k, :], start=(k == 0), stop=(k == NCH - 1))
                P.inc("mm", ins)
                assert P.val("mm") == marks[("h", ft)]
            t.wait_ge(P.sem("evS")[0], smarks[("h", NFT - 1)])
            for fc in range(NFT):
                t.wait_ge(P.sem(w2s)[0], P.w2_marks[(tt, e, fc)])
                for dt in range(NCH):
                    ins = t.matmul(PS[:, dt, 0:TTW], w2buf[:, fc % 2, dt * 128:(dt + 1) * 128],
                                   h_bf[:, fc, :], start=(fc == 0), stop=(fc == NFT - 1))
                P.inc("mm", ins)
                assert P.val("mm") == marks[("o", fc)]

        @blk.scalar
        def _(s):
            s.wait_ge(P.sem("mm")[0], marks["bc"])
            ins = s.activation(bcrw[:], PS[:, 2, 0:TTW], AF.Copy, bias=0.0, scale=1.0)
            P.inc("evS", ins)
            assert P.val("evS") == smarks["bcrw"]
            for ft in range(NFT):
                s.wait_ge(P.sem("mm")[0], marks[("h", ft)])
                ins = s.activation(h_bf[:, ft, :], PS[:, ft % 2, 0:TTW], AF.Relu,
                                   bias=b1c[:, e, ft:ft + 1], scale=1.0)
                P.inc("evS", ins)
                assert P.val("evS") == smarks[("h", ft)]

    assert P.val("mm") == mm_end
    assert P.val("evS") == s_end
    assert P.val("evV") == v_end


def build_program(phases, fused_in_hbm=True):
    P = Prog()
    nc = P.nc
    H = {}
    P.hbm = H
    P.w1_marks = {}
    P.w2_marks = {}
    P.rw8_marks = {}
    P.rwe_marks = {}

    def din(nm, shape, dt=F32):
        H[nm] = nc.dram_tensor(nm, shape, dt, kind="ExternalInput").ap()

    din("cc_ident", [128, 128])
    din("stream_in", [D, T])
    if 2 in phases:
        din("rankmask", [NEXP, 1])
        din("l0_Wr", [D, NEXP])
        din("l0_br", [NEXP])
        din("l0_W1", [NEXP, D, D])
        din("l0_W2", [NEXP, D, D])
        din("l0_b1", [NEXP, D])
        din("l0_b2", [NEXP, D])
    H["stream_out"] = nc.dram_tensor("stream_out", [D, T], F32, kind="ExternalOutput").ap()
    if 1 in phases:
        din("l0a_WKT", [D, D])
        din("l0a_WQT", [D, D])
        din("l0a_WVT", [D, D])
        din("l0a_WOT", [D, D])
        din("cc_trimask", [4, 128, 512])
    if 3 in phases:
        din("l1a_WinT", [D, 2 * D])
        din("l1a_WsT", [D, D])
        din("l1a_WoT", [D, D])
        din("l1a_b_in", [2 * D])
        din("l1a_b_s", [D])
        din("l1a_b_o", [D])
    if 4 in phases:
        din("l1f_WrT", [D, NEXP])
        din("l1f_br", [NEXP])
        din("l1f_W1T", [NEXP, D, 4096])
        din("l1f_W2T", [NEXP, 4096, D])
        din("l1f_b1", [NEXP, 4096])
        din("l1f_b2", [NEXP, D])
        din("l1f_gamma", [D])
        din("l1f_beta", [D])

    R = [P.sb([128, T], F32, "R%d" % c) for c in range(NCH)]
    C = make_consts(P)

    with nc.Block("ldR") as blk:
        @blk.sync
        def _(s):
            for c in range(NCH):
                P.inc("lds", s.dma_start(out=R[c][:], in_=H["stream_in"][c * 128:(c + 1) * 128, :]), 16)

        @blk.vector
        def _(v):
            P.wait(v, "lds")

    for ph in phases:
        if ph == 1:
            emit_phase1(P, R, C)
        elif ph == 2:
            emit_phase2(P, R, C)
        elif ph == 3:
            emit_phase3(P, R, C)
        elif ph == 4:
            emit_phase4(P, R, C)
        else:
            raise NotImplementedError("phase %d" % ph)

    if getattr(P, "skip_store", False):
        P.top.close()
        return P
    with nc.Block("stR") as blk:
        @blk.sync
        def _(s):
            for c in range(NCH):
                P.inc("st", s.dma_start(out=H["stream_out"][c * 128:(c + 1) * 128, :], in_=R[c][:]), 16)
            P.wait(s, "st")

    P.top.close()
    return P


def prep_weights(ins):
    w = {}
    w["cc_ident"] = np.eye(128, dtype=np.float32)
    w["l0a_WKT"] = np.ascontiguousarray(ins["L0_attn_W_K"].reshape(D, D).T)
    w["l0a_WQT"] = np.ascontiguousarray(ins["L0_attn_W_Q"].reshape(D, D).T)
    w["l0a_WVT"] = np.ascontiguousarray(ins["L0_attn_W_V"].reshape(D, D).T)
    w["l0a_WOT"] = np.ascontiguousarray(ins["L0_attn_W_O"].T)
    tri = np.zeros((4, 128, 512), dtype=np.float32)
    for a in range(4):
        for pr in range(128):
            q0 = pr + 128 * a
            if q0 < 512:
                tri[a, pr, q0:] = 1.0
    w["cc_trimask"] = tri
    w["l0_Wr"] = np.ascontiguousarray(ins["L0_ffn_Wr"])
    w["l0_br"] = np.ascontiguousarray(ins["L0_ffn_br"])
    w["l0_W1"] = np.ascontiguousarray(ins["L0_ffn_W1"])
    w["l0_W2"] = np.ascontiguousarray(ins["L0_ffn_W2"])
    w["l0_b1"] = np.ascontiguousarray(ins["L0_ffn_b1"])
    w["l0_b2"] = np.ascontiguousarray(ins["L0_ffn_b2"])
    w["l1a_WinT"] = np.ascontiguousarray(ins["L1_attn_W_in"].T)
    w["l1a_WsT"] = np.ascontiguousarray(ins["L1_attn_W_s"].T)
    w["l1a_WoT"] = np.ascontiguousarray(ins["L1_attn_W_o"].T)
    w["l1a_b_in"] = np.ascontiguousarray(ins["L1_attn_b_in"])
    w["l1a_b_s"] = np.ascontiguousarray(ins["L1_attn_b_s"])
    w["l1a_b_o"] = np.ascontiguousarray(ins["L1_attn_b_o"])
    w["l1f_WrT"] = np.ascontiguousarray(ins["L1_ffn_Wr"].T)
    w["l1f_br"] = np.ascontiguousarray(ins["L1_ffn_br"])
    w["l1f_W1T"] = np.ascontiguousarray(ins["L1_ffn_W1"].transpose(0, 2, 1))
    w["l1f_W2T"] = np.ascontiguousarray(ins["L1_ffn_W2"].transpose(0, 2, 1))
    w["l1f_b1"] = np.ascontiguousarray(ins["L1_ffn_b1"])
    w["l1f_b2"] = np.ascontiguousarray(ins["L1_ffn_b2"])
    w["l1f_gamma"] = np.ascontiguousarray(ins["L1_ffn_gamma"])
    w["l1f_beta"] = np.ascontiguousarray(ins["L1_ffn_beta"])
    return w


def run_phases(phases, stream_in_cores, weights):
    P = build_program(phases)
    in_maps = []
    for cid in range(NCORES):
        m = {"stream_in": stream_in_cores[cid]}
        if "rankmask" in P.hbm:
            m["rankmask"] = (np.arange(NEXP) < cid).astype(np.float32).reshape(NEXP, 1)
        for nm, arr in weights.items():
            if nm in P.hbm and nm != "stream_out":
                m[nm] = arr
        in_maps.append(m)
    res = run_bass_kernel_spmd(P.nc, in_maps, core_ids=list(range(NCORES)))
    return [res.results[c]["stream_out"] for c in range(NCORES)]


PHASE_GROUPS = [(1, 2, 3, 4)]


def kernel(**inputs):
    x = inputs["L0_attn_x"]
    w = prep_weights(inputs)
    streams = [np.ascontiguousarray(x[b].T).astype(np.float32) for b in range(NCORES)]
    for group in PHASE_GROUPS:
        streams = run_phases(group, streams, w)
    y = np.stack([o.T for o in streams], axis=0)
    return np.ascontiguousarray(y.astype(np.float32))
